# version 18
# baseline (speedup 1.0000x reference)
import numpy as np
import concourse.bass as bass
import concourse.mybir as mybir
from concourse.bass_utils import run_bass_kernel_spmd

F32 = mybir.dt.float32
BF16 = mybir.dt.bfloat16
I32 = mybir.dt.int32
U8 = mybir.dt.uint8
AF = mybir.ActivationFunctionType
ALU = mybir.AluOpType
AX = mybir.AxisListType

B, T, D = 8, 2048, 1024
NB = D // 128
NT = T // 128
NH, HD = 16, 64
F0 = 3072
NF0 = F0 // 128
F1 = 4096
NF1 = F1 // 128
QC = 512
NQC = T // QC
EPS = 1e-5
NEGBIG = -1.0e30
KB = 1024

_DEBUG = []
_PHASES = 99


class Sem:
    def __init__(self, nc, name):
        self.h = nc.alloc_semaphore(name=name)
        self.n = 0

    def inc(self, ins, k=1):
        ins.then_inc(self.h, k)
        self.n += k
        return ins


class DmaQ:

    def __init__(self, nc, name, eng, n=4):
        self.nc, self.eng = nc, eng
        self.sems = [Sem(nc, f"{name}{i}") for i in range(n)]
        self.i = -1
        self.cur = None

    def begin(self):
        self.i = (self.i + 1) % len(self.sems)
        s = self.sems[self.i]
        self.eng.wait_ge(s.h, s.n)
        self.cur = s
        return s

    def dma(self, out, in_):
        self.cur.inc(self.eng.dma_start(out=out, in_=in_), 16)

    def tok(self):
        return (self.cur.h, self.cur.n)

    def wait(self, eng, tok=None):
        h, n = tok if tok is not None else (self.cur.h, self.cur.n)
        eng.wait_ge(h, n)

    def wait_all(self, eng):
        for s in self.sems:
            eng.wait_ge(s.h, s.n)


class Ctx:
    pass




def _host_consts():
    import jax
    cpu = jax.devices("cpu")[0]
    with jax.default_device(cpu):
        import jax.numpy as jnp
        thetas = 10000.0 ** (-jnp.arange(0, D, 2, dtype=jnp.float32) / D)
        thetas = jnp.repeat(thetas, 2)
        angles = (jnp.arange(T, dtype=jnp.float32)[:, None] * thetas[None, :])
        s32 = np.asarray(jnp.sin(angles))
        c32 = np.asarray(jnp.cos(angles))
    sin_fm = np.ascontiguousarray(s32.T)
    cos_fm = np.ascontiguousarray(c32.T)
    sin_lo = np.zeros_like(sin_fm)
    cos_lo = np.zeros_like(cos_fm)
    P = np.zeros((128, 128), np.float32)
    for i in range(64):
        P[2 * i, 2 * i + 1] = -1.0
        P[2 * i + 1, 2 * i] = 1.0
    rotPT = np.ascontiguousarray(P.T)
    ident = np.eye(128, dtype=np.float32)
    ones1 = np.ones((1, 128), np.float32)
    allones = np.ones((128, 128), np.float32)
    onescol = np.ones((128, 1), np.float32)
    masks = np.zeros((128, 4 * QC), np.float32)
    pk = np.arange(128)[:, None]
    fq = np.arange(QC)[None, :]
    for j, delta in enumerate((0, 128, 256, 384)):
        masks[:, j * QC:(j + 1) * QC] = np.where(pk + delta > fq, NEGBIG, 0.0)
    return dict(sin_fm=sin_fm, cos_fm=cos_fm, sin_lo=sin_lo, cos_lo=cos_lo,
                rotPT=rotPT, ident=ident,
                ones1=ones1, allones=allones, onescol=onescol, masks=masks)




def build_nc(debug=()):
    nc = bass.Bass("TRN2", target_bir_lowering=False, debug=False)
    c = Ctx()
    c.nc = nc
    c.debug = set(debug)
    c.uid = 0

    def din(name, shape, dt=F32):
        return nc.dram_tensor(name, list(shape), dt, kind="ExternalInput").ap()

    c.xT = din("xT", [D, T])
    c.wqT = din("wqT", [D, D]); c.wkT = din("wkT", [D, D])
    c.wvT = din("wvT", [D, D]); c.woT = din("woT", [D, D])
    c.bo = din("bo", [128, NB])
    c.wg0T = din("wg0T", [D, F0], I32)
    c.wu0T = din("wu0T", [D, F0], I32)
    c.wd0T = din("wd0T", [F0, D], I32)
    c.wrvkT = din("wrvkT", [D, 3 * D])
    c.woutT = din("woutT", [D, D])
    c.td = din("td", [128, NB]); c.tf = din("tf", [128, NB])
    c.wg1T = din("wg1T", [D, F1])
    c.wu1T = din("wu1T", [D, F1])
    c.wd1T = din("wd1T", [F1, D])
    c.sin = din("sin_fm", [D, T]); c.cos = din("cos_fm", [D, T])
    c.sinl = din("sin_lo", [D, T]); c.cosl = din("cos_lo", [D, T])
    c.rotPT_h = din("rotPT", [128, 128], BF16)
    c.ident_h = din("ident", [128, 128], BF16)
    c.identf_h = din("identf", [128, 128], F32)
    c.ones1_h = din("ones1", [1, 128], BF16)
    c.allones_h = din("allones", [128, 128], BF16)
    c.onescol_h = din("onescol", [128, 1], BF16)
    c.masks_h = din("masks", [128, 4 * QC], BF16)

    c.out = nc.dram_tensor("out", [D, T], F32, kind="ExternalOutput").ap()

    def dsc(name, shape, dt):
        return nc.dram_tensor(name, list(shape), dt, kind="Internal").ap()

    c.sp_x = dsc("sp_x", [D, T], F32)
    c.sp_g = dsc("sp_g", [D, T], F32)
    c.sp_s = dsc("sp_s", [D, T], F32)
    c.sp_vh = dsc("sp_vh", [128, NT * 16 * 65], BF16)
    c.sp_vl = dsc("sp_vl", [128, NT * 16 * 65], BF16)
    c.sp_qh = dsc("sp_qh", [D, T], BF16)
    c.sp_ql = dsc("sp_ql", [D, T], BF16)
    c.sp_oh = dsc("sp_oh", [D, T], BF16)
    c.sp_ol = dsc("sp_ol", [D, T], BF16)
    c.sp_hh = dsc("sp_hh", [F0, T], BF16)
    c.sp_hl = dsc("sp_hl", [F0, T], BF16)
    c.sp_wg0 = dsc("sp_wg0", [D, F0], BF16)
    c.sp_wu0 = dsc("sp_wu0", [D, F0], BF16)
    c.sp_wd0 = dsc("sp_wd0", [F0, D], BF16)
    c.sp_n = dsc("sp_n", [D, T], BF16)
    c.sp_wg1 = dsc("sp_wg1", [D, F1], BF16)
    c.sp_wu1 = dsc("sp_wu1", [D, F1], BF16)
    c.sp_wd1 = dsc("sp_wd1", [F1, D], BF16)

    c.A0, A_SZ = 0, 66560
    c.B0, B_SZ = c.A0 + 66560, 64 * KB
    c.C0, C_SZ = c.B0 + 64 * KB, 64 * KB
    c.D0, D_SZ = c.C0 + 64 * KB, 6400
    c.E0, E_SZ = c.D0 + 6400, 2 * KB
    ARENA = c.E0 + E_SZ
    arena = nc.alloc_sbuf_tensor("ARENA", [128, ARENA], U8)
    c.base = nc.lookup_mloc(arena).addr

    def place(name, shape, dtype, off):
        nbytes = int(np.prod(shape[1:])) * mybir.dt.size(dtype)
        assert off + nbytes <= ARENA, (name, off, nbytes)
        c.uid += 1
        nm = f"{name}_{c.uid}"
        h = nc.virtual_sbuf_tensor(nm, list(shape), dtype)
        nc._tensor(h.name, list(shape), dtype, type="SB")
        mloc = nc.lookup_mloc(h)
        mloc.allocated = True
        mloc.addr = c.base + off
        return h

    c.place = place

    c.XF = place("XF", [128, NB * T], F32, c.A0)
    c.VH = place("VH", [128, NT * 16 * 65], BF16, c.A0)
    c.VL = place("VL", [128, NT * 16 * 65], BF16, c.A0 + 33280)
    c.AH = place("AH", [128, NB * T], BF16, c.B0)
    c.AL = place("AL", [128, NB * T], BF16, c.B0 + 32 * KB)
    c.SF = place("SF", [128, NB * T], F32, c.B0)
    c.KH = place("KH", [128, NB * T], BF16, c.C0)
    c.KL = place("KL", [128, NB * T], BF16, c.C0 + 32 * KB)
    c.RWH, c.RWL = c.KH, c.KL

    off = c.D0
    c.k_masks = place("k_masks", [128, 4 * QC], BF16, off); off += 4 * QC * 2
    c.k_rotPT = place("k_rotPT", [128, 128], BF16, off); off += 256
    c.k_ident = place("k_ident", [128, 128], BF16, off); off += 256
    c.k_ident_f32 = place("k_ident_f32", [128, 128], F32, off); off += 512
    c.k_ones1 = place("k_ones1", [1, 128], BF16, off); off += 256
    c.k_allones = place("k_allones", [128, 128], BF16, off); off += 256
    c.k_onescol = place("k_onescol", [128, 1], BF16, off); off += 32
    c.k_bo = place("k_bo", [128, NB], F32, off); off += NB * 4
    c.k_td = place("k_td", [128, NB], F32, off); off += NB * 4
    c.k_tf = place("k_tf", [128, NB], F32, off); off += NB * 4
    assert off <= c.D0 + D_SZ

    off = c.E0
    c.cols = place("cols", [128, 16], F32, off); off += 16 * 4
    c.sinv = place("sinv", [128, NT], F32, off); off += NT * 4
    c.shinv = place("shinv", [128, NT], F32, off); off += NT * 4
    c.e_t1 = place("e_t1", [128, 128], F32, off); off += 128 * 4
    assert off <= c.E0 + E_SZ, off

    c.PS = [nc.alloc_psum_tensor(f"ps{i}", [128, 512], F32) for i in range(7)]
    c.PSB = nc.alloc_psum_tensor("psb", [128, 1024], BF16)

    c.LQ = DmaQ(nc, "ld", nc.sync, 4)
    c.SQ = DmaQ(nc, "sp", nc.gpsimd, 4)
    c.sPE = Sem(nc, "sPE")
    c.sACT = Sem(nc, "sACT")
    c.sDVE = Sem(nc, "sDVE")
    c.sPOOL = Sem(nc, "sPOOL")

    emit_all(c)

    c.SQ.wait_all(nc.sync)
    c.LQ.wait_all(nc.sync)
    nc.all_engine_barrier()
    return nc, c


def bar(c):
    c.nc.all_engine_barrier()


def tap(c, name, src_ap, shape, dtype):
    if name not in c.debug:
        return
    t = c.nc.dram_tensor("tap_" + name, list(shape), dtype,
                         kind="ExternalOutput").ap()
    bar(c)
    c.SQ.begin()
    c.SQ.dma(t[:], src_ap)
    c.SQ.wait(c.nc.gpsimd)
    bar(c)




def emit_all(c):
    steps = [
        lambda: p0_load_consts(c),
        lambda: p_ln(c, 0),
        lambda: (tap(c, "xn0h", c.AH[:], [128, NB * T], BF16),
                 tap(c, "xn0l", c.AL[:], [128, NB * T], BF16)),
        lambda: p_vproj(c),
        lambda: p_ffn0(c),
        lambda: (p_rope(c), tap(c, "rxh", c.AH[:], [128, NB * T], BF16),
                 tap(c, "rxl", c.AL[:], [128, NB * T], BF16)),
        lambda: (p_qk(c), tap(c, "kh", c.KH[:], [128, NB * T], BF16)),
        lambda: p_attn(c),
        lambda: (p_oproj_combine(c),
                 tap(c, "x1", c.XF[:], [128, NB * T], F32)),
        lambda: p_ln(c, 1),
        lambda: p_l1_quant(c),
        lambda: (p_rvk_wkv(c), tap(c, "rwh", c.RWH[:], [128, NB * T], BF16)),
        lambda: p_outproj(c),
        lambda: p_ffn1_combine(c),
    ]
    for i, s in enumerate(steps):
        if i >= _PHASES:
            break
        s()


def p0_load_consts(c):
    nc = c.nc
    c.LQ.begin()
    for dst, src in ((c.k_masks, c.masks_h), (c.k_rotPT, c.rotPT_h),
                     (c.k_ident, c.ident_h), (c.k_ident_f32, c.identf_h),
                     (c.k_ones1, c.ones1_h),
                     (c.k_allones, c.allones_h), (c.k_onescol, c.onescol_h),
                     (c.k_bo, c.bo), (c.k_td, c.td), (c.k_tf, c.tf)):
        c.LQ.dma(dst[:], src[:])
    c.LQ.dma(c.XF[:].rearrange("p (b t) -> p b t", b=NB),
             c.xT[:].rearrange("(b p) t -> p b t", p=128))
    c.LQ.wait(nc.sync)
    bar(c)


def p_ln(c, layer):
    nc = c.nc
    c.SQ.begin()
    c.SQ.dma(c.sp_x[:].rearrange("(b p) t -> p b t", p=128),
             c.XF[:].rearrange("p (b t) -> p b t", b=NB))

    xh = [c.place(f"ln_xh{i}", [128, T], BF16, c.B0 + i * 4 * KB)
          for i in range(2)]
    xl = [c.place(f"ln_xl{i}", [128, T], BF16, c.B0 + 8 * KB + i * 4 * KB)
          for i in range(2)]
    qh = [c.place(f"ln_qh{i}", [128, T], BF16, c.B0 + 16 * KB + i * 4 * KB)
          for i in range(2)]
    ql = [c.place(f"ln_ql{i}", [128, T], BF16, c.B0 + 24 * KB + i * 4 * KB)
          for i in range(2)]
    sq = [c.place(f"ln_sq{i}", [128, T], F32, c.C0 + i * 8 * KB)
          for i in range(2)]
    xn = [c.place(f"ln_xn{i}", [128, 512], F32, c.C0 + 16 * KB + i * 2 * KB)
          for i in range(2)]
    r_mean = c.place("ln_rmean", [1, T], F32, c.C0 + 20 * KB)
    r_var = c.place("ln_rvar", [1, T], F32, c.C0 + 28 * KB)
    r_rstd = c.place("ln_rrstd", [1, T], F32, c.C0 + 36 * KB)
    r_tmp = c.place("ln_rtmp", [1, T], F32, c.C0)
    rb_mh = c.place("ln_rbmh", [1, T], BF16, c.C0 + 44 * KB)
    rb_ml = c.place("ln_rbml", [1, T], BF16, c.C0 + 48 * KB)
    rb_rh = c.place("ln_rbrh", [1, T], BF16, c.C0 + 52 * KB)
    rb_rl = c.place("ln_rbrl", [1, T], BF16, c.C0 + 56 * KB)

    PSmb, PSrb = c.PS[4], c.PS[5]

    pe_m = [0, 0]
    act_m = [0, 0]
    dve_m = [0, 0]
    pool_m = [0, 0]
    for rnd in range(2):
        tcs = (2 * rnd, 2 * rnd + 1)
        for cb in range(NB):
            j = cb % 2
            xb = c.XF[:, cb * T:(cb + 1) * T]
            nc.scalar.wait_ge(c.sPE.h, pe_m[j])
            nc.scalar.wait_ge(c.sDVE.h, dve_m[j])
            nc.scalar.wait_ge(c.sPOOL.h, pool_m[j])
            c.sACT.inc(nc.scalar.activation(xh[j][:], xb, AF.Copy, bias=0.0,
                                            scale=1.0))
            c.sACT.inc(nc.scalar.activation(sq[j][:], xb, AF.Square,
                                            bias=0.0, scale=1.0))
            act_m[j] = c.sACT.n
            nc.vector.wait_ge(c.sACT.h, act_m[j])
            nc.vector.wait_ge(c.sPE.h, pe_m[j])
            c.sDVE.inc(nc.vector.scalar_tensor_tensor(
                xl[j][:], xb, 1.0, xh[j][:], ALU.mult, ALU.subtract))
            dve_m[j] = c.sDVE.n
            nc.gpsimd.wait_ge(c.sACT.h, act_m[j])
            nc.gpsimd.wait_ge(c.sPE.h, pe_m[j])
            c.sPOOL.inc(nc.gpsimd.tensor_copy(qh[j][:], sq[j][:]))
            pool_m[j] = c.sPOOL.n
            nc.vector.wait_ge(c.sPOOL.h, pool_m[j])
            c.sDVE.inc(nc.vector.tensor_sub(ql[j][:], sq[j][:], qh[j][:]))
            dve_m[j] = c.sDVE.n
            nc.tensor.wait_ge(c.sDVE.h, dve_m[j])
            nc.tensor.wait_ge(c.sPOOL.h, pool_m[j])
            ins = None
            for tcj, tc in enumerate(tcs):
                sl = slice(tc * 512, tc * 512 + 512)
                st = (cb == 0)
                sp = (cb == NB - 1)
                nc.tensor.matmul(c.PS[tcj][0:1, :], c.k_onescol[:],
                                 xh[j][:, sl], start=st, stop=False)
                nc.tensor.matmul(c.PS[tcj][0:1, :], c.k_onescol[:],
                                 xl[j][:, sl], start=False, stop=sp)
                nc.tensor.matmul(c.PS[2 + tcj][0:1, :], c.k_onescol[:],
                                 qh[j][:, sl], start=st, stop=False)
                ins = nc.tensor.matmul(c.PS[2 + tcj][0:1, :], c.k_onescol[:],
                                       ql[j][:, sl], start=False, stop=sp)
            c.sPE.inc(ins)
            pe_m[j] = c.sPE.n
        nc.vector.wait_ge(c.sPE.h, c.sPE.n)
        for tcj, tc in enumerate(tcs):
            sl = slice(tc * 512, tc * 512 + 512)
            c.sDVE.inc(nc.vector.tensor_scalar(
                r_mean[:, sl], c.PS[tcj][0:1, :], 1.0 / D, None, ALU.mult))
            c.sDVE.inc(nc.vector.tensor_scalar(
                r_var[:, sl], c.PS[2 + tcj][0:1, :], 1.0 / D, None,
                ALU.mult))
        nc.tensor.wait_ge(c.sDVE.h, c.sDVE.n)

    nc.scalar.wait_ge(c.sDVE.h, c.sDVE.n)
    nc.scalar.wait_ge(c.sPOOL.h, c.sPOOL.n)
    c.sACT.inc(nc.scalar.activation(r_tmp[:], r_mean[:], AF.Square,
                                    bias=0.0, scale=1.0))
    nc.vector.wait_ge(c.sACT.h, c.sACT.n)
    c.sDVE.inc(nc.vector.tensor_sub(r_var[:], r_var[:], r_tmp[:]))
    nc.vector.wait_ge(c.sDVE.h, c.sDVE.n)
    c.sDVE.inc(nc.vector.tensor_scalar(r_var[:], r_var[:], EPS,
                                       None, ALU.add))
    nc.vector.wait_ge(c.sDVE.h, c.sDVE.n)
    c.sDVE.inc(nc.vector.reciprocal(r_tmp[:], r_var[:]))
    nc.scalar.wait_ge(c.sDVE.h, c.sDVE.n)
    c.sACT.inc(nc.scalar.activation(r_rstd[:], r_tmp[:], AF.Sqrt,
                                    bias=0.0, scale=1.0))
    c.sACT.inc(nc.scalar.activation(rb_mh[:], r_mean[:], AF.Copy,
                                    bias=0.0, scale=1.0))
    nc.scalar.wait_ge(c.sACT.h, c.sACT.n)
    c.sACT.inc(nc.scalar.activation(rb_rh[:], r_rstd[:], AF.Copy,
                                    bias=0.0, scale=1.0))
    nc.vector.wait_ge(c.sACT.h, c.sACT.n)
    c.sDVE.inc(nc.vector.scalar_tensor_tensor(
        rb_ml[:], r_mean[:], 1.0, rb_mh[:], ALU.mult, ALU.subtract))
    c.sDVE.inc(nc.vector.scalar_tensor_tensor(
        rb_rl[:], r_rstd[:], 1.0, rb_rh[:], ALU.mult, ALU.subtract))

    for tc in range(4):
        sl = slice(tc * 512, tc * 512 + 512)
        nc.tensor.wait_ge(c.sDVE.h, c.sDVE.n)
        nc.tensor.wait_ge(c.sPOOL.h, c.sPOOL.n)
        nc.tensor.matmul(PSmb[:], c.k_ones1[:], rb_mh[:, sl],
                         start=True, stop=False)
        c.sPE.inc(nc.tensor.matmul(PSmb[:], c.k_ones1[:], rb_ml[:, sl],
                                   start=False, stop=True))
        nc.tensor.matmul(PSrb[:], c.k_ones1[:], rb_rh[:, sl],
                         start=True, stop=False)
        c.sPE.inc(nc.tensor.matmul(PSrb[:], c.k_ones1[:], rb_rl[:, sl],
                                   start=False, stop=True))
        n_act = [0, 0]
        n_pool = [0, 0]
        pe_bc = c.sPE.n
        for cb in range(NB):
            j = cb % 2
            xb = c.XF[:, cb * T + tc * 512: cb * T + tc * 512 + 512]
            ab = slice(cb * T + tc * 512, cb * T + tc * 512 + 512)
            nc.vector.wait_ge(c.sPE.h, pe_bc)
            nc.vector.wait_ge(c.sACT.h, n_act[j])
            nc.vector.wait_ge(c.sPOOL.h, n_pool[j])
            c.sDVE.inc(nc.vector.tensor_sub(xn[j][:], xb, PSmb[:]))
            nc.vector.wait_ge(c.sDVE.h, c.sDVE.n)
            c.sDVE.inc(nc.vector.tensor_mul(xn[j][:], xn[j][:], PSrb[:]))
            xn_mark = c.sDVE.n
            nc.scalar.wait_ge(c.sDVE.h, xn_mark)
            c.sACT.inc(nc.scalar.activation(c.AH[:, ab], xn[j][:], AF.Copy,
                                            bias=0.0, scale=1.0))
            n_act[j] = c.sACT.n
            nc.gpsimd.wait_ge(c.sACT.h, n_act[j])
            c.sPOOL.inc(nc.gpsimd.tensor_sub(c.AL[:, ab], xn[j][:],
                                             c.AH[:, ab]))
            n_pool[j] = c.sPOOL.n
    bar(c)


def wcast0_gen(c):
    nc = c.nc
    ti = [c.place(f"wc_i{i}", [128, F0], I32, c.C0 + 36 * KB + i * 12 * KB)
          for i in range(1)]
    tb = [c.place(f"wc_b{i}", [128, F0], BF16, c.C0 + 48 * KB + i * 6 * KB)
          for i in range(2)]
    dve_mark = [0, 0]
    sp_toks = [None, None]
    for (srcp, dst, rows, cols) in ((c.wg0T, c.sp_wg0, NB, F0),
                                    (c.wu0T, c.sp_wu0, NB, F0),
                                    (c.wd0T, c.sp_wd0, NF0, D)):
        for r in range(rows):
            j = r % 2
            nc.sync.wait_ge(c.sDVE.h, dve_mark[0])
            c.LQ.begin()
            c.LQ.dma(ti[0][:, 0:cols], srcp[r * 128:(r + 1) * 128, :])
            tok = c.LQ.tok()
            c.LQ.wait(nc.vector, tok)
            if sp_toks[j] is not None:
                c.SQ.wait(nc.vector, sp_toks[j])
            c.sDVE.inc(nc.vector.tensor_copy(tb[j][:, 0:cols],
                                             ti[0][:, 0:cols]))
            dve_mark[0] = c.sDVE.n
            nc.gpsimd.wait_ge(c.sDVE.h, dve_mark[0])
            c.SQ.begin()
            c.SQ.dma(dst[r * 128:(r + 1) * 128, :], tb[j][:, 0:cols])
            sp_toks[j] = c.SQ.tok()
            yield
    while True:
        yield


def load_w_split(c, src_ap, wh, wl, nd):
    nc = c.nc
    W = wh.shape[1] // nd
    for d in range(nd):
        tmp = c.e_w32
        nc.sync.wait_ge(c.sACT.h, c.sACT.n)
        nc.sync.wait_ge(c.sDVE.h, c.sDVE.n)
        c.LQ.begin()
        c.LQ.dma(tmp[:, 0:W], src_ap[d * 128:(d + 1) * 128, :])
        c.LQ.wait(nc.scalar)
        c.sACT.inc(nc.scalar.activation(wh[:, d * W:(d + 1) * W], tmp[:, 0:W],
                                        AF.Copy, bias=0.0, scale=1.0))
        nc.vector.wait_ge(c.sACT.h, c.sACT.n)
        c.sDVE.inc(nc.vector.scalar_tensor_tensor(
            wl[:, d * W:(d + 1) * W], tmp[:, 0:W], 1.0,
            wh[:, d * W:(d + 1) * W], ALU.mult, ALU.subtract))
    nc.tensor.wait_ge(c.sDVE.h, c.sDVE.n)


def p_vproj(c):
    nc = c.nc
    wc = wcast0_gen(c)
    WVH = c.place("wvh", [128, NB * D], BF16, c.C0)
    WVL = c.place("wvl", [128, NB * D], BF16, c.C0 + 16 * KB)
    c.e_w32 = c.place("vw32", [128, D], F32, c.C0 + 32 * KB)
    load_w_split(c, c.wvT, WVH, WVL, NB)

    vhg = c.VH[:].rearrange("p (g w) -> p g w", w=65)
    vlg = c.VL[:].rearrange("p (g w) -> p g w", w=65)
    c.SQ.wait_all(nc.gpsimd)
    c.sPOOL.inc(nc.gpsimd.memset(vhg[:, :, 64:65], 1.0))
    c.sPOOL.inc(nc.gpsimd.memset(vlg[:, :, 64:65], 0.0))

    for tt in range(NT):
        for _ in range(3):
            next(wc)
        for cc in range(2):
            ps = c.PS[cc]
            nc.tensor.wait_ge(c.sACT.h, c.sACT.n)
            nc.tensor.wait_ge(c.sDVE.h, c.sDVE.n)
            for d in range(NB):
                ah = c.AH[:, d * T + tt * 128: d * T + tt * 128 + 128]
                al = c.AL[:, d * T + tt * 128: d * T + tt * 128 + 128]
                wh_s = WVH[:, d * D + cc * 512: d * D + cc * 512 + 512]
                wl_s = WVL[:, d * D + cc * 512: d * D + cc * 512 + 512]
                st = (d == 0)
                nc.tensor.matmul(ps[:], ah, wh_s, start=st, stop=False)
                nc.tensor.matmul(ps[:], ah, wl_s, start=False, stop=False)
                ins = nc.tensor.matmul(ps[:], al, wh_s, start=False,
                                       stop=(d == NB - 1))
            c.sPE.inc(ins)
        for cc in range(2):
            ps = c.PS[cc]
            dst_h = c.VH[:].rearrange("p (g w) -> p g w", w=65)[
                :, tt * 16 + cc * 8: tt * 16 + cc * 8 + 8, 0:64]
            dst_l = c.VL[:].rearrange("p (g w) -> p g w", w=65)[
                :, tt * 16 + cc * 8: tt * 16 + cc * 8 + 8, 0:64]
            src = ps[:].rearrange("p (g w) -> p g w", w=64)
            nc.scalar.wait_ge(c.sPE.h, c.sPE.n)
            c.sACT.inc(nc.scalar.activation(dst_h, src, AF.Copy, bias=0.0,
                                            scale=1.0))
            nc.vector.wait_ge(c.sACT.h, c.sACT.n)
            nc.vector.wait_ge(c.sPE.h, c.sPE.n)
            c.sDVE.inc(nc.vector.scalar_tensor_tensor(
                dst_l, src, 1.0, dst_h, ALU.mult, ALU.subtract))
    for _ in range(40):
        next(wc)
    nc.gpsimd.wait_ge(c.sDVE.h, c.sDVE.n)
    nc.gpsimd.wait_ge(c.sACT.h, c.sACT.n)
    nc.gpsimd.wait_ge(c.sPOOL.h, c.sPOOL.n)
    c.SQ.begin()
    c.SQ.dma(c.sp_vh[:], c.VH[:])
    c.SQ.dma(c.sp_vl[:], c.VL[:])
    c.SQ.wait(nc.sync)
    bar(c)


def p_ffn0(c):
    nc = c.nc
    wgt = [c.place(f"f0_wg{i}", [128, NB * 128], BF16, c.C0 + i * 2 * KB)
           for i in range(2)]
    wut = [c.place(f"f0_wu{i}", [128, NB * 128], BF16,
                   c.C0 + 4 * KB + i * 2 * KB) for i in range(2)]
    sg = c.place("f0_sg", [128, T], F32, c.C0 + 8 * KB)
    hh = [c.place(f"f0_hh{i}", [128, T], BF16, c.C0 + 16 * KB + i * 4 * KB)
          for i in range(2)]
    hl = [c.place(f"f0_hl{i}", [128, T], BF16, c.C0 + 24 * KB + i * 4 * KB)
          for i in range(2)]
    hf = [c.place(f"f0_hf{i}", [128, 512], F32, c.C0 + 32 * KB + i * 2 * KB)
          for i in range(2)]
    w_toks = [None, None]
    buf_pe = [0, 0]
    act_g = [0] * 4
    dve_u = [0] * 4
    act_hf = [0, 0]
    pool_hf = [0, 0]
    sp_toks = [None, None]
    hl_pool = [0, 0]

    def load_w(cb):
        j = cb % 2
        nc.sync.wait_ge(c.sPE.h, buf_pe[j])
        c.LQ.begin()
        c.LQ.dma(wgt[j][:].rearrange("p (d w) -> p d w", w=128),
                 c.sp_wg0[:, cb * 128:(cb + 1) * 128].rearrange(
                     "(d p) w -> p d w", p=128))
        c.LQ.dma(wut[j][:].rearrange("p (d w) -> p d w", w=128),
                 c.sp_wu0[:, cb * 128:(cb + 1) * 128].rearrange(
                     "(d p) w -> p d w", p=128))
        w_toks[j] = c.LQ.tok()

    wq = wq1_gen(c)
    load_w(0)
    for cb in range(NF0):
        for _ in range(5):
            next(wq)
        j = cb % 2
        if cb + 1 < NF0:
            load_w(cb + 1)
        c.LQ.wait(nc.tensor, w_toks[j])
        for tc in range(4):
            nc.tensor.wait_ge(c.sACT.h, act_g[tc])
            if tc == 0:
                nc.tensor.wait_ge(c.sDVE.h, dve_u[3])
            ins = None
            for d in range(NB):
                lh = wgt[j][:, d * 128:(d + 1) * 128]
                ah = c.AH[:, d * T + tc * 512: d * T + tc * 512 + 512]
                al = c.AL[:, d * T + tc * 512: d * T + tc * 512 + 512]
                nc.tensor.matmul(c.PS[tc][:], lh, ah, start=(d == 0),
                                 stop=False)
                ins = nc.tensor.matmul(c.PS[tc][:], lh, al, start=False,
                                       stop=(d == NB - 1))
            c.sPE.inc(ins)
            pe_mark = c.sPE.n
            nc.scalar.wait_ge(c.sPE.h, pe_mark)
            nc.scalar.wait_ge(c.sDVE.h, dve_u[tc])
            c.sACT.inc(nc.scalar.activation(
                sg[:, tc * 512:(tc + 1) * 512], c.PS[tc][:], AF.Silu,
                bias=0.0, scale=1.0 / 127.0))
            act_g[tc] = c.sACT.n
        for tc in range(4):
            bank = 4 + tc if tc < 3 else 0
            nc.tensor.wait_ge(c.sDVE.h, dve_u[tc])
            if bank == 6:
                nc.tensor.wait_ge(c.sDVE.h, getattr(c, "ps6_user", 0))
            if tc == 3:
                nc.tensor.wait_ge(c.sACT.h, act_g[0])
            ins = None
            for d in range(NB):
                lh = wut[j][:, d * 128:(d + 1) * 128]
                ah = c.AH[:, d * T + tc * 512: d * T + tc * 512 + 512]
                al = c.AL[:, d * T + tc * 512: d * T + tc * 512 + 512]
                nc.tensor.matmul(c.PS[bank][:], lh, ah, start=(d == 0),
                                 stop=False)
                ins = nc.tensor.matmul(c.PS[bank][:], lh, al, start=False,
                                       stop=(d == NB - 1))
            c.sPE.inc(ins)
            pe_mark = c.sPE.n
            k = tc % 2
            nc.vector.wait_ge(c.sPE.h, pe_mark)
            nc.vector.wait_ge(c.sACT.h, max(act_g[tc], act_hf[k]))
            nc.vector.wait_ge(c.sPOOL.h, pool_hf[k])
            c.sDVE.inc(nc.vector.scalar_tensor_tensor(
                hf[k][:], c.PS[bank][:], 1.0 / 127.0,
                sg[:, tc * 512:(tc + 1) * 512], ALU.mult, ALU.mult))
            dve_u[tc] = c.sDVE.n
            nc.scalar.wait_ge(c.sDVE.h, dve_u[tc])
            if sp_toks[j] is not None:
                c.SQ.wait(nc.scalar, sp_toks[j])
            c.sACT.inc(nc.scalar.activation(
                hh[j][:, tc * 512:(tc + 1) * 512], hf[k][:], AF.Copy,
                bias=0.0, scale=1.0))
            act_hf[k] = c.sACT.n
            nc.gpsimd.wait_ge(c.sACT.h, act_hf[k])
            if sp_toks[j] is not None:
                c.SQ.wait(nc.gpsimd, sp_toks[j])
            c.sPOOL.inc(nc.gpsimd.tensor_sub(
                hl[j][:, tc * 512:(tc + 1) * 512], hf[k][:],
                hh[j][:, tc * 512:(tc + 1) * 512]))
            pool_hf[k] = c.sPOOL.n
            if bank == 6:
                c.ps6_guard = dve_u[tc]
        hl_pool[j] = c.sPOOL.n
        buf_pe[j] = c.sPE.n
        nc.gpsimd.wait_ge(c.sPOOL.h, hl_pool[j])
        nc.gpsimd.wait_ge(c.sACT.h, c.sACT.n)
        c.SQ.begin()
        c.SQ.dma(c.sp_hh[cb * 128:(cb + 1) * 128, :], hh[j][:])
        c.SQ.dma(c.sp_hl[cb * 128:(cb + 1) * 128, :], hl[j][:])
        sp_toks[j] = c.SQ.tok()
    c.SQ.wait_all(nc.sync)
    bar(c)

    hhc = c.place("f0_hhc", [128, NF0 * 512], BF16, c.C0)
    hlc = c.place("f0_hlc", [128, NF0 * 512], BF16, c.C0 + 24 * KB)
    wdc = [c.place(f"f0_wdc{i}", [128, NF0 * 128], BF16,
                   c.C0 + 48 * KB + i * 6 * KB) for i in range(2)]
    c.e_g512 = c.place("f0_g512", [128, 512], F32, c.C0 + 60 * KB)
    for tc in range(4):
        nc.sync.wait_ge(c.sPE.h, c.sPE.n)
        c.LQ.begin()
        c.LQ.dma(hhc[:].rearrange("p (h w) -> p h w", w=512),
                 c.sp_hh[:, tc * 512:(tc + 1) * 512].rearrange(
                     "(h p) w -> p h w", p=128))
        c.LQ.dma(hlc[:].rearrange("p (h w) -> p h w", w=512),
                 c.sp_hl[:, tc * 512:(tc + 1) * 512].rearrange(
                     "(h p) w -> p h w", p=128))
        h_tok = c.LQ.tok()
        for db in range(NB):
            buf = wdc[db % 2]
            nc.sync.wait_ge(c.sPE.h, max(0, c.sPE.n - 1))
            c.LQ.begin()
            c.LQ.dma(buf[:].rearrange("p (h w) -> p h w", w=128),
                     c.sp_wd0[:, db * 128:(db + 1) * 128].rearrange(
                         "(h p) w -> p h w", p=128))
            c.LQ.wait(nc.tensor)
            c.LQ.wait(nc.tensor, h_tok)
            nc.tensor.wait_ge(c.sACT.h, c.sACT.n)
            ps = c.PS[db % 2]
            for hb in range(NF0):
                lh = buf[:, hb * 128:(hb + 1) * 128]
                nc.tensor.matmul(ps[:], lh,
                                 hhc[:, hb * 512:(hb + 1) * 512],
                                 start=(hb == 0), stop=False)
                ins = nc.tensor.matmul(ps[:], lh,
                                       hlc[:, hb * 512:(hb + 1) * 512],
                                       start=False, stop=(hb == NF0 - 1))
            c.sPE.inc(ins)
            nc.scalar.wait_ge(c.sPE.h, c.sPE.n)
            c.SQ.wait_all(nc.scalar)
            c.sACT.inc(nc.scalar.activation(
                c.e_g512[:], ps[:], AF.Silu, bias=0.0, scale=1.0 / 127.0))
            nc.gpsimd.wait_ge(c.sACT.h, c.sACT.n)
            c.SQ.begin()
            c.SQ.dma(c.sp_g[db * 128:(db + 1) * 128, tc * 512:(tc + 1) * 512],
                     c.e_g512[:])
            for _ in range(2):
                next(wq)
    for _ in range(180):
        next(wq)
    c.SQ.wait_all(nc.sync)
    bar(c)


def p_rope(c):
    nc = c.nc
    sct = c.place("rp_sc", [128, 4 * T], F32, c.C0)
    xnr = c.place("rp_xnr", [128, T], F32, c.C0 + 32 * KB)
    ra = c.place("rp_a", [128, 512], F32, c.C0 + 40 * KB)
    rb = c.place("rp_b", [128, 512], F32, c.C0 + 42 * KB)
    rx = c.place("rp_rx", [128, 512], F32, c.C0 + 44 * KB)
    rc = c.place("rp_c", [128, 512], F32, c.C0 + 46 * KB)
    for cb in range(NB):
        nc.sync.wait_ge(c.sDVE.h, c.sDVE.n)
        nc.sync.wait_ge(c.sPOOL.h, c.sPOOL.n)
        c.LQ.begin()
        c.LQ.dma(sct[:, 0:T], c.sin[cb * 128:(cb + 1) * 128, :])
        c.LQ.dma(sct[:, T:2 * T], c.cos[cb * 128:(cb + 1) * 128, :])
        c.LQ.dma(sct[:, 2 * T:3 * T], c.sinl[cb * 128:(cb + 1) * 128, :])
        c.LQ.dma(sct[:, 3 * T:4 * T], c.cosl[cb * 128:(cb + 1) * 128, :])
        ah_b = c.AH[:, cb * T:(cb + 1) * T]
        al_b = c.AL[:, cb * T:(cb + 1) * T]
        nc.vector.wait_ge(c.sACT.h, c.sACT.n)
        c.sDVE.inc(nc.vector.tensor_add(xnr[:], ah_b, al_b))
        for tc in range(4):
            sl = slice(tc * 512, (tc + 1) * 512)
            nc.tensor.wait_ge(c.sPOOL.h, c.sPOOL.n)
            nc.tensor.wait_ge(c.sDVE.h, c.sDVE.n)
            nc.tensor.matmul(c.PS[0][:], c.k_rotPT[:], ah_b[:, sl],
                             start=True, stop=False)
            c.sPE.inc(nc.tensor.matmul(c.PS[0][:], c.k_rotPT[:], al_b[:, sl],
                                       start=False, stop=True))
            c.LQ.wait(nc.gpsimd)
            nc.gpsimd.wait_ge(c.sDVE.h, c.sDVE.n)
            c.sPOOL.inc(nc.gpsimd.tensor_mul(
                ra[:], xnr[:, sl], sct[:, T + tc * 512: T + tc * 512 + 512]))
            nc.gpsimd.wait_ge(c.sPOOL.h, c.sPOOL.n)
            c.sPOOL.inc(nc.gpsimd.tensor_mul(
                rc[:], xnr[:, sl],
                sct[:, 3 * T + tc * 512: 3 * T + tc * 512 + 512]))
            nc.gpsimd.wait_ge(c.sPOOL.h, c.sPOOL.n)
            c.sPOOL.inc(nc.gpsimd.tensor_add(ra[:], ra[:], rc[:]))
            nc.vector.wait_ge(c.sPE.h, c.sPE.n)
            c.LQ.wait(nc.vector)
            nc.vector.wait_ge(c.sDVE.h, c.sDVE.n)
            c.sDVE.inc(nc.vector.tensor_mul(rb[:], c.PS[0][:],
                                            sct[:, tc * 512: tc * 512 + 512]))
            nc.vector.wait_ge(c.sDVE.h, c.sDVE.n)
            c.sDVE.inc(nc.vector.tensor_mul(
                rx[:], c.PS[0][:],
                sct[:, 2 * T + tc * 512: 2 * T + tc * 512 + 512]))
            nc.vector.wait_ge(c.sDVE.h, c.sDVE.n)
            c.sDVE.inc(nc.vector.tensor_add(rb[:], rb[:], rx[:]))
            nc.vector.wait_ge(c.sPOOL.h, c.sPOOL.n)
            nc.vector.wait_ge(c.sDVE.h, c.sDVE.n)
            c.sDVE.inc(nc.vector.tensor_add(rx[:], ra[:], rb[:]))
            nc.scalar.wait_ge(c.sDVE.h, c.sDVE.n)
            c.sACT.inc(nc.scalar.activation(ah_b[:, sl], rx[:], AF.Copy,
                                            bias=0.0, scale=1.0))
            nc.vector.wait_ge(c.sACT.h, c.sACT.n)
            c.sDVE.inc(nc.vector.scalar_tensor_tensor(
                al_b[:, sl], rx[:], 1.0, ah_b[:, sl], ALU.mult, ALU.subtract))
    bar(c)


def p_qk(c):
    nc = c.nc
    c.e_w32 = c.place("qk_w32", [128, D], F32, c.A0)
    qh = c.place("qk_qh", [128, T], BF16, c.A0 + 4 * KB)
    ql = c.place("qk_ql", [128, T], BF16, c.A0 + 8 * KB)
    wbh = c.place("qk_wbh", [128, NB * D], BF16, c.A0 + 12 * KB)
    wbl = c.place("qk_wbl", [128, NB * D], BF16, c.A0 + 28 * KB)
    for which, wsrc in (("k", c.wkT), ("q", c.wqT)):
        load_w_split(c, wsrc, wbh, wbl, NB)
        for cb in range(NB):
            pe0 = c.sPE.n
            for tc in range(4):
                ps = c.PS[tc]
                nc.tensor.wait_ge(c.sACT.h, c.sACT.n)
                nc.tensor.wait_ge(c.sDVE.h, c.sDVE.n)
                for d in range(NB):
                    lh_h = wbh[:, d * D + cb * 128: d * D + cb * 128 + 128]
                    lh_l = wbl[:, d * D + cb * 128: d * D + cb * 128 + 128]
                    bh = c.AH[:, d * T + tc * 512: d * T + tc * 512 + 512]
                    bl = c.AL[:, d * T + tc * 512: d * T + tc * 512 + 512]
                    nc.tensor.matmul(ps[:], lh_h, bh, start=(d == 0),
                                     stop=False)
                    nc.tensor.matmul(ps[:], lh_h, bl, start=False, stop=False)
                    ins = nc.tensor.matmul(ps[:], lh_l, bh, start=False,
                                           stop=(d == NB - 1))
                c.sPE.inc(ins)
            nc.scalar.wait_ge(c.sPE.h, c.sPE.n)
            for tc in range(4):
                sl = slice(tc * 512, (tc + 1) * 512)
                if which == "k":
                    dsth = c.KH[:, cb * T + tc * 512: cb * T + tc * 512 + 512]
                    dstl = c.KL[:, cb * T + tc * 512: cb * T + tc * 512 + 512]
                else:
                    dsth, dstl = qh[:, sl], ql[:, sl]
                c.sACT.inc(nc.scalar.activation(dsth, c.PS[tc][:], AF.Copy,
                                                bias=0.0, scale=1.0))
                nc.vector.wait_ge(c.sACT.h, c.sACT.n)
                c.sDVE.inc(nc.vector.scalar_tensor_tensor(
                    dstl, c.PS[tc][:], 1.0, dsth, ALU.mult, ALU.subtract))
            if which == "q":
                nc.gpsimd.wait_ge(c.sDVE.h, c.sDVE.n)
                nc.gpsimd.wait_ge(c.sACT.h, c.sACT.n)
                c.SQ.begin()
                c.SQ.dma(c.sp_qh[cb * 128:(cb + 1) * 128, :], qh[:])
                c.SQ.dma(c.sp_ql[cb * 128:(cb + 1) * 128, :], ql[:])
                c.SQ.wait(nc.scalar)
                c.SQ.wait(nc.vector)
        bar(c)
    bar(c)


def p_attn(c):
    nc = c.nc
    c.LQ.begin()
    c.LQ.dma(c.VH[:], c.sp_vh[:])
    c.LQ.dma(c.VL[:], c.sp_vl[:])
    v_tok = c.LQ.tok()
    B0 = c.B0
    qch = c.place("at_qch", [128, NB * 512], BF16, B0)
    qcl = c.place("at_qcl", [128, NB * 512], BF16, B0 + 8 * KB)
    eth = c.place("at_eth", [128, 8 * 512], BF16, B0 + 16 * KB)
    etl = c.place("at_etl", [128, 8 * 512], BF16, B0 + 24 * KB)
    e32 = c.place("at_e32", [128, 8 * 512], F32, B0 + 32 * KB)
    zbs = c.place("at_zbs", [64, 512], F32, B0 + 48 * KB)
    t32 = c.place("at_t32", [64, 2 * 512], F32, B0 + 50 * KB)
    ohst = c.place("at_oh", [64, 2 * 512], BF16, B0 + 54 * KB)
    olst = c.place("at_ol", [64, 2 * 512], BF16, B0 + 56 * KB)
    zrow = c.place("at_zr", [1, 512], F32, B0 + 58 * KB)
    zinv = c.place("at_zi", [1, 512], F32, B0 + 60 * KB)
    zih = c.place("at_zih", [1, 512], BF16, B0 + 62 * KB)
    zil = c.place("at_zil", [1, 512], BF16, B0 + 63 * KB)

    act_bank = [0] * 4
    pe_bank = [0] * 4
    dve_slot = [0] * 8
    pool_slot = [0] * 8
    pe_slot = [0] * 8
    dve_mask = [0] * 4
    pv_pe = [0, 0]
    evac_done = [0, 0]
    sp_tok = [None, None]

    for qc in range(NQC):
        nc.sync.wait_ge(c.sPE.h, c.sPE.n)
        c.LQ.begin()
        c.LQ.dma(qch[:].rearrange("p (b w) -> p b w", w=512),
                 c.sp_qh[:, qc * 512:(qc + 1) * 512].rearrange(
                     "(b p) w -> p b w", p=128))
        c.LQ.dma(qcl[:].rearrange("p (b w) -> p b w", w=512),
                 c.sp_ql[:, qc * 512:(qc + 1) * 512].rearrange(
                     "(b p) w -> p b w", p=128))
        c.LQ.wait(nc.tensor)
        c.LQ.wait(nc.tensor, v_tok)
        nkc = 4 * qc + 4
        for hp in range(NH // 2):
            co = hp

            def s_group(kc):
                p = kc % 2
                for hi_half in range(2):
                    bank = 2 * hi_half + p
                    po = 64 * hi_half
                    ps = c.PS[bank]
                    lh_h = c.KH[po:po + 64,
                                co * T + kc * 128: co * T + kc * 128 + 128]
                    lh_l = c.KL[po:po + 64,
                                co * T + kc * 128: co * T + kc * 128 + 128]
                    r_h = qch[po:po + 64, co * 512:(co + 1) * 512]
                    r_l = qcl[po:po + 64, co * 512:(co + 1) * 512]
                    nc.tensor.wait_ge(c.sACT.h, act_bank[bank])
                    nc.tensor.wait_ge(c.sDVE.h, dve_mask[bank])
                    nc.tensor.matmul(ps[:], lh_h, r_h, start=True, stop=False)
                    nc.tensor.matmul(ps[:], lh_h, r_l, start=False,
                                     stop=False)
                    c.sPE.inc(nc.tensor.matmul(ps[:], lh_l, r_h, start=False,
                                               stop=True))
                    pe_bank[bank] = c.sPE.n

            def epv(kc):
                p = kc % 2
                delta = kc * 128 - qc * 512
                for hi_half in range(2):
                    bank = 2 * hi_half + p
                    slot = (kc % 4) * 2 + hi_half
                    ps = c.PS[bank]
                    sl = slice(slot * 512, slot * 512 + 512)
                    if delta >= 0:
                        j = delta // 128
                        nc.vector.wait_ge(c.sPE.h, pe_bank[bank])
                        c.sDVE.inc(nc.vector.tensor_add(
                            ps[:], ps[:],
                            c.k_masks[:, j * 512:(j + 1) * 512]))
                        dve_mask[bank] = c.sDVE.n
                        nc.scalar.wait_ge(c.sDVE.h, dve_mask[bank])
                    else:
                        nc.scalar.wait_ge(c.sPE.h, pe_bank[bank])
                    nc.scalar.wait_ge(c.sPOOL.h, pool_slot[slot])
                    nc.scalar.wait_ge(c.sPE.h, pe_slot[slot])
                    c.sACT.inc(nc.scalar.activation(
                        eth[:, sl], ps[:], AF.Exp, bias=0.0, scale=0.125))
                    hi_mark = c.sACT.n
                    c.sACT.inc(nc.scalar.activation(
                        e32[:, sl], ps[:], AF.Exp, bias=0.0, scale=0.125))
                    act_bank[bank] = c.sACT.n
                    dve_slot[slot] = hi_mark
                    nc.gpsimd.wait_ge(c.sACT.h, act_bank[bank])
                    nc.gpsimd.wait_ge(c.sPE.h, pe_slot[slot])
                    c.sPOOL.inc(nc.gpsimd.tensor_sub(etl[:, sl], e32[:, sl],
                                                     eth[:, sl]))
                    pool_slot[slot] = c.sPOOL.n
                    h = 2 * hp + hi_half
                    pvps = c.PS[4 + hi_half]
                    lh_vh = c.VH[:, (kc * 16 + h) * 65:
                                  (kc * 16 + h) * 65 + 65]
                    lh_vl = c.VL[:, (kc * 16 + h) * 65:
                                  (kc * 16 + h) * 65 + 65]
                    nc.tensor.wait_ge(c.sACT.h, hi_mark)
                    if kc == 0:
                        nc.tensor.wait_ge(c.sDVE.h, evac_done[hi_half])
                    st = (kc == 0)
                    sp = (kc == nkc - 1)
                    nc.tensor.matmul(pvps[0:65, :], lh_vh, eth[:, sl],
                                     start=st, stop=False)
                    nc.tensor.matmul(pvps[0:65, :], lh_vl, eth[:, sl],
                                     start=False, stop=False)
                    nc.tensor.wait_ge(c.sPOOL.h, pool_slot[slot])
                    c.sPE.inc(nc.tensor.matmul(pvps[0:65, :], lh_vh,
                                               etl[:, sl], start=False,
                                               stop=sp))
                    pe_slot[slot] = c.sPE.n
                    pv_pe[hi_half] = c.sPE.n

            s_group(0)
            for kc in range(nkc):
                if kc + 1 < nkc:
                    s_group(kc + 1)
                epv(kc)
            for hi_half in range(2):
                h = 2 * hp + hi_half
                pvps = c.PS[4 + hi_half]
                hsl = slice(hi_half * 512, hi_half * 512 + 512)
                nc.vector.wait_ge(c.sPE.h, pv_pe[hi_half])
                if sp_tok[hi_half] is not None:
                    c.SQ.wait(nc.vector, sp_tok[hi_half])
                c.sDVE.inc(nc.vector.tensor_copy(zrow[:],
                                                 pvps[64:65, :]))
                nc.vector.wait_ge(c.sDVE.h, c.sDVE.n)
                c.sDVE.inc(nc.vector.reciprocal(zinv[:], zrow[:]))
                nc.scalar.wait_ge(c.sDVE.h, c.sDVE.n)
                c.sACT.inc(nc.scalar.activation(zih[:], zinv[:],
                                                AF.Copy, bias=0.0, scale=1.0))
                nc.vector.wait_ge(c.sACT.h, c.sACT.n)
                c.sDVE.inc(nc.vector.scalar_tensor_tensor(
                    zil[:], zinv[:], 1.0, zih[:], ALU.mult,
                    ALU.subtract))
                nc.tensor.wait_ge(c.sDVE.h, c.sDVE.n)
                nc.tensor.wait_ge(c.sDVE.h, evac_done[hi_half])
                nc.tensor.matmul(c.PS[6][0:64, :], c.k_ones1[:, 0:64],
                                 zih[:], start=True, stop=False)
                c.sPE.inc(nc.tensor.matmul(c.PS[6][0:64, :],
                                           c.k_ones1[:, 0:64], zil[:],
                                           start=False, stop=True))
                nc.vector.wait_ge(c.sPE.h, c.sPE.n)
                c.sDVE.inc(nc.vector.tensor_copy(zbs[:], c.PS[6][0:64, :]))
                nc.vector.wait_ge(c.sDVE.h, c.sDVE.n)
                c.sDVE.inc(nc.vector.tensor_mul(t32[:, hsl], pvps[0:64, :],
                                                zbs[:]))
                evac_done[hi_half] = c.sDVE.n
                nc.scalar.wait_ge(c.sDVE.h, evac_done[hi_half])
                if sp_tok[hi_half] is not None:
                    c.SQ.wait(nc.scalar, sp_tok[hi_half])
                    c.SQ.wait(nc.gpsimd, sp_tok[hi_half])
                c.sACT.inc(nc.scalar.activation(ohst[:, hsl], t32[:, hsl],
                                                AF.Copy, bias=0.0, scale=1.0))
                nc.gpsimd.wait_ge(c.sACT.h, c.sACT.n)
                nc.gpsimd.wait_ge(c.sDVE.h, evac_done[hi_half])
                c.sPOOL.inc(nc.gpsimd.tensor_sub(olst[:, hsl], t32[:, hsl],
                                                 ohst[:, hsl]))
                nc.gpsimd.wait_ge(c.sPOOL.h, c.sPOOL.n)
                c.SQ.begin()
                c.SQ.dma(c.sp_oh[h * 64:(h + 1) * 64,
                                 qc * 512:(qc + 1) * 512], ohst[:, hsl])
                c.SQ.dma(c.sp_ol[h * 64:(h + 1) * 64,
                                 qc * 512:(qc + 1) * 512], olst[:, hsl])
                sp_tok[hi_half] = c.SQ.tok()
    bar(c)


def p_oproj_combine(c):
    nc = c.nc
    c.e_w32 = c.place("op_w32", [128, D], F32, c.A0)
    woh = c.place("op_woh", [128, NB * D], BF16, c.A0 + 4 * KB)
    wol = c.place("op_wol", [128, NB * D], BF16, c.A0 + 20 * KB)
    ohc = c.place("op_ohc", [128, NB * 512], BF16, c.A0 + 36 * KB)
    olc = c.place("op_olc", [128, NB * 512], BF16, c.A0 + 44 * KB)
    load_w_split(c, c.woT, woh, wol, NB)
    for tc in range(4):
        nc.sync.wait_ge(c.sPE.h, c.sPE.n)
        c.SQ.wait_all(nc.sync)
        c.LQ.begin()
        c.LQ.dma(ohc[:].rearrange("p (b w) -> p b w", w=512),
                 c.sp_oh[:, tc * 512:(tc + 1) * 512].rearrange(
                     "(b p) w -> p b w", p=128))
        c.LQ.dma(olc[:].rearrange("p (b w) -> p b w", w=512),
                 c.sp_ol[:, tc * 512:(tc + 1) * 512].rearrange(
                     "(b p) w -> p b w", p=128))
        c.LQ.wait(nc.tensor)
        for cb in range(NB):
            ps = c.PS[cb % 2]
            nc.tensor.wait_ge(c.sACT.h, c.sACT.n)
            for d in range(NB):
                lh_h = woh[:, d * D + cb * 128: d * D + cb * 128 + 128]
                lh_l = wol[:, d * D + cb * 128: d * D + cb * 128 + 128]
                oh_s = ohc[:, d * 512:(d + 1) * 512]
                ol_s = olc[:, d * 512:(d + 1) * 512]
                nc.tensor.matmul(ps[:], lh_h, oh_s, start=(d == 0),
                                 stop=False)
                nc.tensor.matmul(ps[:], lh_h, ol_s, start=False, stop=False)
                ins = nc.tensor.matmul(ps[:], lh_l, oh_s, start=False,
                                       stop=(d == NB - 1))
            c.sPE.inc(ins)
            nc.scalar.wait_ge(c.sPE.h, c.sPE.n)
            c.sACT.inc(nc.scalar.activation(
                c.SF[:, cb * T + tc * 512: cb * T + tc * 512 + 512], ps[:],
                AF.Identity, bias=c.k_bo[:, cb:cb + 1], scale=1.0))
    bar(c)
    xb = c.place("cmb_x", [128, 512], F32, c.C0)
    gb = c.place("cmb_g", [128, 512], F32, c.C0 + 2 * KB)
    tb = c.place("cmb_t", [128, 512], F32, c.C0 + 4 * KB)
    for cb in range(NB):
        for tc in range(4):
            sl = slice(cb * T + tc * 512, cb * T + tc * 512 + 512)
            nc.sync.wait_ge(c.sDVE.h, c.sDVE.n)
            nc.sync.wait_ge(c.sPOOL.h, c.sPOOL.n)
            c.LQ.begin()
            c.LQ.dma(xb[:], c.sp_x[cb * 128:(cb + 1) * 128,
                                   tc * 512:(tc + 1) * 512])
            c.LQ.dma(gb[:], c.sp_g[cb * 128:(cb + 1) * 128,
                                   tc * 512:(tc + 1) * 512])
            c.LQ.wait(nc.gpsimd)
            nc.gpsimd.wait_ge(c.sDVE.h, c.sDVE.n)
            c.sPOOL.inc(nc.gpsimd.tensor_mul(tb[:], c.SF[:, sl], gb[:]))
            nc.vector.wait_ge(c.sPOOL.h, c.sPOOL.n)
            c.sDVE.inc(nc.vector.tensor_add(c.XF[:, sl], xb[:], tb[:]))
    bar(c)


def ladder_floor(c, t, u, cl, width):
    nc = c.nc
    nc.vector.wait_ge(c.sPOOL.h, c.sPOOL.n)
    nc.vector.wait_ge(c.sDVE.h, c.sDVE.n)
    c.sDVE.inc(nc.vector.tensor_copy(u[:, 0:width], t[:, 0:width]))
    for i in range(7, -1, -1):
        v = float(2 ** i)
        nc.vector.wait_ge(c.sDVE.h, c.sDVE.n)
        c.sDVE.inc(nc.vector.tensor_scalar(
            cl[:, 0:width], u[:, 0:width], v, -v, ALU.is_ge, ALU.mult))
        nc.vector.wait_ge(c.sDVE.h, c.sDVE.n)
        c.sDVE.inc(nc.vector.tensor_add(u[:, 0:width], u[:, 0:width],
                                        cl[:, 0:width]))


def p_l1_quant(c):
    nc = c.nc
    c.SQ.wait_all(nc.scalar)
    xt = [c.place(f"q1_xt{i}", [128, D], F32, c.A0 + i * 4 * KB)
          for i in range(2)]
    tq = c.place("q1_tq", [128, D], F32, c.A0 + 8 * KB)
    uq = c.place("q1_uq", [128, D], F32, c.A0 + 12 * KB)
    cq = c.place("q1_cq", [128, D], F32, c.A0 + 16 * KB)
    ntm = [c.place(f"q1_ntm{i}", [128, D], F32, c.A0 + 20 * KB + i * 4 * KB)
           for i in range(2)]
    nfs = [c.place(f"q1_nfs{i}", [128, D], BF16, c.C0 + 16 * KB + i * 2 * KB)
           for i in range(2)]
    colm = [c.place(f"q1_colm{i}", [128, 4], F32, c.A0 + 28 * KB + i * 64)
            for i in range(2)]
    xt_free = [0, 0]
    ntm_free = [0, 0]
    ps4_act = [0]
    psb_state = [0]
    nfs_tok = [None, None]
    amax_mark = [0, 0]

    def stage_A(tt):
        par = tt % 2
        psb_mark = psb_state[0]
        for cb in range(NB):
            nc.tensor.wait_ge(c.sDVE.h, psb_mark)
            if cb == 0:
                nc.tensor.wait_ge(c.sDVE.h, xt_free[par])
            nc.tensor.transpose(
                c.PSB[:, 0:128],
                c.AH[:, cb * T + tt * 128: cb * T + tt * 128 + 128],
                c.k_ident[:])
            c.sPE.inc(nc.tensor.transpose(
                c.PSB[:, 128:256],
                c.AL[:, cb * T + tt * 128: cb * T + tt * 128 + 128],
                c.k_ident[:]))
            nc.scalar.wait_ge(c.sPE.h, c.sPE.n)
            c.sACT.inc(nc.scalar.activation(
                xt[par][:, cb * 128:(cb + 1) * 128], c.PSB[:, 0:128],
                AF.Copy, bias=0.0, scale=1.0))
            nc.vector.wait_ge(c.sACT.h, c.sACT.n)
            c.sDVE.inc(nc.vector.tensor_add(
                xt[par][:, cb * 128:(cb + 1) * 128],
                xt[par][:, cb * 128:(cb + 1) * 128], c.PSB[:, 128:256]))
            psb_mark = c.sDVE.n
        psb_state[0] = c.sDVE.n
        cm = colm[par]
        nc.vector.wait_ge(c.sDVE.h, c.sDVE.n)
        c.sDVE.inc(nc.vector.tensor_reduce(
            cm[:, 0:1], xt[par][:], AX.X, ALU.max,
            apply_absolute_value=True))
        nc.vector.wait_ge(c.sDVE.h, c.sDVE.n)
        c.sDVE.inc(nc.vector.tensor_scalar(cm[:, 1:2], cm[:, 0:1], 1e-5,
                                           None, ALU.max))
        nc.vector.wait_ge(c.sDVE.h, c.sDVE.n)
        c.sDVE.inc(nc.vector.reciprocal(cm[:, 2:3], cm[:, 1:2]))
        nc.vector.wait_ge(c.sDVE.h, c.sDVE.n)
        c.sDVE.inc(nc.vector.tensor_scalar(cm[:, 3:4], cm[:, 2:3], 127.0,
                                           None, ALU.mult))
        c.sDVE.inc(nc.vector.tensor_scalar(c.sinv[:, tt:tt + 1],
                                           cm[:, 1:2], 1.0 / 127.0,
                                           None, ALU.mult))
        amax_mark[par] = c.sDVE.n

    def stage_C(tt):
        par = tt % 2
        nc.vector.wait_ge(c.sDVE.h, amax_mark[par])
        c.sDVE.inc(nc.vector.tensor_scalar(
            tq[:], xt[par][:], colm[par][:, 3:4], 128.5, ALU.mult, ALU.add))
        xt_free[par] = c.sDVE.n
        ladder_floor(c, tq, uq, cq, D)
        nc.vector.wait_ge(c.sDVE.h, c.sDVE.n)
        nc.vector.wait_ge(c.sPOOL.h, c.sPOOL.n)
        nc.vector.wait_ge(c.sPE.h, ntm_free[par])
        c.sDVE.inc(nc.vector.scalar_tensor_tensor(
            ntm[par][:], tq[:], -128.0, uq[:], ALU.add, ALU.subtract))

    def stage_D(tt):
        par = tt % 2
        stt_mark = c.sDVE.n
        ev_mark = ps4_act[0]
        for cb4 in range(0, NB, 4):
            nc.tensor.wait_ge(c.sDVE.h, stt_mark)
            nc.tensor.wait_ge(c.sACT.h, ev_mark)
            for j in range(4):
                cb = cb4 + j
                ins = nc.tensor.transpose(
                    c.PS[4][:, j * 128:(j + 1) * 128],
                    ntm[par][:, cb * 128:(cb + 1) * 128], c.k_ident_f32[:])
            c.sPE.inc(ins)
            nc.scalar.wait_ge(c.sPE.h, c.sPE.n)
            if nfs_tok[par] is not None and cb4 == 0:
                c.SQ.wait(nc.scalar, nfs_tok[par])
            for j in range(4):
                cb = cb4 + j
                ins = nc.scalar.activation(
                    nfs[par][:, cb * 128:(cb + 1) * 128],
                    c.PS[4][:, j * 128:(j + 1) * 128], AF.Copy,
                    bias=0.0, scale=1.0)
            c.sACT.inc(ins)
            ev_mark = c.sACT.n
        ps4_act[0] = c.sACT.n
        ntm_free[par] = c.sPE.n
        nc.gpsimd.wait_ge(c.sACT.h, c.sACT.n)
        c.SQ.begin()
        c.SQ.dma(c.sp_n[:, tt * 128:(tt + 1) * 128].rearrange(
            "(cb p) w -> p cb w", p=128),
            nfs[par][:].rearrange("p (cb w) -> p cb w", w=128))
        nfs_tok[par] = c.SQ.tok()

    stage_A(0)
    for tt in range(NT):
        stage_C(tt)
        if tt + 1 < NT:
            stage_A(tt + 1)
        stage_D(tt)
    c.SQ.wait_all(nc.sync)
    bar(c)


def wq1_gen(c):
    nc = c.nc
    CW = 2048
    wbuf = [c.place(f"wq_b{i}", [128, CW], F32, c.A0 + i * 8 * KB)
            for i in range(2)]
    tb = [c.place(f"wq_t{i}", [128, CW], F32, c.A0 + 16 * KB + i * 8 * KB)
          for i in range(2)]
    cb1 = [c.place(f"wq_c{i}", [128, CW], F32, c.A0 + 32 * KB + i * 8 * KB)
           for i in range(2)]
    qb = [c.place(f"wq_q{i}", [128, CW], BF16, c.A0 + 48 * KB + i * 4 * KB)
          for i in range(2)]
    acc = c.place("wq_acc", [128, 8], F32, c.A0 + 56 * KB)
    acch = c.place("wq_acch", [128, 1], BF16, c.A0 + 57 * KB)
    accl = c.place("wq_accl", [128, 1], BF16, c.A0 + 57 * KB + 512)
    toks = [None, None]
    sp_toks = [None, None]
    dve_mark = [0, 0]
    pool_mark = [0, 0]
    for i, (wsrc, wdst, R, W) in enumerate(
            ((c.wg1T, c.sp_wg1, NB, F1), (c.wu1T, c.sp_wu1, NB, F1),
             (c.wd1T, c.sp_wd1, NF1, D))):
        NCH = W // CW if W >= CW else 1
        CWi = min(W, CW)
        nc.vector.wait_ge(c.sPOOL.h, c.sPOOL.n)
        c.sDVE.inc(nc.vector.memset(acc[:, 0:1], 0.0))
        for r in range(R):
            for ch in range(NCH):
                j = (r * NCH + ch) % 2
                nc.sync.wait_ge(c.sDVE.h, dve_mark[j])
                nc.sync.wait_ge(c.sPOOL.h, pool_mark[j])
                c.LQ.begin()
                c.LQ.dma(wbuf[j][:, 0:CWi],
                         wsrc[r * 128:(r + 1) * 128,
                              ch * CWi:(ch + 1) * CWi])
                toks[j] = c.LQ.tok()
                c.LQ.wait(nc.vector, toks[j])
                nc.vector.wait_ge(c.sPOOL.h, c.sPOOL.n)
                c.sDVE.inc(nc.vector.tensor_reduce(
                    acc[:, 1 + j:2 + j], wbuf[j][:, 0:CWi], AX.X, ALU.add,
                    apply_absolute_value=True))
                dve_mark[j] = c.sDVE.n
                pool_mark[j] = 0
                nc.gpsimd.wait_ge(c.sDVE.h, c.sDVE.n)
                c.sPOOL.inc(nc.gpsimd.tensor_add(acc[:, 0:1], acc[:, 0:1],
                                                 acc[:, 1 + j:2 + j]))
                yield
        nc.scalar.wait_ge(c.sPOOL.h, c.sPOOL.n)
        c.sACT.inc(nc.scalar.activation(acch[:], acc[:, 0:1], AF.Copy,
                                        bias=0.0, scale=1.0))
        nc.vector.wait_ge(c.sACT.h, c.sACT.n)
        c.sDVE.inc(nc.vector.scalar_tensor_tensor(
            accl[:], acc[:, 0:1], 1.0, acch[:], ALU.mult, ALU.subtract))
        nc.tensor.wait_ge(c.sDVE.h, c.sDVE.n)
        nc.tensor.wait_ge(c.sPOOL.h, c.sPOOL.n)
        nc.tensor.wait_ge(c.sDVE.h, getattr(c, "ps6_guard", 0))
        nc.tensor.matmul(c.PS[6][:, 0:1], c.k_allones[:], acch[:],
                         start=True, stop=False)
        c.sPE.inc(nc.tensor.matmul(c.PS[6][:, 0:1], c.k_allones[:], accl[:],
                                   start=False, stop=True))
        nc.vector.wait_ge(c.sPE.h, c.sPE.n)
        c.sDVE.inc(nc.vector.tensor_scalar(
            acc[:, 4:5], c.PS[6][:, 0:1], 1.0 / float(R * 128 * W), None,
            ALU.mult))
        c.ps6_user = c.sDVE.n
        nc.vector.wait_ge(c.sDVE.h, c.sDVE.n)
        c.sDVE.inc(nc.vector.tensor_scalar(
            c.cols[:, 2 * i + 1: 2 * i + 2], acc[:, 4:5], 1e-5, None,
            ALU.max))
        nc.vector.wait_ge(c.sDVE.h, c.sDVE.n)
        c.sDVE.inc(nc.vector.reciprocal(c.cols[:, 2 * i: 2 * i + 1],
                                        c.cols[:, 2 * i + 1: 2 * i + 2]))
        scale_mark = c.sDVE.n
        yield
        for r in range(R):
            for ch in range(NCH):
                j = (r * NCH + ch) % 2
                nc.sync.wait_ge(c.sPOOL.h, pool_mark[j])
                nc.sync.wait_ge(c.sDVE.h, dve_mark[j])
                c.LQ.begin()
                c.LQ.dma(wbuf[j][:, 0:CWi],
                         wsrc[r * 128:(r + 1) * 128,
                              ch * CWi:(ch + 1) * CWi])
                toks[j] = c.LQ.tok()
                c.LQ.wait(nc.vector, toks[j])
                nc.vector.wait_ge(c.sDVE.h,
                                  scale_mark if r == 0 and ch == 0
                                  else dve_mark[j])
                nc.vector.wait_ge(c.sPOOL.h, pool_mark[j])
                c.sDVE.inc(nc.vector.tensor_scalar(
                    tb[j][:, 0:CWi], wbuf[j][:, 0:CWi],
                    c.cols[:, 2 * i: 2 * i + 1], None, ALU.mult))
                nc.vector.wait_ge(c.sDVE.h, c.sDVE.n)
                c.sDVE.inc(nc.vector.tensor_scalar(
                    cb1[j][:, 0:CWi], tb[j][:, 0:CWi], 0.5, None, ALU.is_ge))
                nc.vector.wait_ge(c.sDVE.h, c.sDVE.n)
                c.sDVE.inc(nc.vector.tensor_scalar(
                    wbuf[j][:, 0:CWi], tb[j][:, 0:CWi], -0.5, None,
                    ALU.is_le))
                dve_mark[j] = c.sDVE.n
                nc.gpsimd.wait_ge(c.sDVE.h, dve_mark[j])
                if sp_toks[j] is not None:
                    c.SQ.wait(nc.gpsimd, sp_toks[j])
                c.sPOOL.inc(nc.gpsimd.tensor_sub(qb[j][:, 0:CWi],
                                                 cb1[j][:, 0:CWi],
                                                 wbuf[j][:, 0:CWi]))
                pool_mark[j] = c.sPOOL.n
                nc.gpsimd.wait_ge(c.sPOOL.h, pool_mark[j])
                c.SQ.begin()
                c.SQ.dma(wdst[r * 128:(r + 1) * 128,
                              ch * CWi:(ch + 1) * CWi], qb[j][:, 0:CWi])
                sp_toks[j] = c.SQ.tok()
                yield
    while True:
        yield


def p_rvk_wkv(c):
    nc = c.nc
    ewc = c.place("wk_ew", [128, NB], F32, c.E0 + 704)
    nc.scalar.wait_ge(c.sDVE.h, c.sDVE.n)
    c.sACT.inc(nc.scalar.activation(ewc[:], c.k_td[:], AF.Exp, bias=0.0,
                                    scale=1.0))
    nc.scalar.wait_ge(c.sACT.h, c.sACT.n)
    c.sACT.inc(nc.scalar.activation(ewc[:], ewc[:], AF.Exp, bias=0.0,
                                    scale=-1.0))

    kf = c.place("wk_k", [128, T], F32, c.A0)
    vf = c.place("wk_v", [128, T], F32, c.A0 + 8 * KB)
    af = c.place("wk_a", [128, T], F32, c.A0 + 24 * KB)
    ewm = c.place("wk_ewm", [128, T], F32, c.A0 + 32 * KB)
    av = c.place("wk_av", [128, T], F32, c.A0 + 40 * KB)
    rf = c.place("wk_r", [128, T], F32, c.A0 + 40 * KB)
    sz1 = c.place("wk_s1", [128, T + 8], F32, c.A0 + 48 * KB)
    sz2 = c.place("wk_s2", [128, T + 8], F32, c.A0 + 48 * KB + 8224)
    w32 = c.place("wk_w32", [128, D], F32, c.A0 + 16 * KB)
    whb = c.place("wk_whb", [128, D], BF16, c.A0 + 20 * KB)
    wlb = c.place("wk_wlb", [128, D], BF16, c.A0 + 22 * KB)
    colw = c.place("wk_colw", [128, 4], F32, c.E0 + 768)

    evac_mark = [0]

    def load_split(proj, cb):
        col0 = (proj * NB + cb) * 128
        nc.sync.wait_ge(c.sACT.h, c.sACT.n)
        nc.sync.wait_ge(c.sDVE.h, c.sDVE.n)
        c.LQ.begin()
        c.LQ.dma(w32[:].rearrange("p (d w) -> p d w", w=128),
                 c.wrvkT[:, col0:col0 + 128].rearrange(
                     "(d p) w -> p d w", p=128))
        c.LQ.wait(nc.scalar)
        nc.scalar.wait_ge(c.sPE.h, c.sPE.n)
        c.sACT.inc(nc.scalar.activation(whb[:], w32[:], AF.Copy,
                                        bias=0.0, scale=1.0))
        nc.vector.wait_ge(c.sACT.h, c.sACT.n)
        c.sDVE.inc(nc.vector.scalar_tensor_tensor(
            wlb[:], w32[:], 1.0, whb[:], ALU.mult, ALU.subtract))
        return c.sDVE.n

    def proj_mms(wsplit_mark):
        nc.tensor.wait_ge(c.sDVE.h, wsplit_mark)
        nc.tensor.wait_ge(c.sDVE.h, evac_mark[0])
        for tc in range(4):
            ps = c.PS[tc]
            for d in range(NB):
                lh_h = whb[:, d * 128:(d + 1) * 128]
                lh_l = wlb[:, d * 128:(d + 1) * 128]
                ah = c.AH[:, d * T + tc * 512: d * T + tc * 512 + 512]
                al = c.AL[:, d * T + tc * 512: d * T + tc * 512 + 512]
                nc.tensor.matmul(ps[:], lh_h, ah, start=(d == 0),
                                 stop=False)
                nc.tensor.matmul(ps[:], lh_h, al, start=False, stop=False)
                ins = nc.tensor.matmul(ps[:], lh_l, ah, start=False,
                                       stop=(d == NB - 1))
            c.sPE.inc(ins)
        return c.sPE.n

    def proj_evac(dst):
        nc.vector.wait_ge(c.sPE.h, c.sPE.n)
        nc.vector.wait_ge(c.sPOOL.h, c.sPOOL.n)
        nc.vector.wait_ge(c.sACT.h, c.sACT.n)
        for tc in range(4):
            c.sDVE.inc(nc.vector.tensor_copy(
                dst[:, tc * 512:(tc + 1) * 512], c.PS[tc][:]))
        evac_mark[0] = c.sDVE.n

    for cb in range(NB):
        for proj, dst in ((2, kf), (1, vf)):
            proj_mms(load_split(proj, cb))
            proj_evac(dst)
        r_mm = proj_mms(load_split(0, cb))
        ew_col = ewc[:, cb:cb + 1]
        nc.vector.wait_ge(c.sDVE.h, c.sDVE.n)
        c.sDVE.inc(nc.vector.tensor_reduce(colw[:, 0:1], kf[:], AX.X,
                                           ALU.max))
        nc.vector.wait_ge(c.sDVE.h, c.sDVE.n)
        c.sDVE.inc(nc.vector.tensor_scalar(colw[:, 1:2], colw[:, 0:1], -1.0,
                                           None, ALU.mult))
        nc.scalar.wait_ge(c.sDVE.h, c.sDVE.n)
        c.sACT.inc(nc.scalar.activation(af[:], kf[:], AF.Exp,
                                        bias=colw[:, 1:2], scale=1.0))
        c.sACT.inc(nc.scalar.activation(ewm[:], kf[:], AF.Identity,
                                        bias=ew_col, scale=0.0))
        nc.gpsimd.wait_ge(c.sACT.h, c.sACT.n)
        c.sPOOL.inc(nc.gpsimd.tensor_mul(av[:], af[:], vf[:]))
        nc.vector.wait_ge(c.sPOOL.h, c.sPOOL.n)
        c.sDVE.inc(nc.vector.memset(sz1[:, 0:1], 0.0))
        c.sDVE.inc(nc.vector.memset(sz2[:, 0:1], 0.0))
        nc.vector.wait_ge(c.sDVE.h, c.sDVE.n)
        c.sDVE.inc(nc.vector.tensor_tensor_scan(
            sz1[:, 1:T + 1], ewm[:], av[:], 0.0, ALU.mult, ALU.add))
        nc.vector.wait_ge(c.sDVE.h, c.sDVE.n)
        c.sDVE.inc(nc.vector.tensor_tensor_scan(
            sz2[:, 1:T + 1], ewm[:], af[:], 0.0, ALU.mult, ALU.add))
        nc.vector.wait_ge(c.sDVE.h, c.sDVE.n)
        c.sDVE.inc(nc.vector.tensor_add(colw[:, 2:3], colw[:, 1:2],
                                        c.k_tf[:, cb:cb + 1]))
        nc.scalar.wait_ge(c.sDVE.h, c.sDVE.n)
        c.sACT.inc(nc.scalar.activation(ewm[:], kf[:], AF.Exp,
                                        bias=colw[:, 2:3], scale=1.0))
        nc.gpsimd.wait_ge(c.sACT.h, c.sACT.n)
        nc.gpsimd.wait_ge(c.sDVE.h, c.sDVE.n)
        c.sPOOL.inc(nc.gpsimd.tensor_mul(av[:], ewm[:], vf[:]))
        nc.vector.wait_ge(c.sPOOL.h, c.sPOOL.n)
        c.sDVE.inc(nc.vector.tensor_add(vf[:], sz1[:, 0:T], av[:]))
        nc.vector.wait_ge(c.sPE.h, r_mm)
        nc.vector.wait_ge(c.sDVE.h, c.sDVE.n)
        for tc in range(4):
            c.sDVE.inc(nc.vector.tensor_copy(
                rf[:, tc * 512:(tc + 1) * 512], c.PS[tc][:]))
        evac_mark[0] = c.sDVE.n
        nc.gpsimd.wait_ge(c.sDVE.h, c.sDVE.n)
        c.sPOOL.inc(nc.gpsimd.tensor_add(af[:], sz2[:, 0:T], ewm[:]))
        nc.vector.wait_ge(c.sPOOL.h, c.sPOOL.n)
        c.sDVE.inc(nc.vector.reciprocal(kf[:], af[:]))
        nc.gpsimd.wait_ge(c.sDVE.h, c.sDVE.n)
        c.sPOOL.inc(nc.gpsimd.tensor_mul(kf[:], kf[:], vf[:]))
        nc.scalar.wait_ge(c.sPOOL.h, c.sPOOL.n)
        nc.scalar.wait_ge(c.sDVE.h, c.sDVE.n)
        c.sACT.inc(nc.scalar.activation(af[:], rf[:], AF.Sigmoid, bias=0.0,
                                        scale=1.0))
        nc.vector.wait_ge(c.sACT.h, c.sACT.n)
        c.sDVE.inc(nc.vector.tensor_mul(kf[:], kf[:], af[:]))
        nc.scalar.wait_ge(c.sDVE.h, c.sDVE.n)
        c.sACT.inc(nc.scalar.activation(c.RWH[:, cb * T:(cb + 1) * T], kf[:],
                                        AF.Copy, bias=0.0, scale=1.0))
        nc.gpsimd.wait_ge(c.sACT.h, c.sACT.n)
        c.sPOOL.inc(nc.gpsimd.tensor_sub(
            c.RWL[:, cb * T:(cb + 1) * T], kf[:],
            c.RWH[:, cb * T:(cb + 1) * T]))
    bar(c)


def p_outproj(c):
    nc = c.nc
    c.e_w32 = c.place("o1_w32", [128, D], F32, c.A0)
    wh = c.place("o1_wh", [128, NB * D], BF16, c.A0 + 4 * KB)
    wl = c.place("o1_wl", [128, NB * D], BF16, c.A0 + 20 * KB)
    load_w_split(c, c.woutT, wh, wl, NB)
    for cb in range(NB):
        for tc in range(4):
            ps = c.PS[tc % 2]
            nc.tensor.wait_ge(c.sACT.h, c.sACT.n)
            for d in range(NB):
                lh_h = wh[:, d * D + cb * 128: d * D + cb * 128 + 128]
                lh_l = wl[:, d * D + cb * 128: d * D + cb * 128 + 128]
                rh = c.RWH[:, d * T + tc * 512: d * T + tc * 512 + 512]
                rl = c.RWL[:, d * T + tc * 512: d * T + tc * 512 + 512]
                nc.tensor.matmul(ps[:], lh_h, rh, start=(d == 0), stop=False)
                nc.tensor.matmul(ps[:], lh_h, rl, start=False, stop=False)
                ins = nc.tensor.matmul(ps[:], lh_l, rh, start=False,
                                       stop=(d == NB - 1))
            c.sPE.inc(ins)
            nc.scalar.wait_ge(c.sPE.h, c.sPE.n)
            c.sACT.inc(nc.scalar.activation(
                c.SF[:, cb * T + tc * 512: cb * T + tc * 512 + 512], ps[:],
                AF.Identity, bias=0.0, scale=1.0))
    nc.gpsimd.wait_ge(c.sACT.h, c.sACT.n)
    c.SQ.begin()
    c.SQ.dma(c.sp_s[:].rearrange("(b p) t -> p b t", p=128),
             c.SF[:].rearrange("p (b t) -> p b t", b=NB))
    c.SQ.wait(nc.sync)
    bar(c)


def p_ffn1_combine(c):
    nc = c.nc
    G = [c.place(f"f1_g{i}", [128, F1], F32, c.A0 + i * 16 * KB)
         for i in range(2)]
    nfc = [c.place(f"f1_nfc{i}", [128, NB * 128], BF16,
                   c.A0 + 32 * KB + i * 2 * KB) for i in range(2)]
    gt = c.place("f1_gt", [128, D], F32, c.A0 + 36 * KB)
    ostg = c.place("f1_ostg", [128, NB * 128], F32, c.A0 + 36 * KB)
    nh = [c.place(f"f1_nh{i}", [128, F1], BF16, c.A0 + 40 * KB + i * 8 * KB)
          for i in range(2)]
    nhf = c.place("f1_nhf", [128, F1], BF16, c.A0 + 56 * KB)
    xsb = [c.place("f1_xs", [128, 256], F32, c.A0 + 64 * KB)] * 2
    wgb = [c.place(f"f1_wgb{i}", [128, NB * 512], BF16,
                   c.B0 + i * 8 * KB) for i in range(2)]
    wub = [c.place(f"f1_wub{i}", [128, NB * 512], BF16,
                   c.B0 + 16 * KB + i * 8 * KB) for i in range(2)]
    uq = c.place("f1_uq", [128, F1], F32, c.B0 + 32 * KB)
    cq = c.place("f1_cq", [128, F1], F32, c.B0 + 48 * KB)
    wdq = c.place("f1_wdq", [128, NF1 * D], BF16, c.C0)
    ppc = c.place("f1_ppc", [128, 16], F32, c.E0 + 832)

    c.LQ.begin()
    c.LQ.dma(wdq[:].rearrange("p (h w) -> p h w", w=D),
             c.sp_wd1[:].rearrange("(h p) w -> p h w", p=128))
    wdq_tok = c.LQ.tok()

    w_toks = [None, None]
    w_free = [0, 0]
    nfc_free = [0, 0]
    act_bank = [0] * 4
    dve_bank = [0] * 4
    g_free = [0, 0]
    nh_marks = [0, 0]
    nh_pe_free = [0, 0]
    out_tok = None
    state = {"gt_act": 0, "out_pool": 0}

    def load_w(cc):
        j = cc % 2
        nc.sync.wait_ge(c.sPE.h, w_free[j])
        c.LQ.begin()
        c.LQ.dma(wgb[j][:].rearrange("p (d w) -> p d w", w=512),
                 c.sp_wg1[:, cc * 512:(cc + 1) * 512].rearrange(
                     "(d p) w -> p d w", p=128))
        c.LQ.dma(wub[j][:].rearrange("p (d w) -> p d w", w=512),
                 c.sp_wu1[:, cc * 512:(cc + 1) * 512].rearrange(
                     "(d p) w -> p d w", p=128))
        w_toks[j] = c.LQ.tok()

    def gates_ti(tcn, ti):
        tt = 2 * tcn + ti
        nc.vector.wait_ge(c.sACT.h, c.sACT.n)
        nc.vector.wait_ge(c.sDVE.h, c.sDVE.n)
        c.sDVE.inc(nc.vector.tensor_mul(
            ppc[:, 2 * ti:2 * ti + 1], c.sinv[:, tt:tt + 1],
            c.cols[:, 1:2]))
        c.sDVE.inc(nc.vector.tensor_mul(
            ppc[:, 2 * ti + 1:2 * ti + 2], c.sinv[:, tt:tt + 1],
            c.cols[:, 3:4]))
        pp_mark = c.sDVE.n
        nc.sync.wait_ge(c.sPE.h, nfc_free[ti])
        c.LQ.begin()
        c.LQ.dma(nfc[ti][:].rearrange("p (d w) -> p d w", w=128),
                 c.sp_n[:, tt * 128:(tt + 1) * 128].rearrange(
                     "(d p) w -> p d w", p=128))
        nfc_tok = c.LQ.tok()
        load_w(0)
        for cc in range(NB):
            if cc + 1 < NB:
                load_w(cc + 1)
            j = cc % 2
            c.LQ.wait(nc.tensor, w_toks[j])
            c.LQ.wait(nc.tensor, nfc_tok)
            psg, psu = c.PS[2 * ti], c.PS[2 * ti + 1]
            nc.tensor.wait_ge(c.sACT.h, act_bank[2 * ti])
            nc.tensor.wait_ge(c.sDVE.h, dve_bank[2 * ti + 1])
            ins = None
            for d in range(NB):
                lh = nfc[ti][:, d * 128:(d + 1) * 128]
                nc.tensor.matmul(psg[:], lh,
                                 wgb[j][:, d * 512:(d + 1) * 512],
                                 start=(d == 0), stop=(d == NB - 1))
                ins = nc.tensor.matmul(psu[:], lh,
                                       wub[j][:, d * 512:(d + 1) * 512],
                                       start=(d == 0),
                                       stop=(d == NB - 1))
            c.sPE.inc(ins)
            mm_mark = c.sPE.n
            w_free[j] = mm_mark
            gsl = slice(cc * 512, (cc + 1) * 512)
            nc.scalar.wait_ge(c.sPE.h, mm_mark)
            nc.scalar.wait_ge(c.sDVE.h, max(pp_mark, g_free[ti]))
            c.sACT.inc(nc.scalar.activation(
                G[ti][:, gsl], psg[:], AF.Silu, bias=0.0,
                scale=ppc[:, 2 * ti:2 * ti + 1]))
            act_bank[2 * ti] = c.sACT.n
            nc.vector.wait_ge(c.sACT.h, act_bank[2 * ti])
            nc.vector.wait_ge(c.sDVE.h, g_free[ti])
            c.sDVE.inc(nc.vector.scalar_tensor_tensor(
                G[ti][:, gsl], psu[:], ppc[:, 2 * ti + 1:2 * ti + 2],
                G[ti][:, gsl], ALU.mult, ALU.mult))
            dve_bank[2 * ti + 1] = c.sDVE.n
        nfc_free[ti] = c.sPE.n

    def quantA(tcn, ti):
        tt = 2 * tcn + ti
        hq = G[ti]
        q = 8 + 4 * ti
        nc.vector.wait_ge(c.sDVE.h, dve_bank[2 * ti + 1])
        nc.vector.wait_ge(c.sACT.h, act_bank[2 * ti])
        c.sDVE.inc(nc.vector.tensor_reduce(
            ppc[:, q:q + 1], hq[:], AX.X, ALU.max, apply_absolute_value=True))
        nc.vector.wait_ge(c.sDVE.h, c.sDVE.n)
        c.sDVE.inc(nc.vector.tensor_scalar(ppc[:, q:q + 1], ppc[:, q:q + 1],
                                           1e-5, None, ALU.max))
        nc.vector.wait_ge(c.sDVE.h, c.sDVE.n)
        c.sDVE.inc(nc.vector.reciprocal(ppc[:, q + 1:q + 2], ppc[:, q:q + 1]))
        nc.vector.wait_ge(c.sDVE.h, c.sDVE.n)
        c.sDVE.inc(nc.vector.tensor_scalar(
            c.shinv[:, tt:tt + 1], ppc[:, q:q + 1], 1.0 / 127.0, None,
            ALU.mult))
        c.sDVE.inc(nc.vector.tensor_scalar(ppc[:, q + 1:q + 2],
                                           ppc[:, q + 1:q + 2], 127.0, None,
                                           ALU.mult))
        nc.vector.wait_ge(c.sDVE.h, c.sDVE.n)
        c.sDVE.inc(nc.vector.tensor_scalar(hq[:], hq[:], ppc[:, q + 1:q + 2],
                                           128.5, ALU.mult, ALU.add))
        ladder_floor(c, hq, uq, cq, F1)
        nc.vector.wait_ge(c.sDVE.h, c.sDVE.n)
        nc.vector.wait_ge(c.sPOOL.h, c.sPOOL.n)
        nc.vector.wait_ge(c.sPE.h, nh_pe_free[ti])
        c.sDVE.inc(nc.vector.scalar_tensor_tensor(
            nh[ti][:], hq[:], -128.0, uq[:], ALU.add, ALU.subtract))
        nh_marks[ti] = c.sDVE.n
        g_free[ti] = c.sDVE.n

    def peB(tcn, ti):
        nonlocal out_tok
        tt = 2 * tcn + ti
        ev_mark = 0
        nc.tensor.wait_ge(c.sDVE.h, nh_marks[ti])
        for hb4 in range(0, NF1, 4):
            nc.tensor.wait_ge(c.sACT.h, ev_mark)
            for j in range(4):
                hb = hb4 + j
                ins = nc.tensor.transpose(
                    c.PSB[:, 512 + j * 128:512 + (j + 1) * 128],
                    nh[ti][:, hb * 128:(hb + 1) * 128], c.k_ident[:])
            c.sPE.inc(ins)
            nc.scalar.wait_ge(c.sPE.h, c.sPE.n)
            for j in range(4):
                hb = hb4 + j
                ins = nc.scalar.activation(
                    nhf[:, hb * 128:(hb + 1) * 128],
                    c.PSB[:, 512 + j * 128:512 + (j + 1) * 128],
                    AF.Copy, bias=0.0, scale=1.0)
            c.sACT.inc(ins)
            ev_mark = c.sACT.n
        tr_mark = c.sACT.n
        nh_pe_free[ti] = c.sPE.n
        c.LQ.wait(nc.tensor, wdq_tok)
        nc.tensor.wait_ge(c.sACT.h, tr_mark)
        for dc in range(2):
            ps = c.PS[4 + dc]
            nc.tensor.wait_ge(c.sACT.h, state["gt_act"])
            ins = None
            for hb in range(NF1):
                ins = nc.tensor.matmul(
                    ps[:], nhf[:, hb * 128:(hb + 1) * 128],
                    wdq[:, hb * D + dc * 512: hb * D + dc * 512 + 512],
                    start=(hb == 0), stop=(hb == NF1 - 1))
            c.sPE.inc(ins)
        down_mark = c.sPE.n
        nc.vector.wait_ge(c.sDVE.h, c.sDVE.n)
        c.sDVE.inc(nc.vector.tensor_mul(
            ppc[:, 6:7], c.shinv[:, tt:tt + 1], c.cols[:, 5:6]))
        ppd_mark = c.sDVE.n
        nc.scalar.wait_ge(c.sPE.h, down_mark)
        nc.scalar.wait_ge(c.sDVE.h, ppd_mark)
        if out_tok is not None:
            c.SQ.wait(nc.scalar, out_tok)
        nc.scalar.wait_ge(c.sPOOL.h, state["out_pool"])
        for dc in range(2):
            c.sACT.inc(nc.scalar.activation(
                gt[:, dc * 512:(dc + 1) * 512], c.PS[4 + dc][:], AF.Silu,
                bias=0.0, scale=ppc[:, 6:7]))
        g_mark = c.sACT.n
        state["gt_act"] = g_mark
        for grp in range(2):
            nc.tensor.wait_ge(c.sACT.h, g_mark)
            nc.tensor.wait_ge(c.sDVE.h, c.sDVE.n)
            for q4 in range(4):
                cb = grp * 4 + q4
                ins = nc.tensor.transpose(
                    c.PS[6][:, q4 * 128:(q4 + 1) * 128],
                    gt[:, cb * 128:(cb + 1) * 128], c.k_ident_f32[:])
            c.sPE.inc(ins)
            tr4 = c.sPE.n
            for q4 in range(4):
                cb = grp * 4 + q4
                par = cb % 2
                nc.sync.wait_ge(c.sDVE.h, c.sDVE.n)
                nc.sync.wait_ge(c.sPOOL.h, c.sPOOL.n)
                c.LQ.begin()
                c.LQ.dma(xsb[par][:, 0:128],
                         c.sp_x[cb * 128:(cb + 1) * 128,
                                tt * 128:(tt + 1) * 128])
                c.LQ.dma(xsb[par][:, 128:256],
                         c.sp_s[cb * 128:(cb + 1) * 128,
                                tt * 128:(tt + 1) * 128])
                xs_tok = c.LQ.tok()
                c.LQ.wait(nc.vector, xs_tok)
                nc.vector.wait_ge(c.sPE.h, tr4)
                nc.vector.wait_ge(c.sPOOL.h, c.sPOOL.n)
                if out_tok is not None:
                    c.SQ.wait(nc.vector, out_tok)
                c.sDVE.inc(nc.vector.tensor_mul(
                    ostg[:, cb * 128:(cb + 1) * 128],
                    xsb[par][:, 128:256], c.PS[6][:, q4 * 128:(q4 + 1) * 128]))
                nc.gpsimd.wait_ge(c.sDVE.h, c.sDVE.n)
                if out_tok is not None:
                    c.SQ.wait(nc.gpsimd, out_tok)
                c.sPOOL.inc(nc.gpsimd.tensor_add(
                    ostg[:, cb * 128:(cb + 1) * 128],
                    ostg[:, cb * 128:(cb + 1) * 128],
                    xsb[par][:, 0:128]))
        state["out_pool"] = c.sPOOL.n
        nc.gpsimd.wait_ge(c.sPOOL.h, c.sPOOL.n)
        c.SQ.begin()
        c.SQ.dma(c.out[:, tt * 128:(tt + 1) * 128].rearrange(
            "(b p) w -> p b w", p=128),
            ostg[:].rearrange("p (b w) -> p b w", w=128))
        out_tok = c.SQ.tok()

    pending = None
    for tcn in range(NT // 2):
        gates_ti(tcn, 0)
        quantA(tcn, 0)
        gates_ti(tcn, 1)
        if pending is not None:
            peB(*pending)
        quantA(tcn, 1)
        peB(tcn, 0)
        pending = (tcn, 1)
    peB(*pending)
    c.SQ.wait_all(nc.sync)
    bar(c)




def kernel(L0_attn_input, L0_attn_Wq, L0_attn_Wk, L0_attn_Wv, L0_attn_Wo,
           L0_attn_bo, L0_ffn_w_gate, L0_ffn_w_up, L0_ffn_w_down,
           L1_attn_W_rvk, L1_attn_W_out, L1_attn_time_decay,
           L1_attn_time_first, L1_ffn_w_gate, L1_ffn_w_up, L1_ffn_w_down):
    import ml_dtypes

    def bf(x):
        return np.ascontiguousarray(x).astype(ml_dtypes.bfloat16)

    consts = _host_consts()
    common = dict(
        wqT=np.ascontiguousarray(np.asarray(L0_attn_Wq).T),
        wkT=np.ascontiguousarray(np.asarray(L0_attn_Wk).T),
        wvT=np.ascontiguousarray(np.asarray(L0_attn_Wv).T),
        woT=np.ascontiguousarray(np.asarray(L0_attn_Wo).T),
        bo=np.ascontiguousarray(np.asarray(L0_attn_bo).reshape(NB, 128).T),
        wg0T=np.ascontiguousarray(np.asarray(L0_ffn_w_gate).T),
        wu0T=np.ascontiguousarray(np.asarray(L0_ffn_w_up).T),
        wd0T=np.ascontiguousarray(np.asarray(L0_ffn_w_down).T),
        wrvkT=np.ascontiguousarray(np.asarray(L1_attn_W_rvk).T),
        woutT=np.ascontiguousarray(np.asarray(L1_attn_W_out).T),
        td=np.ascontiguousarray(np.asarray(L1_attn_time_decay).reshape(NB, 128).T),
        tf=np.ascontiguousarray(np.asarray(L1_attn_time_first).reshape(NB, 128).T),
        wg1T=np.ascontiguousarray(np.asarray(L1_ffn_w_gate).T),
        wu1T=np.ascontiguousarray(np.asarray(L1_ffn_w_up).T),
        wd1T=np.ascontiguousarray(np.asarray(L1_ffn_w_down).T),
        sin_fm=consts["sin_fm"], cos_fm=consts["cos_fm"],
        sin_lo=consts["sin_lo"], cos_lo=consts["cos_lo"],
        rotPT=bf(consts["rotPT"]), ident=bf(consts["ident"]),
        identf=consts["ident"],
        ones1=bf(consts["ones1"]), allones=bf(consts["allones"]),
        onescol=bf(consts["onescol"]), masks=bf(consts["masks"]),
    )
    x = np.asarray(L0_attn_input)
    nc, _ = build_nc(tuple(_DEBUG))
    in_maps = []
    for b in range(B):
        m = dict(common)
        m["xT"] = np.ascontiguousarray(x[b].T)
        in_maps.append(m)
    res = run_bass_kernel_spmd(nc, in_maps, core_ids=list(range(B)))
    out = np.stack([np.asarray(res.results[b]["out"]).T for b in range(B)], 0)
    return out


# version 19
# speedup vs baseline: 1.0412x; 1.0412x over previous
import numpy as np
import concourse.bass as bass
import concourse.mybir as mybir
from concourse.bass_utils import run_bass_kernel_spmd

F32 = mybir.dt.float32
BF16 = mybir.dt.bfloat16
I32 = mybir.dt.int32
U8 = mybir.dt.uint8
AF = mybir.ActivationFunctionType
ALU = mybir.AluOpType
AX = mybir.AxisListType

B, T, D = 8, 2048, 1024
NB = D // 128
NT = T // 128
NH, HD = 16, 64
F0 = 3072
NF0 = F0 // 128
F1 = 4096
NF1 = F1 // 128
QC = 512
NQC = T // QC
EPS = 1e-5
NEGBIG = -1.0e30
KB = 1024

_DEBUG = []
_PHASES = 99


class Sem:
    def __init__(self, nc, name):
        self.h = nc.alloc_semaphore(name=name)
        self.n = 0

    def inc(self, ins, k=1):
        ins.then_inc(self.h, k)
        self.n += k
        return ins


class DmaQ:

    def __init__(self, nc, name, eng, n=4):
        self.nc, self.eng = nc, eng
        self.sems = [Sem(nc, f"{name}{i}") for i in range(n)]
        self.i = -1
        self.cur = None

    def begin(self):
        self.i = (self.i + 1) % len(self.sems)
        s = self.sems[self.i]
        self.eng.wait_ge(s.h, s.n)
        self.cur = s
        return s

    def dma(self, out, in_):
        self.cur.inc(self.eng.dma_start(out=out, in_=in_), 16)

    def tok(self):
        return (self.cur.h, self.cur.n)

    def wait(self, eng, tok=None):
        h, n = tok if tok is not None else (self.cur.h, self.cur.n)
        eng.wait_ge(h, n)

    def wait_all(self, eng):
        for s in self.sems:
            eng.wait_ge(s.h, s.n)


class Ctx:
    pass




def _host_consts():
    import jax
    cpu = jax.devices("cpu")[0]
    with jax.default_device(cpu):
        import jax.numpy as jnp
        thetas = 10000.0 ** (-jnp.arange(0, D, 2, dtype=jnp.float32) / D)
        thetas = jnp.repeat(thetas, 2)
        angles = (jnp.arange(T, dtype=jnp.float32)[:, None] * thetas[None, :])
        s32 = np.asarray(jnp.sin(angles))
        c32 = np.asarray(jnp.cos(angles))
    sin_fm = np.ascontiguousarray(s32.T)
    cos_fm = np.ascontiguousarray(c32.T)
    sin_lo = np.zeros_like(sin_fm)
    cos_lo = np.zeros_like(cos_fm)
    P = np.zeros((128, 128), np.float32)
    for i in range(64):
        P[2 * i, 2 * i + 1] = -1.0
        P[2 * i + 1, 2 * i] = 1.0
    rotPT = np.ascontiguousarray(P.T)
    ident = np.eye(128, dtype=np.float32)
    ones1 = np.ones((1, 128), np.float32)
    allones = np.ones((128, 128), np.float32)
    onescol = np.ones((128, 1), np.float32)
    masks = np.zeros((128, 4 * QC), np.float32)
    pk = np.arange(128)[:, None]
    fq = np.arange(QC)[None, :]
    for j, delta in enumerate((0, 128, 256, 384)):
        masks[:, j * QC:(j + 1) * QC] = np.where(pk + delta > fq, NEGBIG, 0.0)
    return dict(sin_fm=sin_fm, cos_fm=cos_fm, sin_lo=sin_lo, cos_lo=cos_lo,
                rotPT=rotPT, ident=ident,
                ones1=ones1, allones=allones, onescol=onescol, masks=masks)




def build_nc(debug=()):
    nc = bass.Bass("TRN2", target_bir_lowering=False, debug=False)
    c = Ctx()
    c.nc = nc
    c.debug = set(debug)
    c.uid = 0

    def din(name, shape, dt=F32):
        return nc.dram_tensor(name, list(shape), dt, kind="ExternalInput").ap()

    c.xT = din("xT", [D, T])
    c.wqT = din("wqT", [D, D]); c.wkT = din("wkT", [D, D])
    c.wvT = din("wvT", [D, D]); c.woT = din("woT", [D, D])
    c.bo = din("bo", [128, NB])
    c.wg0T = din("wg0T", [D, F0], I32)
    c.wu0T = din("wu0T", [D, F0], I32)
    c.wd0T = din("wd0T", [F0, D], I32)
    c.wrvkT = din("wrvkT", [D, 3 * D])
    c.woutT = din("woutT", [D, D])
    c.td = din("td", [128, NB]); c.tf = din("tf", [128, NB])
    c.wg1T = din("wg1T", [D, F1])
    c.wu1T = din("wu1T", [D, F1])
    c.wd1T = din("wd1T", [F1, D])
    c.sin = din("sin_fm", [D, T]); c.cos = din("cos_fm", [D, T])
    c.sinl = din("sin_lo", [D, T]); c.cosl = din("cos_lo", [D, T])
    c.rotPT_h = din("rotPT", [128, 128], BF16)
    c.ident_h = din("ident", [128, 128], BF16)
    c.identf_h = din("identf", [128, 128], F32)
    c.ones1_h = din("ones1", [1, 128], BF16)
    c.allones_h = din("allones", [128, 128], BF16)
    c.onescol_h = din("onescol", [128, 1], BF16)
    c.masks_h = din("masks", [128, 4 * QC], BF16)

    c.out = nc.dram_tensor("out", [D, T], F32, kind="ExternalOutput").ap()

    def dsc(name, shape, dt):
        return nc.dram_tensor(name, list(shape), dt, kind="Internal").ap()

    c.sp_x = dsc("sp_x", [D, T], F32)
    c.sp_g = dsc("sp_g", [D, T], F32)
    c.sp_s = dsc("sp_s", [D, T], F32)
    c.sp_vh = dsc("sp_vh", [128, NT * 16 * 65], BF16)
    c.sp_vl = dsc("sp_vl", [128, NT * 16 * 65], BF16)
    c.sp_qh = dsc("sp_qh", [D, T], BF16)
    c.sp_ql = dsc("sp_ql", [D, T], BF16)
    c.sp_oh = dsc("sp_oh", [D, T], BF16)
    c.sp_ol = dsc("sp_ol", [D, T], BF16)
    c.sp_hh = dsc("sp_hh", [F0, T], BF16)
    c.sp_hl = dsc("sp_hl", [F0, T], BF16)
    c.sp_wg0 = dsc("sp_wg0", [D, F0], BF16)
    c.sp_wu0 = dsc("sp_wu0", [D, F0], BF16)
    c.sp_wd0 = dsc("sp_wd0", [F0, D], BF16)
    c.sp_n = dsc("sp_n", [D, T], BF16)
    c.sp_wg1 = dsc("sp_wg1", [D, F1], BF16)
    c.sp_wu1 = dsc("sp_wu1", [D, F1], BF16)
    c.sp_wd1 = dsc("sp_wd1", [F1, D], BF16)

    c.A0, A_SZ = 0, 66560
    c.B0, B_SZ = c.A0 + 66560, 64 * KB
    c.C0, C_SZ = c.B0 + 64 * KB, 64 * KB
    c.D0, D_SZ = c.C0 + 64 * KB, 6400
    c.E0, E_SZ = c.D0 + 6400, 2 * KB
    ARENA = c.E0 + E_SZ
    arena = nc.alloc_sbuf_tensor("ARENA", [128, ARENA], U8)
    c.base = nc.lookup_mloc(arena).addr

    def place(name, shape, dtype, off):
        nbytes = int(np.prod(shape[1:])) * mybir.dt.size(dtype)
        assert off + nbytes <= ARENA, (name, off, nbytes)
        c.uid += 1
        nm = f"{name}_{c.uid}"
        h = nc.virtual_sbuf_tensor(nm, list(shape), dtype)
        nc._tensor(h.name, list(shape), dtype, type="SB")
        mloc = nc.lookup_mloc(h)
        mloc.allocated = True
        mloc.addr = c.base + off
        return h

    c.place = place

    c.XF = place("XF", [128, NB * T], F32, c.A0)
    c.VH = place("VH", [128, NT * 16 * 65], BF16, c.A0)
    c.VL = place("VL", [128, NT * 16 * 65], BF16, c.A0 + 33280)
    c.AH = place("AH", [128, NB * T], BF16, c.B0)
    c.AL = place("AL", [128, NB * T], BF16, c.B0 + 32 * KB)
    c.SF = place("SF", [128, NB * T], F32, c.B0)
    c.KH = place("KH", [128, NB * T], BF16, c.C0)
    c.KL = place("KL", [128, NB * T], BF16, c.C0 + 32 * KB)
    c.RWH, c.RWL = c.KH, c.KL

    off = c.D0
    c.k_masks = place("k_masks", [128, 4 * QC], BF16, off); off += 4 * QC * 2
    c.k_rotPT = place("k_rotPT", [128, 128], BF16, off); off += 256
    c.k_ident = place("k_ident", [128, 128], BF16, off); off += 256
    c.k_ident_f32 = place("k_ident_f32", [128, 128], F32, off); off += 512
    c.k_ones1 = place("k_ones1", [1, 128], BF16, off); off += 256
    c.k_allones = place("k_allones", [128, 128], BF16, off); off += 256
    c.k_onescol = place("k_onescol", [128, 1], BF16, off); off += 32
    c.k_bo = place("k_bo", [128, NB], F32, off); off += NB * 4
    c.k_td = place("k_td", [128, NB], F32, off); off += NB * 4
    c.k_tf = place("k_tf", [128, NB], F32, off); off += NB * 4
    assert off <= c.D0 + D_SZ

    off = c.E0
    c.cols = place("cols", [128, 16], F32, off); off += 16 * 4
    c.sinv = place("sinv", [128, NT], F32, off); off += NT * 4
    c.shinv = place("shinv", [128, NT], F32, off); off += NT * 4
    c.e_t1 = place("e_t1", [128, 128], F32, off); off += 128 * 4
    assert off <= c.E0 + E_SZ, off

    c.PS = [nc.alloc_psum_tensor(f"ps{i}", [128, 512], F32) for i in range(7)]
    c.PSB = nc.alloc_psum_tensor("psb", [128, 1024], BF16)

    c.LQ = DmaQ(nc, "ld", nc.sync, 4)
    c.SQ = DmaQ(nc, "sp", nc.gpsimd, 4)
    c.sPE = Sem(nc, "sPE")
    c.sACT = Sem(nc, "sACT")
    c.sDVE = Sem(nc, "sDVE")
    c.sPOOL = Sem(nc, "sPOOL")

    emit_all(c)

    c.SQ.wait_all(nc.sync)
    c.LQ.wait_all(nc.sync)
    nc.all_engine_barrier()
    return nc, c


def bar(c):
    c.nc.all_engine_barrier()


def tap(c, name, src_ap, shape, dtype):
    if name not in c.debug:
        return
    t = c.nc.dram_tensor("tap_" + name, list(shape), dtype,
                         kind="ExternalOutput").ap()
    bar(c)
    c.SQ.begin()
    c.SQ.dma(t[:], src_ap)
    c.SQ.wait(c.nc.gpsimd)
    bar(c)




def emit_all(c):
    steps = [
        lambda: p0_load_consts(c),
        lambda: p_ln(c, 0),
        lambda: (tap(c, "xn0h", c.AH[:], [128, NB * T], BF16),
                 tap(c, "xn0l", c.AL[:], [128, NB * T], BF16)),
        lambda: p_vproj(c),
        lambda: p_ffn0(c),
        lambda: (p_rope(c), tap(c, "rxh", c.AH[:], [128, NB * T], BF16),
                 tap(c, "rxl", c.AL[:], [128, NB * T], BF16)),
        lambda: (p_qk(c), tap(c, "kh", c.KH[:], [128, NB * T], BF16)),
        lambda: p_attn(c),
        lambda: (p_oproj_combine(c),
                 tap(c, "x1", c.XF[:], [128, NB * T], F32)),
        lambda: p_ln(c, 1),
        lambda: p_l1_quant(c),
        lambda: (p_rvk_wkv(c), tap(c, "rwh", c.RWH[:], [128, NB * T], BF16)),
        lambda: p_outproj(c),
        lambda: p_ffn1_combine(c),
    ]
    for i, s in enumerate(steps):
        if i >= _PHASES:
            break
        s()


def p0_load_consts(c):
    nc = c.nc
    c.LQ.begin()
    for dst, src in ((c.k_masks, c.masks_h), (c.k_rotPT, c.rotPT_h),
                     (c.k_ident, c.ident_h), (c.k_ident_f32, c.identf_h),
                     (c.k_ones1, c.ones1_h),
                     (c.k_allones, c.allones_h), (c.k_onescol, c.onescol_h),
                     (c.k_bo, c.bo), (c.k_td, c.td), (c.k_tf, c.tf)):
        c.LQ.dma(dst[:], src[:])
    c.LQ.dma(c.XF[:].rearrange("p (b t) -> p b t", b=NB),
             c.xT[:].rearrange("(b p) t -> p b t", p=128))
    c.LQ.wait(nc.sync)
    bar(c)


def p_ln(c, layer):
    nc = c.nc
    c.SQ.begin()
    c.SQ.dma(c.sp_x[:].rearrange("(b p) t -> p b t", p=128),
             c.XF[:].rearrange("p (b t) -> p b t", b=NB))

    xh = [c.place(f"ln_xh{i}", [128, T], BF16, c.B0 + i * 4 * KB)
          for i in range(2)]
    xl = [c.place(f"ln_xl{i}", [128, T], BF16, c.B0 + 8 * KB + i * 4 * KB)
          for i in range(2)]
    qh = [c.place(f"ln_qh{i}", [128, T], BF16, c.B0 + 16 * KB + i * 4 * KB)
          for i in range(2)]
    ql = [c.place(f"ln_ql{i}", [128, T], BF16, c.B0 + 24 * KB + i * 4 * KB)
          for i in range(2)]
    sq = [c.place(f"ln_sq{i}", [128, T], F32, c.C0 + i * 8 * KB)
          for i in range(2)]
    xn = [c.place(f"ln_xn{i}", [128, 512], F32, c.C0 + 16 * KB + i * 2 * KB)
          for i in range(2)]
    r_mean = c.place("ln_rmean", [1, T], F32, c.C0 + 20 * KB)
    r_var = c.place("ln_rvar", [1, T], F32, c.C0 + 28 * KB)
    r_rstd = c.place("ln_rrstd", [1, T], F32, c.C0 + 36 * KB)
    r_tmp = c.place("ln_rtmp", [1, T], F32, c.C0)
    rb_mh = c.place("ln_rbmh", [1, T], BF16, c.C0 + 44 * KB)
    rb_ml = c.place("ln_rbml", [1, T], BF16, c.C0 + 48 * KB)
    rb_rh = c.place("ln_rbrh", [1, T], BF16, c.C0 + 52 * KB)
    rb_rl = c.place("ln_rbrl", [1, T], BF16, c.C0 + 56 * KB)

    PSmb, PSrb = c.PS[4], c.PS[5]

    pe_m = [0, 0]
    act_m = [0, 0]
    dve_m = [0, 0]
    pool_m = [0, 0]
    for rnd in range(2):
        tcs = (2 * rnd, 2 * rnd + 1)
        for cb in range(NB):
            j = cb % 2
            xb = c.XF[:, cb * T:(cb + 1) * T]
            nc.scalar.wait_ge(c.sPE.h, pe_m[j])
            nc.scalar.wait_ge(c.sDVE.h, dve_m[j])
            nc.scalar.wait_ge(c.sPOOL.h, pool_m[j])
            c.sACT.inc(nc.scalar.activation(xh[j][:], xb, AF.Copy, bias=0.0,
                                            scale=1.0))
            c.sACT.inc(nc.scalar.activation(sq[j][:], xb, AF.Square,
                                            bias=0.0, scale=1.0))
            act_m[j] = c.sACT.n
            nc.vector.wait_ge(c.sACT.h, act_m[j])
            nc.vector.wait_ge(c.sPE.h, pe_m[j])
            c.sDVE.inc(nc.vector.scalar_tensor_tensor(
                xl[j][:], xb, 1.0, xh[j][:], ALU.mult, ALU.subtract))
            dve_m[j] = c.sDVE.n
            nc.gpsimd.wait_ge(c.sACT.h, act_m[j])
            nc.gpsimd.wait_ge(c.sPE.h, pe_m[j])
            c.sPOOL.inc(nc.gpsimd.tensor_copy(qh[j][:], sq[j][:]))
            pool_m[j] = c.sPOOL.n
            nc.vector.wait_ge(c.sPOOL.h, pool_m[j])
            c.sDVE.inc(nc.vector.tensor_sub(ql[j][:], sq[j][:], qh[j][:]))
            dve_m[j] = c.sDVE.n
            nc.tensor.wait_ge(c.sDVE.h, dve_m[j])
            nc.tensor.wait_ge(c.sPOOL.h, pool_m[j])
            ins = None
            for tcj, tc in enumerate(tcs):
                sl = slice(tc * 512, tc * 512 + 512)
                st = (cb == 0)
                sp = (cb == NB - 1)
                nc.tensor.matmul(c.PS[tcj][0:1, :], c.k_onescol[:],
                                 xh[j][:, sl], start=st, stop=False)
                nc.tensor.matmul(c.PS[tcj][0:1, :], c.k_onescol[:],
                                 xl[j][:, sl], start=False, stop=sp)
                nc.tensor.matmul(c.PS[2 + tcj][0:1, :], c.k_onescol[:],
                                 qh[j][:, sl], start=st, stop=False)
                ins = nc.tensor.matmul(c.PS[2 + tcj][0:1, :], c.k_onescol[:],
                                       ql[j][:, sl], start=False, stop=sp)
            c.sPE.inc(ins)
            pe_m[j] = c.sPE.n
        nc.vector.wait_ge(c.sPE.h, c.sPE.n)
        for tcj, tc in enumerate(tcs):
            sl = slice(tc * 512, tc * 512 + 512)
            c.sDVE.inc(nc.vector.tensor_scalar(
                r_mean[:, sl], c.PS[tcj][0:1, :], 1.0 / D, None, ALU.mult))
            c.sDVE.inc(nc.vector.tensor_scalar(
                r_var[:, sl], c.PS[2 + tcj][0:1, :], 1.0 / D, None,
                ALU.mult))
        nc.tensor.wait_ge(c.sDVE.h, c.sDVE.n)

    nc.scalar.wait_ge(c.sDVE.h, c.sDVE.n)
    nc.scalar.wait_ge(c.sPOOL.h, c.sPOOL.n)
    c.sACT.inc(nc.scalar.activation(r_tmp[:], r_mean[:], AF.Square,
                                    bias=0.0, scale=1.0))
    nc.vector.wait_ge(c.sACT.h, c.sACT.n)
    c.sDVE.inc(nc.vector.tensor_sub(r_var[:], r_var[:], r_tmp[:]))
    nc.vector.wait_ge(c.sDVE.h, c.sDVE.n)
    c.sDVE.inc(nc.vector.tensor_scalar(r_var[:], r_var[:], EPS,
                                       None, ALU.add))
    nc.vector.wait_ge(c.sDVE.h, c.sDVE.n)
    c.sDVE.inc(nc.vector.reciprocal(r_tmp[:], r_var[:]))
    nc.scalar.wait_ge(c.sDVE.h, c.sDVE.n)
    c.sACT.inc(nc.scalar.activation(r_rstd[:], r_tmp[:], AF.Sqrt,
                                    bias=0.0, scale=1.0))
    c.sACT.inc(nc.scalar.activation(rb_mh[:], r_mean[:], AF.Copy,
                                    bias=0.0, scale=1.0))
    nc.scalar.wait_ge(c.sACT.h, c.sACT.n)
    c.sACT.inc(nc.scalar.activation(rb_rh[:], r_rstd[:], AF.Copy,
                                    bias=0.0, scale=1.0))
    nc.vector.wait_ge(c.sACT.h, c.sACT.n)
    c.sDVE.inc(nc.vector.scalar_tensor_tensor(
        rb_ml[:], r_mean[:], 1.0, rb_mh[:], ALU.mult, ALU.subtract))
    c.sDVE.inc(nc.vector.scalar_tensor_tensor(
        rb_rl[:], r_rstd[:], 1.0, rb_rh[:], ALU.mult, ALU.subtract))

    for tc in range(4):
        sl = slice(tc * 512, tc * 512 + 512)
        nc.tensor.wait_ge(c.sDVE.h, c.sDVE.n)
        nc.tensor.wait_ge(c.sPOOL.h, c.sPOOL.n)
        nc.tensor.matmul(PSmb[:], c.k_ones1[:], rb_mh[:, sl],
                         start=True, stop=False)
        c.sPE.inc(nc.tensor.matmul(PSmb[:], c.k_ones1[:], rb_ml[:, sl],
                                   start=False, stop=True))
        nc.tensor.matmul(PSrb[:], c.k_ones1[:], rb_rh[:, sl],
                         start=True, stop=False)
        c.sPE.inc(nc.tensor.matmul(PSrb[:], c.k_ones1[:], rb_rl[:, sl],
                                   start=False, stop=True))
        n_act = [0, 0]
        n_pool = [0, 0]
        pe_bc = c.sPE.n
        for cb in range(NB):
            j = cb % 2
            xb = c.XF[:, cb * T + tc * 512: cb * T + tc * 512 + 512]
            ab = slice(cb * T + tc * 512, cb * T + tc * 512 + 512)
            nc.vector.wait_ge(c.sPE.h, pe_bc)
            nc.vector.wait_ge(c.sACT.h, n_act[j])
            nc.vector.wait_ge(c.sPOOL.h, n_pool[j])
            c.sDVE.inc(nc.vector.tensor_sub(xn[j][:], xb, PSmb[:]))
            nc.vector.wait_ge(c.sDVE.h, c.sDVE.n)
            c.sDVE.inc(nc.vector.tensor_mul(xn[j][:], xn[j][:], PSrb[:]))
            xn_mark = c.sDVE.n
            nc.scalar.wait_ge(c.sDVE.h, xn_mark)
            c.sACT.inc(nc.scalar.activation(c.AH[:, ab], xn[j][:], AF.Copy,
                                            bias=0.0, scale=1.0))
            n_act[j] = c.sACT.n
            nc.gpsimd.wait_ge(c.sACT.h, n_act[j])
            c.sPOOL.inc(nc.gpsimd.tensor_sub(c.AL[:, ab], xn[j][:],
                                             c.AH[:, ab]))
            n_pool[j] = c.sPOOL.n
    bar(c)


def wcast0_gen(c):
    nc = c.nc
    ti = [c.place(f"wc_i{i}", [128, F0], I32, c.C0 + 36 * KB + i * 12 * KB)
          for i in range(1)]
    tb = [c.place(f"wc_b{i}", [128, F0], BF16, c.C0 + 48 * KB + i * 6 * KB)
          for i in range(2)]
    dve_mark = [0, 0]
    sp_toks = [None, None]
    for (srcp, dst, rows, cols) in ((c.wg0T, c.sp_wg0, NB, F0),
                                    (c.wu0T, c.sp_wu0, NB, F0),
                                    (c.wd0T, c.sp_wd0, NF0, D)):
        for r in range(rows):
            j = r % 2
            nc.sync.wait_ge(c.sDVE.h, dve_mark[0])
            c.LQ.begin()
            c.LQ.dma(ti[0][:, 0:cols], srcp[r * 128:(r + 1) * 128, :])
            tok = c.LQ.tok()
            c.LQ.wait(nc.vector, tok)
            if sp_toks[j] is not None:
                c.SQ.wait(nc.vector, sp_toks[j])
            c.sDVE.inc(nc.vector.tensor_copy(tb[j][:, 0:cols],
                                             ti[0][:, 0:cols]))
            dve_mark[0] = c.sDVE.n
            nc.gpsimd.wait_ge(c.sDVE.h, dve_mark[0])
            c.SQ.begin()
            c.SQ.dma(dst[r * 128:(r + 1) * 128, :], tb[j][:, 0:cols])
            sp_toks[j] = c.SQ.tok()
            yield
    while True:
        yield


def load_w_split(c, src_ap, wh, wl, nd):
    nc = c.nc
    W = wh.shape[1] // nd
    for d in range(nd):
        tmp = c.e_w32
        nc.sync.wait_ge(c.sACT.h, c.sACT.n)
        nc.sync.wait_ge(c.sDVE.h, c.sDVE.n)
        c.LQ.begin()
        c.LQ.dma(tmp[:, 0:W], src_ap[d * 128:(d + 1) * 128, :])
        c.LQ.wait(nc.scalar)
        c.sACT.inc(nc.scalar.activation(wh[:, d * W:(d + 1) * W], tmp[:, 0:W],
                                        AF.Copy, bias=0.0, scale=1.0))
        nc.vector.wait_ge(c.sACT.h, c.sACT.n)
        c.sDVE.inc(nc.vector.scalar_tensor_tensor(
            wl[:, d * W:(d + 1) * W], tmp[:, 0:W], 1.0,
            wh[:, d * W:(d + 1) * W], ALU.mult, ALU.subtract))
    nc.tensor.wait_ge(c.sDVE.h, c.sDVE.n)


def p_vproj(c):
    nc = c.nc
    wc = wcast0_gen(c)
    WVH = c.place("wvh", [128, NB * D], BF16, c.C0)
    WVL = c.place("wvl", [128, NB * D], BF16, c.C0 + 16 * KB)
    c.e_w32 = c.place("vw32", [128, D], F32, c.C0 + 32 * KB)
    load_w_split(c, c.wvT, WVH, WVL, NB)

    vhg = c.VH[:].rearrange("p (g w) -> p g w", w=65)
    vlg = c.VL[:].rearrange("p (g w) -> p g w", w=65)
    c.SQ.wait_all(nc.gpsimd)
    c.sPOOL.inc(nc.gpsimd.memset(vhg[:, :, 64:65], 1.0))
    c.sPOOL.inc(nc.gpsimd.memset(vlg[:, :, 64:65], 0.0))

    for tt in range(NT):
        for _ in range(3):
            next(wc)
        for cc in range(2):
            ps = c.PS[cc]
            nc.tensor.wait_ge(c.sACT.h, c.sACT.n)
            nc.tensor.wait_ge(c.sDVE.h, c.sDVE.n)
            for d in range(NB):
                ah = c.AH[:, d * T + tt * 128: d * T + tt * 128 + 128]
                al = c.AL[:, d * T + tt * 128: d * T + tt * 128 + 128]
                wh_s = WVH[:, d * D + cc * 512: d * D + cc * 512 + 512]
                wl_s = WVL[:, d * D + cc * 512: d * D + cc * 512 + 512]
                st = (d == 0)
                nc.tensor.matmul(ps[:], ah, wh_s, start=st, stop=False)
                nc.tensor.matmul(ps[:], ah, wl_s, start=False, stop=False)
                ins = nc.tensor.matmul(ps[:], al, wh_s, start=False,
                                       stop=(d == NB - 1))
            c.sPE.inc(ins)
        for cc in range(2):
            ps = c.PS[cc]
            dst_h = c.VH[:].rearrange("p (g w) -> p g w", w=65)[
                :, tt * 16 + cc * 8: tt * 16 + cc * 8 + 8, 0:64]
            dst_l = c.VL[:].rearrange("p (g w) -> p g w", w=65)[
                :, tt * 16 + cc * 8: tt * 16 + cc * 8 + 8, 0:64]
            src = ps[:].rearrange("p (g w) -> p g w", w=64)
            nc.scalar.wait_ge(c.sPE.h, c.sPE.n)
            c.sACT.inc(nc.scalar.activation(dst_h, src, AF.Copy, bias=0.0,
                                            scale=1.0))
            nc.vector.wait_ge(c.sACT.h, c.sACT.n)
            nc.vector.wait_ge(c.sPE.h, c.sPE.n)
            c.sDVE.inc(nc.vector.scalar_tensor_tensor(
                dst_l, src, 1.0, dst_h, ALU.mult, ALU.subtract))
    for _ in range(40):
        next(wc)
    nc.gpsimd.wait_ge(c.sDVE.h, c.sDVE.n)
    nc.gpsimd.wait_ge(c.sACT.h, c.sACT.n)
    nc.gpsimd.wait_ge(c.sPOOL.h, c.sPOOL.n)
    c.SQ.begin()
    c.SQ.dma(c.sp_vh[:], c.VH[:])
    c.SQ.dma(c.sp_vl[:], c.VL[:])
    c.SQ.wait(nc.sync)
    bar(c)


def p_ffn0(c):
    nc = c.nc
    wgt = [c.place(f"f0_wg{i}", [128, NB * 128], BF16, c.C0 + i * 2 * KB)
           for i in range(2)]
    wut = [c.place(f"f0_wu{i}", [128, NB * 128], BF16,
                   c.C0 + 4 * KB + i * 2 * KB) for i in range(2)]
    sg = c.place("f0_sg", [128, T], F32, c.C0 + 8 * KB)
    hh = [c.place(f"f0_hh{i}", [128, T], BF16, c.C0 + 16 * KB + i * 4 * KB)
          for i in range(2)]
    hl = [c.place(f"f0_hl{i}", [128, T], BF16, c.C0 + 24 * KB + i * 4 * KB)
          for i in range(2)]
    hf = [c.place(f"f0_hf{i}", [128, 512], F32, c.C0 + 32 * KB + i * 2 * KB)
          for i in range(2)]
    w_toks = [None, None]
    buf_pe = [0, 0]
    act_g = [0] * 4
    dve_u = [0] * 4
    act_hf = [0, 0]
    pool_hf = [0, 0]
    sp_toks = [None, None]
    hl_pool = [0, 0]

    def load_w(cb):
        j = cb % 2
        nc.sync.wait_ge(c.sPE.h, buf_pe[j])
        c.LQ.begin()
        c.LQ.dma(wgt[j][:].rearrange("p (d w) -> p d w", w=128),
                 c.sp_wg0[:, cb * 128:(cb + 1) * 128].rearrange(
                     "(d p) w -> p d w", p=128))
        c.LQ.dma(wut[j][:].rearrange("p (d w) -> p d w", w=128),
                 c.sp_wu0[:, cb * 128:(cb + 1) * 128].rearrange(
                     "(d p) w -> p d w", p=128))
        w_toks[j] = c.LQ.tok()

    wq = wq1_gen(c)
    load_w(0)
    for cb in range(NF0):
        for _ in range(5):
            next(wq)
        j = cb % 2
        if cb + 1 < NF0:
            load_w(cb + 1)
        c.LQ.wait(nc.tensor, w_toks[j])
        for tc in range(4):
            nc.tensor.wait_ge(c.sACT.h, act_g[tc])
            if tc == 0:
                nc.tensor.wait_ge(c.sDVE.h, dve_u[3])
            ins = None
            for d in range(NB):
                lh = wgt[j][:, d * 128:(d + 1) * 128]
                ah = c.AH[:, d * T + tc * 512: d * T + tc * 512 + 512]
                al = c.AL[:, d * T + tc * 512: d * T + tc * 512 + 512]
                nc.tensor.matmul(c.PS[tc][:], lh, ah, start=(d == 0),
                                 stop=False)
                ins = nc.tensor.matmul(c.PS[tc][:], lh, al, start=False,
                                       stop=(d == NB - 1))
            c.sPE.inc(ins)
            pe_mark = c.sPE.n
            nc.scalar.wait_ge(c.sPE.h, pe_mark)
            nc.scalar.wait_ge(c.sDVE.h, dve_u[tc])
            c.sACT.inc(nc.scalar.activation(
                sg[:, tc * 512:(tc + 1) * 512], c.PS[tc][:], AF.Silu,
                bias=0.0, scale=1.0 / 127.0))
            act_g[tc] = c.sACT.n
        for tc in range(4):
            bank = 4 + tc if tc < 3 else 0
            nc.tensor.wait_ge(c.sDVE.h, dve_u[tc])
            if bank == 6:
                nc.tensor.wait_ge(c.sDVE.h, getattr(c, "ps6_user", 0))
            if tc == 3:
                nc.tensor.wait_ge(c.sACT.h, act_g[0])
            ins = None
            for d in range(NB):
                lh = wut[j][:, d * 128:(d + 1) * 128]
                ah = c.AH[:, d * T + tc * 512: d * T + tc * 512 + 512]
                al = c.AL[:, d * T + tc * 512: d * T + tc * 512 + 512]
                nc.tensor.matmul(c.PS[bank][:], lh, ah, start=(d == 0),
                                 stop=False)
                ins = nc.tensor.matmul(c.PS[bank][:], lh, al, start=False,
                                       stop=(d == NB - 1))
            c.sPE.inc(ins)
            pe_mark = c.sPE.n
            k = tc % 2
            nc.vector.wait_ge(c.sPE.h, pe_mark)
            nc.vector.wait_ge(c.sACT.h, max(act_g[tc], act_hf[k]))
            nc.vector.wait_ge(c.sPOOL.h, pool_hf[k])
            c.sDVE.inc(nc.vector.scalar_tensor_tensor(
                hf[k][:], c.PS[bank][:], 1.0 / 127.0,
                sg[:, tc * 512:(tc + 1) * 512], ALU.mult, ALU.mult))
            dve_u[tc] = c.sDVE.n
            nc.scalar.wait_ge(c.sDVE.h, dve_u[tc])
            if sp_toks[j] is not None:
                c.SQ.wait(nc.scalar, sp_toks[j])
            c.sACT.inc(nc.scalar.activation(
                hh[j][:, tc * 512:(tc + 1) * 512], hf[k][:], AF.Copy,
                bias=0.0, scale=1.0))
            act_hf[k] = c.sACT.n
            nc.gpsimd.wait_ge(c.sACT.h, act_hf[k])
            if sp_toks[j] is not None:
                c.SQ.wait(nc.gpsimd, sp_toks[j])
            c.sPOOL.inc(nc.gpsimd.tensor_sub(
                hl[j][:, tc * 512:(tc + 1) * 512], hf[k][:],
                hh[j][:, tc * 512:(tc + 1) * 512]))
            pool_hf[k] = c.sPOOL.n
            if bank == 6:
                c.ps6_guard = dve_u[tc]
        hl_pool[j] = c.sPOOL.n
        buf_pe[j] = c.sPE.n
        nc.gpsimd.wait_ge(c.sPOOL.h, hl_pool[j])
        nc.gpsimd.wait_ge(c.sACT.h, c.sACT.n)
        c.SQ.begin()
        c.SQ.dma(c.sp_hh[cb * 128:(cb + 1) * 128, :], hh[j][:])
        c.SQ.dma(c.sp_hl[cb * 128:(cb + 1) * 128, :], hl[j][:])
        sp_toks[j] = c.SQ.tok()
    c.SQ.wait_all(nc.sync)
    bar(c)

    hhc = c.place("f0_hhc", [128, NF0 * 512], BF16, c.C0)
    hlc = c.place("f0_hlc", [128, NF0 * 512], BF16, c.C0 + 24 * KB)
    wdc = [c.place(f"f0_wdc{i}", [128, NF0 * 128], BF16,
                   c.C0 + 48 * KB + i * 6 * KB) for i in range(2)]
    c.e_g512 = c.place("f0_g512", [128, 512], F32, c.C0 + 60 * KB)
    for tc in range(4):
        nc.sync.wait_ge(c.sPE.h, c.sPE.n)
        c.LQ.begin()
        c.LQ.dma(hhc[:].rearrange("p (h w) -> p h w", w=512),
                 c.sp_hh[:, tc * 512:(tc + 1) * 512].rearrange(
                     "(h p) w -> p h w", p=128))
        c.LQ.dma(hlc[:].rearrange("p (h w) -> p h w", w=512),
                 c.sp_hl[:, tc * 512:(tc + 1) * 512].rearrange(
                     "(h p) w -> p h w", p=128))
        h_tok = c.LQ.tok()
        for db in range(NB):
            buf = wdc[db % 2]
            nc.sync.wait_ge(c.sPE.h, max(0, c.sPE.n - 1))
            c.LQ.begin()
            c.LQ.dma(buf[:].rearrange("p (h w) -> p h w", w=128),
                     c.sp_wd0[:, db * 128:(db + 1) * 128].rearrange(
                         "(h p) w -> p h w", p=128))
            c.LQ.wait(nc.tensor)
            c.LQ.wait(nc.tensor, h_tok)
            nc.tensor.wait_ge(c.sACT.h, c.sACT.n)
            ps = c.PS[db % 2]
            for hb in range(NF0):
                lh = buf[:, hb * 128:(hb + 1) * 128]
                nc.tensor.matmul(ps[:], lh,
                                 hhc[:, hb * 512:(hb + 1) * 512],
                                 start=(hb == 0), stop=False)
                ins = nc.tensor.matmul(ps[:], lh,
                                       hlc[:, hb * 512:(hb + 1) * 512],
                                       start=False, stop=(hb == NF0 - 1))
            c.sPE.inc(ins)
            nc.scalar.wait_ge(c.sPE.h, c.sPE.n)
            c.SQ.wait_all(nc.scalar)
            c.sACT.inc(nc.scalar.activation(
                c.e_g512[:], ps[:], AF.Silu, bias=0.0, scale=1.0 / 127.0))
            nc.gpsimd.wait_ge(c.sACT.h, c.sACT.n)
            c.SQ.begin()
            c.SQ.dma(c.sp_g[db * 128:(db + 1) * 128, tc * 512:(tc + 1) * 512],
                     c.e_g512[:])
            for _ in range(2):
                next(wq)
    for _ in range(180):
        next(wq)
    c.SQ.wait_all(nc.sync)
    bar(c)


def p_rope(c):
    nc = c.nc
    sct = c.place("rp_sc", [128, 4 * T], F32, c.C0)
    xnr = c.place("rp_xnr", [128, T], F32, c.C0 + 32 * KB)
    ra = c.place("rp_a", [128, 512], F32, c.C0 + 40 * KB)
    rb = c.place("rp_b", [128, 512], F32, c.C0 + 42 * KB)
    rx = c.place("rp_rx", [128, 512], F32, c.C0 + 44 * KB)
    rc = c.place("rp_c", [128, 512], F32, c.C0 + 46 * KB)
    for cb in range(NB):
        nc.sync.wait_ge(c.sDVE.h, c.sDVE.n)
        nc.sync.wait_ge(c.sPOOL.h, c.sPOOL.n)
        c.LQ.begin()
        c.LQ.dma(sct[:, 0:T], c.sin[cb * 128:(cb + 1) * 128, :])
        c.LQ.dma(sct[:, T:2 * T], c.cos[cb * 128:(cb + 1) * 128, :])
        c.LQ.dma(sct[:, 2 * T:3 * T], c.sinl[cb * 128:(cb + 1) * 128, :])
        c.LQ.dma(sct[:, 3 * T:4 * T], c.cosl[cb * 128:(cb + 1) * 128, :])
        ah_b = c.AH[:, cb * T:(cb + 1) * T]
        al_b = c.AL[:, cb * T:(cb + 1) * T]
        nc.vector.wait_ge(c.sACT.h, c.sACT.n)
        c.sDVE.inc(nc.vector.tensor_add(xnr[:], ah_b, al_b))
        for tc in range(4):
            sl = slice(tc * 512, (tc + 1) * 512)
            nc.tensor.wait_ge(c.sPOOL.h, c.sPOOL.n)
            nc.tensor.wait_ge(c.sDVE.h, c.sDVE.n)
            nc.tensor.matmul(c.PS[0][:], c.k_rotPT[:], ah_b[:, sl],
                             start=True, stop=False)
            c.sPE.inc(nc.tensor.matmul(c.PS[0][:], c.k_rotPT[:], al_b[:, sl],
                                       start=False, stop=True))
            c.LQ.wait(nc.gpsimd)
            nc.gpsimd.wait_ge(c.sDVE.h, c.sDVE.n)
            c.sPOOL.inc(nc.gpsimd.tensor_mul(
                ra[:], xnr[:, sl], sct[:, T + tc * 512: T + tc * 512 + 512]))
            nc.gpsimd.wait_ge(c.sPOOL.h, c.sPOOL.n)
            c.sPOOL.inc(nc.gpsimd.tensor_mul(
                rc[:], xnr[:, sl],
                sct[:, 3 * T + tc * 512: 3 * T + tc * 512 + 512]))
            nc.gpsimd.wait_ge(c.sPOOL.h, c.sPOOL.n)
            c.sPOOL.inc(nc.gpsimd.tensor_add(ra[:], ra[:], rc[:]))
            nc.vector.wait_ge(c.sPE.h, c.sPE.n)
            c.LQ.wait(nc.vector)
            nc.vector.wait_ge(c.sDVE.h, c.sDVE.n)
            c.sDVE.inc(nc.vector.tensor_mul(rb[:], c.PS[0][:],
                                            sct[:, tc * 512: tc * 512 + 512]))
            nc.vector.wait_ge(c.sDVE.h, c.sDVE.n)
            c.sDVE.inc(nc.vector.tensor_mul(
                rx[:], c.PS[0][:],
                sct[:, 2 * T + tc * 512: 2 * T + tc * 512 + 512]))
            nc.vector.wait_ge(c.sDVE.h, c.sDVE.n)
            c.sDVE.inc(nc.vector.tensor_add(rb[:], rb[:], rx[:]))
            nc.vector.wait_ge(c.sPOOL.h, c.sPOOL.n)
            nc.vector.wait_ge(c.sDVE.h, c.sDVE.n)
            c.sDVE.inc(nc.vector.tensor_add(rx[:], ra[:], rb[:]))
            nc.scalar.wait_ge(c.sDVE.h, c.sDVE.n)
            c.sACT.inc(nc.scalar.activation(ah_b[:, sl], rx[:], AF.Copy,
                                            bias=0.0, scale=1.0))
            nc.vector.wait_ge(c.sACT.h, c.sACT.n)
            c.sDVE.inc(nc.vector.scalar_tensor_tensor(
                al_b[:, sl], rx[:], 1.0, ah_b[:, sl], ALU.mult, ALU.subtract))
    bar(c)


def p_qk(c):
    nc = c.nc
    c.e_w32 = c.place("qk_w32", [128, D], F32, c.A0)
    qh = c.place("qk_qh", [128, T], BF16, c.A0 + 4 * KB)
    ql = c.place("qk_ql", [128, T], BF16, c.A0 + 8 * KB)
    wbh = c.place("qk_wbh", [128, NB * D], BF16, c.A0 + 12 * KB)
    wbl = c.place("qk_wbl", [128, NB * D], BF16, c.A0 + 28 * KB)
    for which, wsrc in (("k", c.wkT), ("q", c.wqT)):
        load_w_split(c, wsrc, wbh, wbl, NB)
        for cb in range(NB):
            pe0 = c.sPE.n
            for tc in range(4):
                ps = c.PS[tc]
                nc.tensor.wait_ge(c.sACT.h, c.sACT.n)
                nc.tensor.wait_ge(c.sDVE.h, c.sDVE.n)
                for d in range(NB):
                    lh_h = wbh[:, d * D + cb * 128: d * D + cb * 128 + 128]
                    lh_l = wbl[:, d * D + cb * 128: d * D + cb * 128 + 128]
                    bh = c.AH[:, d * T + tc * 512: d * T + tc * 512 + 512]
                    bl = c.AL[:, d * T + tc * 512: d * T + tc * 512 + 512]
                    nc.tensor.matmul(ps[:], lh_h, bh, start=(d == 0),
                                     stop=False)
                    nc.tensor.matmul(ps[:], lh_h, bl, start=False, stop=False)
                    ins = nc.tensor.matmul(ps[:], lh_l, bh, start=False,
                                           stop=(d == NB - 1))
                c.sPE.inc(ins)
            nc.scalar.wait_ge(c.sPE.h, c.sPE.n)
            for tc in range(4):
                sl = slice(tc * 512, (tc + 1) * 512)
                if which == "k":
                    dsth = c.KH[:, cb * T + tc * 512: cb * T + tc * 512 + 512]
                    dstl = c.KL[:, cb * T + tc * 512: cb * T + tc * 512 + 512]
                else:
                    dsth, dstl = qh[:, sl], ql[:, sl]
                c.sACT.inc(nc.scalar.activation(dsth, c.PS[tc][:], AF.Copy,
                                                bias=0.0, scale=1.0))
                nc.vector.wait_ge(c.sACT.h, c.sACT.n)
                c.sDVE.inc(nc.vector.scalar_tensor_tensor(
                    dstl, c.PS[tc][:], 1.0, dsth, ALU.mult, ALU.subtract))
            if which == "q":
                nc.gpsimd.wait_ge(c.sDVE.h, c.sDVE.n)
                nc.gpsimd.wait_ge(c.sACT.h, c.sACT.n)
                c.SQ.begin()
                c.SQ.dma(c.sp_qh[cb * 128:(cb + 1) * 128, :], qh[:])
                c.SQ.dma(c.sp_ql[cb * 128:(cb + 1) * 128, :], ql[:])
                c.SQ.wait(nc.scalar)
                c.SQ.wait(nc.vector)
        bar(c)
    bar(c)


def p_attn(c):
    nc = c.nc
    c.LQ.begin()
    c.LQ.dma(c.VH[:], c.sp_vh[:])
    c.LQ.dma(c.VL[:], c.sp_vl[:])
    v_tok = c.LQ.tok()
    B0 = c.B0
    qch = c.place("at_qch", [128, NB * 512], BF16, B0)
    qcl = c.place("at_qcl", [128, NB * 512], BF16, B0 + 8 * KB)
    eth = c.place("at_eth", [128, 8 * 512], BF16, B0 + 16 * KB)
    etl = c.place("at_etl", [128, 8 * 512], BF16, B0 + 24 * KB)
    e32 = c.place("at_e32", [128, 8 * 512], F32, B0 + 32 * KB)
    zbs = c.place("at_zbs", [64, 512], F32, B0 + 48 * KB)
    t32 = c.place("at_t32", [64, 2 * 512], F32, B0 + 50 * KB)
    ohst = c.place("at_oh", [64, 2 * 512], BF16, B0 + 54 * KB)
    olst = c.place("at_ol", [64, 2 * 512], BF16, B0 + 56 * KB)
    zrow = c.place("at_zr", [1, 512], F32, B0 + 58 * KB)
    zinv = c.place("at_zi", [1, 512], F32, B0 + 60 * KB)
    zih = c.place("at_zih", [1, 512], BF16, B0 + 62 * KB)
    zil = c.place("at_zil", [1, 512], BF16, B0 + 63 * KB)

    act_bank = [0] * 4
    pe_bank = [0] * 4
    dve_slot = [0] * 8
    pool_slot = [0] * 8
    pe_slot = [0] * 8
    dve_mask = [0] * 4
    pv_pe = [0, 0]
    evac_done = [0, 0]
    sp_tok = [None, None]

    for qc in range(NQC):
        nc.sync.wait_ge(c.sPE.h, c.sPE.n)
        c.LQ.begin()
        c.LQ.dma(qch[:].rearrange("p (b w) -> p b w", w=512),
                 c.sp_qh[:, qc * 512:(qc + 1) * 512].rearrange(
                     "(b p) w -> p b w", p=128))
        c.LQ.dma(qcl[:].rearrange("p (b w) -> p b w", w=512),
                 c.sp_ql[:, qc * 512:(qc + 1) * 512].rearrange(
                     "(b p) w -> p b w", p=128))
        c.LQ.wait(nc.tensor)
        c.LQ.wait(nc.tensor, v_tok)
        nkc = 4 * qc + 4
        for hp in range(NH // 2):
            co = hp

            def s_group(kc):
                p = kc % 2
                for hi_half in range(2):
                    bank = 2 * hi_half + p
                    po = 64 * hi_half
                    ps = c.PS[bank]
                    lh_h = c.KH[po:po + 64,
                                co * T + kc * 128: co * T + kc * 128 + 128]
                    lh_l = c.KL[po:po + 64,
                                co * T + kc * 128: co * T + kc * 128 + 128]
                    r_h = qch[po:po + 64, co * 512:(co + 1) * 512]
                    r_l = qcl[po:po + 64, co * 512:(co + 1) * 512]
                    nc.tensor.wait_ge(c.sACT.h, act_bank[bank])
                    nc.tensor.wait_ge(c.sDVE.h, dve_mask[bank])
                    nc.tensor.matmul(ps[:], lh_h, r_h, start=True, stop=False)
                    nc.tensor.matmul(ps[:], lh_h, r_l, start=False,
                                     stop=False)
                    c.sPE.inc(nc.tensor.matmul(ps[:], lh_l, r_h, start=False,
                                               stop=True))
                    pe_bank[bank] = c.sPE.n

            def epv(kc):
                p = kc % 2
                delta = kc * 128 - qc * 512
                for hi_half in range(2):
                    bank = 2 * hi_half + p
                    slot = (kc % 4) * 2 + hi_half
                    ps = c.PS[bank]
                    sl = slice(slot * 512, slot * 512 + 512)
                    if delta >= 0:
                        j = delta // 128
                        nc.vector.wait_ge(c.sPE.h, pe_bank[bank])
                        c.sDVE.inc(nc.vector.tensor_add(
                            ps[:], ps[:],
                            c.k_masks[:, j * 512:(j + 1) * 512]))
                        dve_mask[bank] = c.sDVE.n
                        nc.scalar.wait_ge(c.sDVE.h, dve_mask[bank])
                    else:
                        nc.scalar.wait_ge(c.sPE.h, pe_bank[bank])
                    nc.scalar.wait_ge(c.sPOOL.h, pool_slot[slot])
                    nc.scalar.wait_ge(c.sPE.h, pe_slot[slot])
                    c.sACT.inc(nc.scalar.activation(
                        eth[:, sl], ps[:], AF.Exp, bias=0.0, scale=0.125))
                    hi_mark = c.sACT.n
                    c.sACT.inc(nc.scalar.activation(
                        e32[:, sl], ps[:], AF.Exp, bias=0.0, scale=0.125))
                    act_bank[bank] = c.sACT.n
                    dve_slot[slot] = hi_mark
                    nc.gpsimd.wait_ge(c.sACT.h, act_bank[bank])
                    nc.gpsimd.wait_ge(c.sPE.h, pe_slot[slot])
                    c.sPOOL.inc(nc.gpsimd.tensor_sub(etl[:, sl], e32[:, sl],
                                                     eth[:, sl]))
                    pool_slot[slot] = c.sPOOL.n
                    h = 2 * hp + hi_half
                    pvps = c.PS[4 + hi_half]
                    lh_vh = c.VH[:, (kc * 16 + h) * 65:
                                  (kc * 16 + h) * 65 + 65]
                    lh_vl = c.VL[:, (kc * 16 + h) * 65:
                                  (kc * 16 + h) * 65 + 65]
                    nc.tensor.wait_ge(c.sACT.h, hi_mark)
                    if kc == 0:
                        nc.tensor.wait_ge(c.sDVE.h, evac_done[hi_half])
                    st = (kc == 0)
                    sp = (kc == nkc - 1)
                    nc.tensor.matmul(pvps[0:65, :], lh_vh, eth[:, sl],
                                     start=st, stop=False)
                    nc.tensor.matmul(pvps[0:65, :], lh_vl, eth[:, sl],
                                     start=False, stop=False)
                    nc.tensor.wait_ge(c.sPOOL.h, pool_slot[slot])
                    c.sPE.inc(nc.tensor.matmul(pvps[0:65, :], lh_vh,
                                               etl[:, sl], start=False,
                                               stop=sp))
                    pe_slot[slot] = c.sPE.n
                    pv_pe[hi_half] = c.sPE.n

            s_group(0)
            for kc in range(nkc):
                if kc + 1 < nkc:
                    s_group(kc + 1)
                epv(kc)
            for hi_half in range(2):
                h = 2 * hp + hi_half
                pvps = c.PS[4 + hi_half]
                hsl = slice(hi_half * 512, hi_half * 512 + 512)
                nc.vector.wait_ge(c.sPE.h, pv_pe[hi_half])
                if sp_tok[hi_half] is not None:
                    c.SQ.wait(nc.vector, sp_tok[hi_half])
                c.sDVE.inc(nc.vector.tensor_copy(zrow[:],
                                                 pvps[64:65, :]))
                nc.vector.wait_ge(c.sDVE.h, c.sDVE.n)
                c.sDVE.inc(nc.vector.reciprocal(zinv[:], zrow[:]))
                nc.scalar.wait_ge(c.sDVE.h, c.sDVE.n)
                c.sACT.inc(nc.scalar.activation(zih[:], zinv[:],
                                                AF.Copy, bias=0.0, scale=1.0))
                nc.vector.wait_ge(c.sACT.h, c.sACT.n)
                c.sDVE.inc(nc.vector.scalar_tensor_tensor(
                    zil[:], zinv[:], 1.0, zih[:], ALU.mult,
                    ALU.subtract))
                nc.tensor.wait_ge(c.sDVE.h, c.sDVE.n)
                nc.tensor.wait_ge(c.sDVE.h, evac_done[hi_half])
                nc.tensor.matmul(c.PS[6][0:64, :], c.k_ones1[:, 0:64],
                                 zih[:], start=True, stop=False)
                c.sPE.inc(nc.tensor.matmul(c.PS[6][0:64, :],
                                           c.k_ones1[:, 0:64], zil[:],
                                           start=False, stop=True))
                nc.vector.wait_ge(c.sPE.h, c.sPE.n)
                c.sDVE.inc(nc.vector.tensor_copy(zbs[:], c.PS[6][0:64, :]))
                nc.vector.wait_ge(c.sDVE.h, c.sDVE.n)
                c.sDVE.inc(nc.vector.tensor_mul(t32[:, hsl], pvps[0:64, :],
                                                zbs[:]))
                evac_done[hi_half] = c.sDVE.n
                nc.scalar.wait_ge(c.sDVE.h, evac_done[hi_half])
                if sp_tok[hi_half] is not None:
                    c.SQ.wait(nc.scalar, sp_tok[hi_half])
                    c.SQ.wait(nc.gpsimd, sp_tok[hi_half])
                c.sACT.inc(nc.scalar.activation(ohst[:, hsl], t32[:, hsl],
                                                AF.Copy, bias=0.0, scale=1.0))
                nc.gpsimd.wait_ge(c.sACT.h, c.sACT.n)
                nc.gpsimd.wait_ge(c.sDVE.h, evac_done[hi_half])
                c.sPOOL.inc(nc.gpsimd.tensor_sub(olst[:, hsl], t32[:, hsl],
                                                 ohst[:, hsl]))
                nc.gpsimd.wait_ge(c.sPOOL.h, c.sPOOL.n)
                c.SQ.begin()
                c.SQ.dma(c.sp_oh[h * 64:(h + 1) * 64,
                                 qc * 512:(qc + 1) * 512], ohst[:, hsl])
                c.SQ.dma(c.sp_ol[h * 64:(h + 1) * 64,
                                 qc * 512:(qc + 1) * 512], olst[:, hsl])
                sp_tok[hi_half] = c.SQ.tok()
    bar(c)


def p_oproj_combine(c):
    nc = c.nc
    c.e_w32 = c.place("op_w32", [128, D], F32, c.A0)
    woh = c.place("op_woh", [128, NB * D], BF16, c.A0 + 4 * KB)
    wol = c.place("op_wol", [128, NB * D], BF16, c.A0 + 20 * KB)
    ohc = c.place("op_ohc", [128, NB * 512], BF16, c.A0 + 36 * KB)
    olc = c.place("op_olc", [128, NB * 512], BF16, c.A0 + 44 * KB)
    load_w_split(c, c.woT, woh, wol, NB)
    for tc in range(4):
        nc.sync.wait_ge(c.sPE.h, c.sPE.n)
        c.SQ.wait_all(nc.sync)
        c.LQ.begin()
        c.LQ.dma(ohc[:].rearrange("p (b w) -> p b w", w=512),
                 c.sp_oh[:, tc * 512:(tc + 1) * 512].rearrange(
                     "(b p) w -> p b w", p=128))
        c.LQ.dma(olc[:].rearrange("p (b w) -> p b w", w=512),
                 c.sp_ol[:, tc * 512:(tc + 1) * 512].rearrange(
                     "(b p) w -> p b w", p=128))
        c.LQ.wait(nc.tensor)
        for cb in range(NB):
            ps = c.PS[cb % 2]
            nc.tensor.wait_ge(c.sACT.h, c.sACT.n)
            for d in range(NB):
                lh_h = woh[:, d * D + cb * 128: d * D + cb * 128 + 128]
                lh_l = wol[:, d * D + cb * 128: d * D + cb * 128 + 128]
                oh_s = ohc[:, d * 512:(d + 1) * 512]
                ol_s = olc[:, d * 512:(d + 1) * 512]
                nc.tensor.matmul(ps[:], lh_h, oh_s, start=(d == 0),
                                 stop=False)
                nc.tensor.matmul(ps[:], lh_h, ol_s, start=False, stop=False)
                ins = nc.tensor.matmul(ps[:], lh_l, oh_s, start=False,
                                       stop=(d == NB - 1))
            c.sPE.inc(ins)
            nc.scalar.wait_ge(c.sPE.h, c.sPE.n)
            c.sACT.inc(nc.scalar.activation(
                c.SF[:, cb * T + tc * 512: cb * T + tc * 512 + 512], ps[:],
                AF.Identity, bias=c.k_bo[:, cb:cb + 1], scale=1.0))
    bar(c)
    xb = c.place("cmb_x", [128, 512], F32, c.C0)
    gb = c.place("cmb_g", [128, 512], F32, c.C0 + 2 * KB)
    tb = c.place("cmb_t", [128, 512], F32, c.C0 + 4 * KB)
    for cb in range(NB):
        for tc in range(4):
            sl = slice(cb * T + tc * 512, cb * T + tc * 512 + 512)
            nc.sync.wait_ge(c.sDVE.h, c.sDVE.n)
            nc.sync.wait_ge(c.sPOOL.h, c.sPOOL.n)
            c.LQ.begin()
            c.LQ.dma(xb[:], c.sp_x[cb * 128:(cb + 1) * 128,
                                   tc * 512:(tc + 1) * 512])
            c.LQ.dma(gb[:], c.sp_g[cb * 128:(cb + 1) * 128,
                                   tc * 512:(tc + 1) * 512])
            c.LQ.wait(nc.gpsimd)
            nc.gpsimd.wait_ge(c.sDVE.h, c.sDVE.n)
            c.sPOOL.inc(nc.gpsimd.tensor_mul(tb[:], c.SF[:, sl], gb[:]))
            nc.vector.wait_ge(c.sPOOL.h, c.sPOOL.n)
            c.sDVE.inc(nc.vector.tensor_add(c.XF[:, sl], xb[:], tb[:]))
    bar(c)


def ladder_floor(c, t, u, cl, width):
    nc = c.nc
    nc.vector.wait_ge(c.sPOOL.h, c.sPOOL.n)
    nc.vector.wait_ge(c.sDVE.h, c.sDVE.n)
    c.sDVE.inc(nc.vector.tensor_copy(u[:, 0:width], t[:, 0:width]))
    for i in range(7, -1, -1):
        v = float(2 ** i)
        nc.vector.wait_ge(c.sDVE.h, c.sDVE.n)
        c.sDVE.inc(nc.vector.tensor_scalar(
            cl[:, 0:width], u[:, 0:width], v, -v, ALU.is_ge, ALU.mult))
        nc.vector.wait_ge(c.sDVE.h, c.sDVE.n)
        c.sDVE.inc(nc.vector.tensor_add(u[:, 0:width], u[:, 0:width],
                                        cl[:, 0:width]))


def p_l1_quant(c):
    nc = c.nc
    c.SQ.wait_all(nc.scalar)
    xt = [c.place(f"q1_xt{i}", [128, D], F32, c.A0 + i * 4 * KB)
          for i in range(2)]
    tq = c.place("q1_tq", [128, D], F32, c.A0 + 8 * KB)
    uq = c.place("q1_uq", [128, D], F32, c.A0 + 12 * KB)
    cq = c.place("q1_cq", [128, D], F32, c.A0 + 16 * KB)
    ntm = [c.place(f"q1_ntm{i}", [128, D], F32, c.A0 + 20 * KB + i * 4 * KB)
           for i in range(2)]
    nfs = [c.place(f"q1_nfs{i}", [128, D], BF16, c.C0 + 16 * KB + i * 2 * KB)
           for i in range(2)]
    colm = [c.place(f"q1_colm{i}", [128, 4], F32, c.A0 + 28 * KB + i * 64)
            for i in range(2)]
    xt_free = [0, 0]
    ntm_free = [0, 0]
    ps4_act = [0]
    psb_state = [0]
    nfs_tok = [None, None]
    amax_mark = [0, 0]

    def stage_A(tt):
        par = tt % 2
        psb_mark = psb_state[0]
        for cb in range(NB):
            nc.tensor.wait_ge(c.sDVE.h, psb_mark)
            if cb == 0:
                nc.tensor.wait_ge(c.sDVE.h, xt_free[par])
            nc.tensor.transpose(
                c.PSB[:, 0:128],
                c.AH[:, cb * T + tt * 128: cb * T + tt * 128 + 128],
                c.k_ident[:])
            c.sPE.inc(nc.tensor.transpose(
                c.PSB[:, 128:256],
                c.AL[:, cb * T + tt * 128: cb * T + tt * 128 + 128],
                c.k_ident[:]))
            nc.scalar.wait_ge(c.sPE.h, c.sPE.n)
            c.sACT.inc(nc.scalar.activation(
                xt[par][:, cb * 128:(cb + 1) * 128], c.PSB[:, 0:128],
                AF.Copy, bias=0.0, scale=1.0))
            nc.vector.wait_ge(c.sACT.h, c.sACT.n)
            c.sDVE.inc(nc.vector.tensor_add(
                xt[par][:, cb * 128:(cb + 1) * 128],
                xt[par][:, cb * 128:(cb + 1) * 128], c.PSB[:, 128:256]))
            psb_mark = c.sDVE.n
        psb_state[0] = c.sDVE.n
        cm = colm[par]
        nc.vector.wait_ge(c.sDVE.h, c.sDVE.n)
        c.sDVE.inc(nc.vector.tensor_reduce(
            cm[:, 0:1], xt[par][:], AX.X, ALU.max,
            apply_absolute_value=True))
        nc.vector.wait_ge(c.sDVE.h, c.sDVE.n)
        c.sDVE.inc(nc.vector.tensor_scalar(cm[:, 1:2], cm[:, 0:1], 1e-5,
                                           None, ALU.max))
        nc.vector.wait_ge(c.sDVE.h, c.sDVE.n)
        c.sDVE.inc(nc.vector.reciprocal(cm[:, 2:3], cm[:, 1:2]))
        nc.vector.wait_ge(c.sDVE.h, c.sDVE.n)
        c.sDVE.inc(nc.vector.tensor_scalar(cm[:, 3:4], cm[:, 2:3], 127.0,
                                           None, ALU.mult))
        c.sDVE.inc(nc.vector.tensor_scalar(c.sinv[:, tt:tt + 1],
                                           cm[:, 1:2], 1.0 / 127.0,
                                           None, ALU.mult))
        amax_mark[par] = c.sDVE.n

    def stage_C(tt):
        par = tt % 2
        nc.vector.wait_ge(c.sDVE.h, amax_mark[par])
        c.sDVE.inc(nc.vector.tensor_scalar(
            tq[:], xt[par][:], colm[par][:, 3:4], 128.5, ALU.mult, ALU.add))
        xt_free[par] = c.sDVE.n
        ladder_floor(c, tq, uq, cq, D)
        nc.vector.wait_ge(c.sDVE.h, c.sDVE.n)
        nc.vector.wait_ge(c.sPOOL.h, c.sPOOL.n)
        nc.vector.wait_ge(c.sPE.h, ntm_free[par])
        c.sDVE.inc(nc.vector.scalar_tensor_tensor(
            ntm[par][:], tq[:], -128.0, uq[:], ALU.add, ALU.subtract))

    def stage_D(tt):
        par = tt % 2
        stt_mark = c.sDVE.n
        ev_mark = ps4_act[0]
        for cb4 in range(0, NB, 4):
            nc.tensor.wait_ge(c.sDVE.h, stt_mark)
            nc.tensor.wait_ge(c.sACT.h, ev_mark)
            for j in range(4):
                cb = cb4 + j
                ins = nc.tensor.transpose(
                    c.PS[4][:, j * 128:(j + 1) * 128],
                    ntm[par][:, cb * 128:(cb + 1) * 128], c.k_ident_f32[:])
            c.sPE.inc(ins)
            nc.scalar.wait_ge(c.sPE.h, c.sPE.n)
            if nfs_tok[par] is not None and cb4 == 0:
                c.SQ.wait(nc.scalar, nfs_tok[par])
            for j in range(4):
                cb = cb4 + j
                ins = nc.scalar.activation(
                    nfs[par][:, cb * 128:(cb + 1) * 128],
                    c.PS[4][:, j * 128:(j + 1) * 128], AF.Copy,
                    bias=0.0, scale=1.0)
            c.sACT.inc(ins)
            ev_mark = c.sACT.n
        ps4_act[0] = c.sACT.n
        ntm_free[par] = c.sPE.n
        nc.gpsimd.wait_ge(c.sACT.h, c.sACT.n)
        c.SQ.begin()
        c.SQ.dma(c.sp_n[:, tt * 128:(tt + 1) * 128].rearrange(
            "(cb p) w -> p cb w", p=128),
            nfs[par][:].rearrange("p (cb w) -> p cb w", w=128))
        nfs_tok[par] = c.SQ.tok()

    stage_A(0)
    for tt in range(NT):
        stage_C(tt)
        if tt + 1 < NT:
            stage_A(tt + 1)
        stage_D(tt)
    c.SQ.wait_all(nc.sync)
    bar(c)


def wq1_gen(c):
    nc = c.nc
    CW = 2048
    wbuf = [c.place(f"wq_b{i}", [128, CW], F32, c.A0 + i * 8 * KB)
            for i in range(2)]
    tb = [c.place(f"wq_t{i}", [128, CW], F32, c.A0 + 16 * KB + i * 8 * KB)
          for i in range(2)]
    cb1 = [c.place(f"wq_c{i}", [128, CW], F32, c.A0 + 32 * KB + i * 8 * KB)
           for i in range(2)]
    qb = [c.place(f"wq_q{i}", [128, CW], BF16, c.A0 + 48 * KB + i * 4 * KB)
          for i in range(2)]
    acc = c.place("wq_acc", [128, 8], F32, c.A0 + 56 * KB)
    acch = c.place("wq_acch", [128, 1], BF16, c.A0 + 57 * KB)
    accl = c.place("wq_accl", [128, 1], BF16, c.A0 + 57 * KB + 512)
    toks = [None, None]
    sp_toks = [None, None]
    dve_mark = [0, 0]
    pool_mark = [0, 0]
    for i, (wsrc, wdst, R, W) in enumerate(
            ((c.wg1T, c.sp_wg1, NB, F1), (c.wu1T, c.sp_wu1, NB, F1),
             (c.wd1T, c.sp_wd1, NF1, D))):
        NCH = W // CW if W >= CW else 1
        CWi = min(W, CW)
        nc.vector.wait_ge(c.sPOOL.h, c.sPOOL.n)
        c.sDVE.inc(nc.vector.memset(acc[:, 0:1], 0.0))
        for r in range(R):
            for ch in range(NCH):
                j = (r * NCH + ch) % 2
                nc.sync.wait_ge(c.sDVE.h, dve_mark[j])
                nc.sync.wait_ge(c.sPOOL.h, pool_mark[j])
                c.LQ.begin()
                c.LQ.dma(wbuf[j][:, 0:CWi],
                         wsrc[r * 128:(r + 1) * 128,
                              ch * CWi:(ch + 1) * CWi])
                toks[j] = c.LQ.tok()
                c.LQ.wait(nc.vector, toks[j])
                nc.vector.wait_ge(c.sPOOL.h, c.sPOOL.n)
                c.sDVE.inc(nc.vector.tensor_reduce(
                    acc[:, 1 + j:2 + j], wbuf[j][:, 0:CWi], AX.X, ALU.add,
                    apply_absolute_value=True))
                dve_mark[j] = c.sDVE.n
                pool_mark[j] = 0
                nc.gpsimd.wait_ge(c.sDVE.h, c.sDVE.n)
                c.sPOOL.inc(nc.gpsimd.tensor_add(acc[:, 0:1], acc[:, 0:1],
                                                 acc[:, 1 + j:2 + j]))
                yield
        nc.scalar.wait_ge(c.sPOOL.h, c.sPOOL.n)
        c.sACT.inc(nc.scalar.activation(acch[:], acc[:, 0:1], AF.Copy,
                                        bias=0.0, scale=1.0))
        nc.vector.wait_ge(c.sACT.h, c.sACT.n)
        c.sDVE.inc(nc.vector.scalar_tensor_tensor(
            accl[:], acc[:, 0:1], 1.0, acch[:], ALU.mult, ALU.subtract))
        nc.tensor.wait_ge(c.sDVE.h, c.sDVE.n)
        nc.tensor.wait_ge(c.sPOOL.h, c.sPOOL.n)
        nc.tensor.wait_ge(c.sDVE.h, getattr(c, "ps6_guard", 0))
        nc.tensor.matmul(c.PS[6][:, 0:1], c.k_allones[:], acch[:],
                         start=True, stop=False)
        c.sPE.inc(nc.tensor.matmul(c.PS[6][:, 0:1], c.k_allones[:], accl[:],
                                   start=False, stop=True))
        nc.vector.wait_ge(c.sPE.h, c.sPE.n)
        c.sDVE.inc(nc.vector.tensor_scalar(
            acc[:, 4:5], c.PS[6][:, 0:1], 1.0 / float(R * 128 * W), None,
            ALU.mult))
        c.ps6_user = c.sDVE.n
        nc.vector.wait_ge(c.sDVE.h, c.sDVE.n)
        c.sDVE.inc(nc.vector.tensor_scalar(
            c.cols[:, 2 * i + 1: 2 * i + 2], acc[:, 4:5], 1e-5, None,
            ALU.max))
        nc.vector.wait_ge(c.sDVE.h, c.sDVE.n)
        c.sDVE.inc(nc.vector.reciprocal(c.cols[:, 2 * i: 2 * i + 1],
                                        c.cols[:, 2 * i + 1: 2 * i + 2]))
        scale_mark = c.sDVE.n
        yield
        for r in range(R):
            for ch in range(NCH):
                j = (r * NCH + ch) % 2
                nc.sync.wait_ge(c.sPOOL.h, pool_mark[j])
                nc.sync.wait_ge(c.sDVE.h, dve_mark[j])
                c.LQ.begin()
                c.LQ.dma(wbuf[j][:, 0:CWi],
                         wsrc[r * 128:(r + 1) * 128,
                              ch * CWi:(ch + 1) * CWi])
                toks[j] = c.LQ.tok()
                c.LQ.wait(nc.vector, toks[j])
                nc.vector.wait_ge(c.sDVE.h,
                                  scale_mark if r == 0 and ch == 0
                                  else dve_mark[j])
                nc.vector.wait_ge(c.sPOOL.h, pool_mark[j])
                c.sDVE.inc(nc.vector.tensor_scalar(
                    tb[j][:, 0:CWi], wbuf[j][:, 0:CWi],
                    c.cols[:, 2 * i: 2 * i + 1], None, ALU.mult))
                nc.vector.wait_ge(c.sDVE.h, c.sDVE.n)
                c.sDVE.inc(nc.vector.tensor_scalar(
                    cb1[j][:, 0:CWi], tb[j][:, 0:CWi], 0.5, None, ALU.is_ge))
                nc.vector.wait_ge(c.sDVE.h, c.sDVE.n)
                c.sDVE.inc(nc.vector.tensor_scalar(
                    wbuf[j][:, 0:CWi], tb[j][:, 0:CWi], -0.5, None,
                    ALU.is_le))
                dve_mark[j] = c.sDVE.n
                nc.gpsimd.wait_ge(c.sDVE.h, dve_mark[j])
                if sp_toks[j] is not None:
                    c.SQ.wait(nc.gpsimd, sp_toks[j])
                c.sPOOL.inc(nc.gpsimd.tensor_sub(qb[j][:, 0:CWi],
                                                 cb1[j][:, 0:CWi],
                                                 wbuf[j][:, 0:CWi]))
                pool_mark[j] = c.sPOOL.n
                nc.gpsimd.wait_ge(c.sPOOL.h, pool_mark[j])
                c.SQ.begin()
                c.SQ.dma(wdst[r * 128:(r + 1) * 128,
                              ch * CWi:(ch + 1) * CWi], qb[j][:, 0:CWi])
                sp_toks[j] = c.SQ.tok()
                yield
    while True:
        yield


def p_rvk_wkv(c):
    nc = c.nc
    ewc = c.place("wk_ew", [128, NB], F32, c.E0 + 704)
    nc.scalar.wait_ge(c.sDVE.h, c.sDVE.n)
    c.sACT.inc(nc.scalar.activation(ewc[:], c.k_td[:], AF.Exp, bias=0.0,
                                    scale=1.0))
    nc.scalar.wait_ge(c.sACT.h, c.sACT.n)
    c.sACT.inc(nc.scalar.activation(ewc[:], ewc[:], AF.Exp, bias=0.0,
                                    scale=-1.0))

    kf = c.place("wk_k", [128, T], F32, c.A0)
    vf = c.place("wk_v", [128, T], F32, c.A0 + 8 * KB)
    af = c.place("wk_a", [128, T], F32, c.A0 + 24 * KB)
    ewm = c.place("wk_ewm", [128, T], F32, c.A0 + 32 * KB)
    av = c.place("wk_av", [128, T], F32, c.A0 + 40 * KB)
    rf = c.place("wk_r", [128, T], F32, c.A0 + 40 * KB)
    sz1 = c.place("wk_s1", [128, T + 8], F32, c.A0 + 48 * KB)
    sz2 = c.place("wk_s2", [128, T + 8], F32, c.A0 + 48 * KB + 8224)
    w32 = c.place("wk_w32", [128, D], F32, c.A0 + 16 * KB)
    whb = c.place("wk_whb", [128, D], BF16, c.A0 + 20 * KB)
    wlb = c.place("wk_wlb", [128, D], BF16, c.A0 + 22 * KB)
    colw = c.place("wk_colw", [128, 4], F32, c.E0 + 768)

    evac_mark = [0, 0, 0, 0]

    def load_split(proj, cb):
        col0 = (proj * NB + cb) * 128
        nc.sync.wait_ge(c.sACT.h, c.sACT.n)
        nc.sync.wait_ge(c.sDVE.h, c.sDVE.n)
        c.LQ.begin()
        c.LQ.dma(w32[:].rearrange("p (d w) -> p d w", w=128),
                 c.wrvkT[:, col0:col0 + 128].rearrange(
                     "(d p) w -> p d w", p=128))
        c.LQ.wait(nc.scalar)
        nc.scalar.wait_ge(c.sPE.h, c.sPE.n)
        c.sACT.inc(nc.scalar.activation(whb[:], w32[:], AF.Copy,
                                        bias=0.0, scale=1.0))
        nc.vector.wait_ge(c.sACT.h, c.sACT.n)
        c.sDVE.inc(nc.vector.scalar_tensor_tensor(
            wlb[:], w32[:], 1.0, whb[:], ALU.mult, ALU.subtract))
        return c.sDVE.n

    def proj_mms(wsplit_mark):
        nc.tensor.wait_ge(c.sDVE.h, wsplit_mark)
        for tc in range(4):
            nc.tensor.wait_ge(c.sDVE.h, evac_mark[tc])
            ps = c.PS[tc]
            for d in range(NB):
                lh_h = whb[:, d * 128:(d + 1) * 128]
                lh_l = wlb[:, d * 128:(d + 1) * 128]
                ah = c.AH[:, d * T + tc * 512: d * T + tc * 512 + 512]
                al = c.AL[:, d * T + tc * 512: d * T + tc * 512 + 512]
                nc.tensor.matmul(ps[:], lh_h, ah, start=(d == 0),
                                 stop=False)
                nc.tensor.matmul(ps[:], lh_h, al, start=False, stop=False)
                ins = nc.tensor.matmul(ps[:], lh_l, ah, start=False,
                                       stop=(d == NB - 1))
            c.sPE.inc(ins)
        return c.sPE.n

    def proj_evac(dst):
        nc.vector.wait_ge(c.sPE.h, c.sPE.n)
        nc.vector.wait_ge(c.sPOOL.h, c.sPOOL.n)
        nc.vector.wait_ge(c.sACT.h, c.sACT.n)
        for tc in range(4):
            c.sDVE.inc(nc.vector.tensor_copy(
                dst[:, tc * 512:(tc + 1) * 512], c.PS[tc][:]))
            evac_mark[tc] = c.sDVE.n

    for cb in range(NB):
        for proj, dst in ((2, kf), (1, vf)):
            proj_mms(load_split(proj, cb))
            proj_evac(dst)
        r_mm = proj_mms(load_split(0, cb))
        ew_col = ewc[:, cb:cb + 1]
        nc.vector.wait_ge(c.sDVE.h, c.sDVE.n)
        c.sDVE.inc(nc.vector.tensor_reduce(colw[:, 0:1], kf[:], AX.X,
                                           ALU.max))
        nc.vector.wait_ge(c.sDVE.h, c.sDVE.n)
        c.sDVE.inc(nc.vector.tensor_scalar(colw[:, 1:2], colw[:, 0:1], -1.0,
                                           None, ALU.mult))
        nc.scalar.wait_ge(c.sDVE.h, c.sDVE.n)
        c.sACT.inc(nc.scalar.activation(af[:], kf[:], AF.Exp,
                                        bias=colw[:, 1:2], scale=1.0))
        c.sACT.inc(nc.scalar.activation(ewm[:], kf[:], AF.Identity,
                                        bias=ew_col, scale=0.0))
        nc.gpsimd.wait_ge(c.sACT.h, c.sACT.n)
        c.sPOOL.inc(nc.gpsimd.tensor_mul(av[:], af[:], vf[:]))
        nc.vector.wait_ge(c.sPOOL.h, c.sPOOL.n)
        c.sDVE.inc(nc.vector.memset(sz1[:, 0:1], 0.0))
        c.sDVE.inc(nc.vector.memset(sz2[:, 0:1], 0.0))
        nc.vector.wait_ge(c.sDVE.h, c.sDVE.n)
        c.sDVE.inc(nc.vector.tensor_tensor_scan(
            sz1[:, 1:T + 1], ewm[:], av[:], 0.0, ALU.mult, ALU.add))
        nc.vector.wait_ge(c.sDVE.h, c.sDVE.n)
        c.sDVE.inc(nc.vector.tensor_tensor_scan(
            sz2[:, 1:T + 1], ewm[:], af[:], 0.0, ALU.mult, ALU.add))
        nc.vector.wait_ge(c.sDVE.h, c.sDVE.n)
        c.sDVE.inc(nc.vector.tensor_add(colw[:, 2:3], colw[:, 1:2],
                                        c.k_tf[:, cb:cb + 1]))
        nc.scalar.wait_ge(c.sDVE.h, c.sDVE.n)
        c.sACT.inc(nc.scalar.activation(ewm[:], kf[:], AF.Exp,
                                        bias=colw[:, 2:3], scale=1.0))
        nc.gpsimd.wait_ge(c.sACT.h, c.sACT.n)
        nc.gpsimd.wait_ge(c.sDVE.h, c.sDVE.n)
        c.sPOOL.inc(nc.gpsimd.tensor_mul(av[:], ewm[:], vf[:]))
        nc.vector.wait_ge(c.sPOOL.h, c.sPOOL.n)
        c.sDVE.inc(nc.vector.tensor_add(vf[:], sz1[:, 0:T], av[:]))
        nc.vector.wait_ge(c.sPE.h, r_mm)
        nc.vector.wait_ge(c.sDVE.h, c.sDVE.n)
        for tc in range(4):
            c.sDVE.inc(nc.vector.tensor_copy(
                rf[:, tc * 512:(tc + 1) * 512], c.PS[tc][:]))
            evac_mark[tc] = c.sDVE.n
        nc.gpsimd.wait_ge(c.sDVE.h, c.sDVE.n)
        c.sPOOL.inc(nc.gpsimd.tensor_add(af[:], sz2[:, 0:T], ewm[:]))
        nc.vector.wait_ge(c.sPOOL.h, c.sPOOL.n)
        c.sDVE.inc(nc.vector.reciprocal(kf[:], af[:]))
        nc.gpsimd.wait_ge(c.sDVE.h, c.sDVE.n)
        c.sPOOL.inc(nc.gpsimd.tensor_mul(kf[:], kf[:], vf[:]))
        nc.scalar.wait_ge(c.sPOOL.h, c.sPOOL.n)
        nc.scalar.wait_ge(c.sDVE.h, c.sDVE.n)
        c.sACT.inc(nc.scalar.activation(af[:], rf[:], AF.Sigmoid, bias=0.0,
                                        scale=1.0))
        nc.vector.wait_ge(c.sACT.h, c.sACT.n)
        c.sDVE.inc(nc.vector.tensor_mul(kf[:], kf[:], af[:]))
        nc.scalar.wait_ge(c.sDVE.h, c.sDVE.n)
        c.sACT.inc(nc.scalar.activation(c.RWH[:, cb * T:(cb + 1) * T], kf[:],
                                        AF.Copy, bias=0.0, scale=1.0))
        nc.gpsimd.wait_ge(c.sACT.h, c.sACT.n)
        c.sPOOL.inc(nc.gpsimd.tensor_sub(
            c.RWL[:, cb * T:(cb + 1) * T], kf[:],
            c.RWH[:, cb * T:(cb + 1) * T]))
    bar(c)


def p_outproj(c):
    nc = c.nc
    c.e_w32 = c.place("o1_w32", [128, D], F32, c.A0)
    wh = c.place("o1_wh", [128, NB * D], BF16, c.A0 + 4 * KB)
    wl = c.place("o1_wl", [128, NB * D], BF16, c.A0 + 20 * KB)
    load_w_split(c, c.woutT, wh, wl, NB)
    op_bank = [0, 0]
    for cb in range(NB):
        for tc in range(4):
            ps = c.PS[tc % 2]
            nc.tensor.wait_ge(c.sACT.h, op_bank[tc % 2])
            for d in range(NB):
                lh_h = wh[:, d * D + cb * 128: d * D + cb * 128 + 128]
                lh_l = wl[:, d * D + cb * 128: d * D + cb * 128 + 128]
                rh = c.RWH[:, d * T + tc * 512: d * T + tc * 512 + 512]
                rl = c.RWL[:, d * T + tc * 512: d * T + tc * 512 + 512]
                nc.tensor.matmul(ps[:], lh_h, rh, start=(d == 0), stop=False)
                nc.tensor.matmul(ps[:], lh_h, rl, start=False, stop=False)
                ins = nc.tensor.matmul(ps[:], lh_l, rh, start=False,
                                       stop=(d == NB - 1))
            c.sPE.inc(ins)
            nc.scalar.wait_ge(c.sPE.h, c.sPE.n)
            c.sACT.inc(nc.scalar.activation(
                c.SF[:, cb * T + tc * 512: cb * T + tc * 512 + 512], ps[:],
                AF.Identity, bias=0.0, scale=1.0))
            op_bank[tc % 2] = c.sACT.n
    nc.gpsimd.wait_ge(c.sACT.h, c.sACT.n)
    c.SQ.begin()
    c.SQ.dma(c.sp_s[:].rearrange("(b p) t -> p b t", p=128),
             c.SF[:].rearrange("p (b t) -> p b t", b=NB))
    c.SQ.wait(nc.sync)
    bar(c)


def p_ffn1_combine(c):
    nc = c.nc
    G = [c.place(f"f1_g{i}", [128, F1], F32, c.A0 + i * 16 * KB)
         for i in range(2)]
    nfc = [c.place(f"f1_nfc{i}", [128, NB * 128], BF16,
                   c.A0 + 32 * KB + i * 2 * KB) for i in range(2)]
    gt = c.place("f1_gt", [128, D], F32, c.A0 + 36 * KB)
    ostg = c.place("f1_ostg", [128, NB * 128], F32, c.A0 + 36 * KB)
    nh = [c.place(f"f1_nh{i}", [128, F1], BF16, c.A0 + 40 * KB + i * 8 * KB)
          for i in range(2)]
    nhf = c.place("f1_nhf", [128, F1], BF16, c.A0 + 56 * KB)
    xsb = [c.place("f1_xs", [128, 256], F32, c.A0 + 64 * KB)] * 2
    wgb = [c.place(f"f1_wgb{i}", [128, NB * 512], BF16,
                   c.B0 + i * 8 * KB) for i in range(2)]
    wub = [c.place(f"f1_wub{i}", [128, NB * 512], BF16,
                   c.B0 + 16 * KB + i * 8 * KB) for i in range(2)]
    uq = c.place("f1_uq", [128, F1], F32, c.B0 + 32 * KB)
    cq = c.place("f1_cq", [128, F1], F32, c.B0 + 48 * KB)
    wdq = c.place("f1_wdq", [128, NF1 * D], BF16, c.C0)
    ppc = c.place("f1_ppc", [128, 16], F32, c.E0 + 832)

    c.LQ.begin()
    c.LQ.dma(wdq[:].rearrange("p (h w) -> p h w", w=D),
             c.sp_wd1[:].rearrange("(h p) w -> p h w", p=128))
    wdq_tok = c.LQ.tok()

    w_toks = [None, None]
    act_bank = [0] * 4
    dve_bank = [0] * 4
    g_free = [0, 0]
    nh_marks = [0, 0]
    nh_pe_free = [0, 0]
    out_tok = None
    state = {"gt_act": 0, "out_pool": 0}

    def load_w(cc):
        j = cc % 2
        nc.sync.wait_ge(c.sPE.h, c.sPE.n)
        c.LQ.begin()
        c.LQ.dma(wgb[j][:].rearrange("p (d w) -> p d w", w=512),
                 c.sp_wg1[:, cc * 512:(cc + 1) * 512].rearrange(
                     "(d p) w -> p d w", p=128))
        c.LQ.dma(wub[j][:].rearrange("p (d w) -> p d w", w=512),
                 c.sp_wu1[:, cc * 512:(cc + 1) * 512].rearrange(
                     "(d p) w -> p d w", p=128))
        w_toks[j] = c.LQ.tok()

    def gates_h(tcn):
        nc.vector.wait_ge(c.sACT.h, c.sACT.n)
        nc.vector.wait_ge(c.sDVE.h, c.sDVE.n)
        for ti in range(2):
            tt = 2 * tcn + ti
            c.sDVE.inc(nc.vector.tensor_mul(
                ppc[:, 2 * ti:2 * ti + 1], c.sinv[:, tt:tt + 1],
                c.cols[:, 1:2]))
            c.sDVE.inc(nc.vector.tensor_mul(
                ppc[:, 2 * ti + 1:2 * ti + 2], c.sinv[:, tt:tt + 1],
                c.cols[:, 3:4]))
        pp_mark = c.sDVE.n
        nfc_toks = [None, None]
        for ti in range(2):
            tt = 2 * tcn + ti
            nc.sync.wait_ge(c.sPE.h, c.sPE.n)
            c.LQ.begin()
            c.LQ.dma(nfc[ti][:].rearrange("p (d w) -> p d w", w=128),
                     c.sp_n[:, tt * 128:(tt + 1) * 128].rearrange(
                         "(d p) w -> p d w", p=128))
            nfc_toks[ti] = c.LQ.tok()
        load_w(0)
        for cc in range(NB):
            if cc + 1 < NB:
                load_w(cc + 1)
            j = cc % 2
            c.LQ.wait(nc.tensor, w_toks[j])
            for ti in range(2):
                c.LQ.wait(nc.tensor, nfc_toks[ti])
                psg, psu = c.PS[2 * ti], c.PS[2 * ti + 1]
                nc.tensor.wait_ge(c.sACT.h, act_bank[2 * ti])
                nc.tensor.wait_ge(c.sDVE.h, dve_bank[2 * ti + 1])
                ins = None
                for d in range(NB):
                    lh = nfc[ti][:, d * 128:(d + 1) * 128]
                    nc.tensor.matmul(psg[:], lh,
                                     wgb[j][:, d * 512:(d + 1) * 512],
                                     start=(d == 0), stop=(d == NB - 1))
                    ins = nc.tensor.matmul(psu[:], lh,
                                           wub[j][:, d * 512:(d + 1) * 512],
                                           start=(d == 0),
                                           stop=(d == NB - 1))
                c.sPE.inc(ins)
                mm_mark = c.sPE.n
                gsl = slice(cc * 512, (cc + 1) * 512)
                nc.scalar.wait_ge(c.sPE.h, mm_mark)
                nc.scalar.wait_ge(c.sDVE.h, max(pp_mark, g_free[ti]))
                c.sACT.inc(nc.scalar.activation(
                    G[ti][:, gsl], psg[:], AF.Silu, bias=0.0,
                    scale=ppc[:, 2 * ti:2 * ti + 1]))
                act_bank[2 * ti] = c.sACT.n
                nc.vector.wait_ge(c.sACT.h, act_bank[2 * ti])
                nc.vector.wait_ge(c.sDVE.h, g_free[ti])
                c.sDVE.inc(nc.vector.scalar_tensor_tensor(
                    G[ti][:, gsl], psu[:], ppc[:, 2 * ti + 1:2 * ti + 2],
                    G[ti][:, gsl], ALU.mult, ALU.mult))
                dve_bank[2 * ti + 1] = c.sDVE.n

    def quantA(tcn, ti):
        tt = 2 * tcn + ti
        hq = G[ti]
        q = 8 + 4 * ti
        nc.vector.wait_ge(c.sDVE.h, dve_bank[2 * ti + 1])
        nc.vector.wait_ge(c.sACT.h, act_bank[2 * ti])
        c.sDVE.inc(nc.vector.tensor_reduce(
            ppc[:, q:q + 1], hq[:], AX.X, ALU.max, apply_absolute_value=True))
        nc.vector.wait_ge(c.sDVE.h, c.sDVE.n)
        c.sDVE.inc(nc.vector.tensor_scalar(ppc[:, q:q + 1], ppc[:, q:q + 1],
                                           1e-5, None, ALU.max))
        nc.vector.wait_ge(c.sDVE.h, c.sDVE.n)
        c.sDVE.inc(nc.vector.reciprocal(ppc[:, q + 1:q + 2], ppc[:, q:q + 1]))
        nc.vector.wait_ge(c.sDVE.h, c.sDVE.n)
        c.sDVE.inc(nc.vector.tensor_scalar(
            c.shinv[:, tt:tt + 1], ppc[:, q:q + 1], 1.0 / 127.0, None,
            ALU.mult))
        c.sDVE.inc(nc.vector.tensor_scalar(ppc[:, q + 1:q + 2],
                                           ppc[:, q + 1:q + 2], 127.0, None,
                                           ALU.mult))
        nc.vector.wait_ge(c.sDVE.h, c.sDVE.n)
        c.sDVE.inc(nc.vector.tensor_scalar(hq[:], hq[:], ppc[:, q + 1:q + 2],
                                           128.5, ALU.mult, ALU.add))
        ladder_floor(c, hq, uq, cq, F1)
        nc.vector.wait_ge(c.sDVE.h, c.sDVE.n)
        nc.vector.wait_ge(c.sPOOL.h, c.sPOOL.n)
        nc.vector.wait_ge(c.sPE.h, nh_pe_free[ti])
        c.sDVE.inc(nc.vector.scalar_tensor_tensor(
            nh[ti][:], hq[:], -128.0, uq[:], ALU.add, ALU.subtract))
        nh_marks[ti] = c.sDVE.n
        g_free[ti] = c.sDVE.n

    def peB(tcn, ti):
        nonlocal out_tok
        tt = 2 * tcn + ti
        ev_mark = 0
        nc.tensor.wait_ge(c.sDVE.h, nh_marks[ti])
        for hb4 in range(0, NF1, 4):
            nc.tensor.wait_ge(c.sACT.h, ev_mark)
            for j in range(4):
                hb = hb4 + j
                ins = nc.tensor.transpose(
                    c.PSB[:, 512 + j * 128:512 + (j + 1) * 128],
                    nh[ti][:, hb * 128:(hb + 1) * 128], c.k_ident[:])
            c.sPE.inc(ins)
            nc.scalar.wait_ge(c.sPE.h, c.sPE.n)
            for j in range(4):
                hb = hb4 + j
                ins = nc.scalar.activation(
                    nhf[:, hb * 128:(hb + 1) * 128],
                    c.PSB[:, 512 + j * 128:512 + (j + 1) * 128],
                    AF.Copy, bias=0.0, scale=1.0)
            c.sACT.inc(ins)
            ev_mark = c.sACT.n
        tr_mark = c.sACT.n
        nh_pe_free[ti] = c.sPE.n
        c.LQ.wait(nc.tensor, wdq_tok)
        nc.tensor.wait_ge(c.sACT.h, tr_mark)
        for dc in range(2):
            ps = c.PS[4 + dc]
            nc.tensor.wait_ge(c.sACT.h, state["gt_act"])
            ins = None
            for hb in range(NF1):
                ins = nc.tensor.matmul(
                    ps[:], nhf[:, hb * 128:(hb + 1) * 128],
                    wdq[:, hb * D + dc * 512: hb * D + dc * 512 + 512],
                    start=(hb == 0), stop=(hb == NF1 - 1))
            c.sPE.inc(ins)
        down_mark = c.sPE.n
        nc.vector.wait_ge(c.sDVE.h, c.sDVE.n)
        c.sDVE.inc(nc.vector.tensor_mul(
            ppc[:, 6:7], c.shinv[:, tt:tt + 1], c.cols[:, 5:6]))
        ppd_mark = c.sDVE.n
        nc.scalar.wait_ge(c.sPE.h, down_mark)
        nc.scalar.wait_ge(c.sDVE.h, ppd_mark)
        if out_tok is not None:
            c.SQ.wait(nc.scalar, out_tok)
        nc.scalar.wait_ge(c.sPOOL.h, state["out_pool"])
        for dc in range(2):
            c.sACT.inc(nc.scalar.activation(
                gt[:, dc * 512:(dc + 1) * 512], c.PS[4 + dc][:], AF.Silu,
                bias=0.0, scale=ppc[:, 6:7]))
        g_mark = c.sACT.n
        state["gt_act"] = g_mark
        for grp in range(2):
            nc.tensor.wait_ge(c.sACT.h, g_mark)
            nc.tensor.wait_ge(c.sDVE.h, c.sDVE.n)
            for q4 in range(4):
                cb = grp * 4 + q4
                ins = nc.tensor.transpose(
                    c.PS[6][:, q4 * 128:(q4 + 1) * 128],
                    gt[:, cb * 128:(cb + 1) * 128], c.k_ident_f32[:])
            c.sPE.inc(ins)
            tr4 = c.sPE.n
            for q4 in range(4):
                cb = grp * 4 + q4
                par = cb % 2
                nc.sync.wait_ge(c.sDVE.h, c.sDVE.n)
                nc.sync.wait_ge(c.sPOOL.h, c.sPOOL.n)
                c.LQ.begin()
                c.LQ.dma(xsb[par][:, 0:128],
                         c.sp_x[cb * 128:(cb + 1) * 128,
                                tt * 128:(tt + 1) * 128])
                c.LQ.dma(xsb[par][:, 128:256],
                         c.sp_s[cb * 128:(cb + 1) * 128,
                                tt * 128:(tt + 1) * 128])
                xs_tok = c.LQ.tok()
                c.LQ.wait(nc.vector, xs_tok)
                nc.vector.wait_ge(c.sPE.h, tr4)
                nc.vector.wait_ge(c.sPOOL.h, c.sPOOL.n)
                if out_tok is not None:
                    c.SQ.wait(nc.vector, out_tok)
                c.sDVE.inc(nc.vector.tensor_mul(
                    ostg[:, cb * 128:(cb + 1) * 128],
                    xsb[par][:, 128:256], c.PS[6][:, q4 * 128:(q4 + 1) * 128]))
                nc.gpsimd.wait_ge(c.sDVE.h, c.sDVE.n)
                if out_tok is not None:
                    c.SQ.wait(nc.gpsimd, out_tok)
                c.sPOOL.inc(nc.gpsimd.tensor_add(
                    ostg[:, cb * 128:(cb + 1) * 128],
                    ostg[:, cb * 128:(cb + 1) * 128],
                    xsb[par][:, 0:128]))
        state["out_pool"] = c.sPOOL.n
        nc.gpsimd.wait_ge(c.sPOOL.h, c.sPOOL.n)
        c.SQ.begin()
        c.SQ.dma(c.out[:, tt * 128:(tt + 1) * 128].rearrange(
            "(b p) w -> p b w", p=128),
            ostg[:].rearrange("p (b w) -> p b w", w=128))
        out_tok = c.SQ.tok()

    pending = None
    for tcn in range(NT // 2):
        gates_h(tcn)
        quantA(tcn, 0)
        if pending is not None:
            peB(*pending)
        quantA(tcn, 1)
        peB(tcn, 0)
        pending = (tcn, 1)
    peB(*pending)
    c.SQ.wait_all(nc.sync)
    bar(c)




def kernel(L0_attn_input, L0_attn_Wq, L0_attn_Wk, L0_attn_Wv, L0_attn_Wo,
           L0_attn_bo, L0_ffn_w_gate, L0_ffn_w_up, L0_ffn_w_down,
           L1_attn_W_rvk, L1_attn_W_out, L1_attn_time_decay,
           L1_attn_time_first, L1_ffn_w_gate, L1_ffn_w_up, L1_ffn_w_down):
    import ml_dtypes

    def bf(x):
        return np.ascontiguousarray(x).astype(ml_dtypes.bfloat16)

    consts = _host_consts()
    common = dict(
        wqT=np.ascontiguousarray(np.asarray(L0_attn_Wq).T),
        wkT=np.ascontiguousarray(np.asarray(L0_attn_Wk).T),
        wvT=np.ascontiguousarray(np.asarray(L0_attn_Wv).T),
        woT=np.ascontiguousarray(np.asarray(L0_attn_Wo).T),
        bo=np.ascontiguousarray(np.asarray(L0_attn_bo).reshape(NB, 128).T),
        wg0T=np.ascontiguousarray(np.asarray(L0_ffn_w_gate).T),
        wu0T=np.ascontiguousarray(np.asarray(L0_ffn_w_up).T),
        wd0T=np.ascontiguousarray(np.asarray(L0_ffn_w_down).T),
        wrvkT=np.ascontiguousarray(np.asarray(L1_attn_W_rvk).T),
        woutT=np.ascontiguousarray(np.asarray(L1_attn_W_out).T),
        td=np.ascontiguousarray(np.asarray(L1_attn_time_decay).reshape(NB, 128).T),
        tf=np.ascontiguousarray(np.asarray(L1_attn_time_first).reshape(NB, 128).T),
        wg1T=np.ascontiguousarray(np.asarray(L1_ffn_w_gate).T),
        wu1T=np.ascontiguousarray(np.asarray(L1_ffn_w_up).T),
        wd1T=np.ascontiguousarray(np.asarray(L1_ffn_w_down).T),
        sin_fm=consts["sin_fm"], cos_fm=consts["cos_fm"],
        sin_lo=consts["sin_lo"], cos_lo=consts["cos_lo"],
        rotPT=bf(consts["rotPT"]), ident=bf(consts["ident"]),
        identf=consts["ident"],
        ones1=bf(consts["ones1"]), allones=bf(consts["allones"]),
        onescol=bf(consts["onescol"]), masks=bf(consts["masks"]),
    )
    x = np.asarray(L0_attn_input)
    nc, _ = build_nc(tuple(_DEBUG))
    in_maps = []
    for b in range(B):
        m = dict(common)
        m["xT"] = np.ascontiguousarray(x[b].T)
        in_maps.append(m)
    res = run_bass_kernel_spmd(nc, in_maps, core_ids=list(range(B)))
    out = np.stack([np.asarray(res.results[b]["out"]).T for b in range(B)], 0)
    return out


# version 20
# speedup vs baseline: 1.0686x; 1.0263x over previous
import numpy as np
import concourse.bass as bass
import concourse.mybir as mybir
from concourse.bass_utils import run_bass_kernel_spmd

F32 = mybir.dt.float32
BF16 = mybir.dt.bfloat16
I32 = mybir.dt.int32
U8 = mybir.dt.uint8
AF = mybir.ActivationFunctionType
ALU = mybir.AluOpType
AX = mybir.AxisListType

B, T, D = 8, 2048, 1024
NB = D // 128
NT = T // 128
NH, HD = 16, 64
F0 = 3072
NF0 = F0 // 128
F1 = 4096
NF1 = F1 // 128
QC = 512
NQC = T // QC
EPS = 1e-5
NEGBIG = -1.0e30
KB = 1024

_DEBUG = []
_PHASES = 99


class Sem:
    def __init__(self, nc, name):
        self.h = nc.alloc_semaphore(name=name)
        self.n = 0

    def inc(self, ins, k=1):
        ins.then_inc(self.h, k)
        self.n += k
        return ins


class DmaQ:

    def __init__(self, nc, name, eng, n=4):
        self.nc, self.eng = nc, eng
        self.sems = [Sem(nc, f"{name}{i}") for i in range(n)]
        self.i = -1
        self.cur = None

    def begin(self):
        self.i = (self.i + 1) % len(self.sems)
        s = self.sems[self.i]
        self.eng.wait_ge(s.h, s.n)
        self.cur = s
        return s

    def dma(self, out, in_):
        self.cur.inc(self.eng.dma_start(out=out, in_=in_), 16)

    def tok(self):
        return (self.cur.h, self.cur.n)

    def wait(self, eng, tok=None):
        h, n = tok if tok is not None else (self.cur.h, self.cur.n)
        eng.wait_ge(h, n)

    def wait_all(self, eng):
        for s in self.sems:
            eng.wait_ge(s.h, s.n)


class Ctx:
    pass




def _host_consts():
    import jax
    cpu = jax.devices("cpu")[0]
    with jax.default_device(cpu):
        import jax.numpy as jnp
        thetas = 10000.0 ** (-jnp.arange(0, D, 2, dtype=jnp.float32) / D)
        thetas = jnp.repeat(thetas, 2)
        angles = (jnp.arange(T, dtype=jnp.float32)[:, None] * thetas[None, :])
        s32 = np.asarray(jnp.sin(angles))
        c32 = np.asarray(jnp.cos(angles))
    sin_fm = np.ascontiguousarray(s32.T)
    cos_fm = np.ascontiguousarray(c32.T)
    sin_lo = np.zeros_like(sin_fm)
    cos_lo = np.zeros_like(cos_fm)
    P = np.zeros((128, 128), np.float32)
    for i in range(64):
        P[2 * i, 2 * i + 1] = -1.0
        P[2 * i + 1, 2 * i] = 1.0
    rotPT = np.ascontiguousarray(P.T)
    ident = np.eye(128, dtype=np.float32)
    ones1 = np.ones((1, 128), np.float32)
    allones = np.ones((128, 128), np.float32)
    onescol = np.ones((128, 1), np.float32)
    masks = np.zeros((128, 4 * QC), np.float32)
    pk = np.arange(128)[:, None]
    fq = np.arange(QC)[None, :]
    for j, delta in enumerate((0, 128, 256, 384)):
        masks[:, j * QC:(j + 1) * QC] = np.where(pk + delta > fq, NEGBIG, 0.0)
    return dict(sin_fm=sin_fm, cos_fm=cos_fm, sin_lo=sin_lo, cos_lo=cos_lo,
                rotPT=rotPT, ident=ident,
                ones1=ones1, allones=allones, onescol=onescol, masks=masks)




def build_nc(debug=()):
    nc = bass.Bass("TRN2", target_bir_lowering=False, debug=False)
    c = Ctx()
    c.nc = nc
    c.debug = set(debug)
    c.uid = 0

    def din(name, shape, dt=F32):
        return nc.dram_tensor(name, list(shape), dt, kind="ExternalInput").ap()

    c.xT = din("xT", [D, T])
    c.wqT = din("wqT", [D, D]); c.wkT = din("wkT", [D, D])
    c.wvT = din("wvT", [D, D]); c.woT = din("woT", [D, D])
    c.bo = din("bo", [128, NB])
    c.wg0T = din("wg0T", [D, F0], I32)
    c.wu0T = din("wu0T", [D, F0], I32)
    c.wd0T = din("wd0T", [F0, D], I32)
    c.wrvkT = din("wrvkT", [D, 3 * D])
    c.woutT = din("woutT", [D, D])
    c.td = din("td", [128, NB]); c.tf = din("tf", [128, NB])
    c.wg1T = din("wg1T", [D, F1])
    c.wu1T = din("wu1T", [D, F1])
    c.wd1T = din("wd1T", [F1, D])
    c.sin = din("sin_fm", [D, T]); c.cos = din("cos_fm", [D, T])
    c.sinl = din("sin_lo", [D, T]); c.cosl = din("cos_lo", [D, T])
    c.rotPT_h = din("rotPT", [128, 128], BF16)
    c.ident_h = din("ident", [128, 128], BF16)
    c.identf_h = din("identf", [128, 128], F32)
    c.ones1_h = din("ones1", [1, 128], BF16)
    c.allones_h = din("allones", [128, 128], BF16)
    c.onescol_h = din("onescol", [128, 1], BF16)
    c.masks_h = din("masks", [128, 4 * QC], BF16)

    c.out = nc.dram_tensor("out", [D, T], F32, kind="ExternalOutput").ap()

    def dsc(name, shape, dt):
        return nc.dram_tensor(name, list(shape), dt, kind="Internal").ap()

    c.sp_x = dsc("sp_x", [D, T], F32)
    c.sp_g = dsc("sp_g", [D, T], F32)
    c.sp_s = dsc("sp_s", [D, T], F32)
    c.sp_vh = dsc("sp_vh", [128, NT * 16 * 65], BF16)
    c.sp_vl = dsc("sp_vl", [128, NT * 16 * 65], BF16)
    c.sp_qh = dsc("sp_qh", [D, T], BF16)
    c.sp_ql = dsc("sp_ql", [D, T], BF16)
    c.sp_oh = dsc("sp_oh", [D, T], BF16)
    c.sp_ol = dsc("sp_ol", [D, T], BF16)
    c.sp_hh = dsc("sp_hh", [F0, T], BF16)
    c.sp_hl = dsc("sp_hl", [F0, T], BF16)
    c.sp_wg0 = dsc("sp_wg0", [D, F0], BF16)
    c.sp_wu0 = dsc("sp_wu0", [D, F0], BF16)
    c.sp_wd0 = dsc("sp_wd0", [F0, D], BF16)
    c.sp_n = dsc("sp_n", [D, T], BF16)
    c.sp_wg1 = dsc("sp_wg1", [D, F1], BF16)
    c.sp_wu1 = dsc("sp_wu1", [D, F1], BF16)
    c.sp_wd1 = dsc("sp_wd1", [F1, D], BF16)

    c.A0, A_SZ = 0, 66560
    c.B0, B_SZ = c.A0 + 66560, 64 * KB
    c.C0, C_SZ = c.B0 + 64 * KB, 64 * KB
    c.D0, D_SZ = c.C0 + 64 * KB, 6400
    c.E0, E_SZ = c.D0 + 6400, 2 * KB
    ARENA = c.E0 + E_SZ
    arena = nc.alloc_sbuf_tensor("ARENA", [128, ARENA], U8)
    c.base = nc.lookup_mloc(arena).addr

    def place(name, shape, dtype, off):
        nbytes = int(np.prod(shape[1:])) * mybir.dt.size(dtype)
        assert off + nbytes <= ARENA, (name, off, nbytes)
        c.uid += 1
        nm = f"{name}_{c.uid}"
        h = nc.virtual_sbuf_tensor(nm, list(shape), dtype)
        nc._tensor(h.name, list(shape), dtype, type="SB")
        mloc = nc.lookup_mloc(h)
        mloc.allocated = True
        mloc.addr = c.base + off
        return h

    c.place = place

    c.XF = place("XF", [128, NB * T], F32, c.A0)
    c.VH = place("VH", [128, NT * 16 * 65], BF16, c.A0)
    c.VL = place("VL", [128, NT * 16 * 65], BF16, c.A0 + 33280)
    c.AH = place("AH", [128, NB * T], BF16, c.B0)
    c.AL = place("AL", [128, NB * T], BF16, c.B0 + 32 * KB)
    c.SF = place("SF", [128, NB * T], F32, c.B0)
    c.KH = place("KH", [128, NB * T], BF16, c.C0)
    c.KL = place("KL", [128, NB * T], BF16, c.C0 + 32 * KB)
    c.RWH, c.RWL = c.KH, c.KL

    off = c.D0
    c.k_masks = place("k_masks", [128, 4 * QC], BF16, off); off += 4 * QC * 2
    c.k_rotPT = place("k_rotPT", [128, 128], BF16, off); off += 256
    c.k_ident = place("k_ident", [128, 128], BF16, off); off += 256
    c.k_ident_f32 = place("k_ident_f32", [128, 128], F32, off); off += 512
    c.k_ones1 = place("k_ones1", [1, 128], BF16, off); off += 256
    c.k_allones = place("k_allones", [128, 128], BF16, off); off += 256
    c.k_onescol = place("k_onescol", [128, 1], BF16, off); off += 32
    c.k_bo = place("k_bo", [128, NB], F32, off); off += NB * 4
    c.k_td = place("k_td", [128, NB], F32, off); off += NB * 4
    c.k_tf = place("k_tf", [128, NB], F32, off); off += NB * 4
    assert off <= c.D0 + D_SZ

    off = c.E0
    c.cols = place("cols", [128, 16], F32, off); off += 16 * 4
    c.sinv = place("sinv", [128, NT], F32, off); off += NT * 4
    c.shinv = place("shinv", [128, NT], F32, off); off += NT * 4
    c.e_t1 = place("e_t1", [128, 128], F32, off); off += 128 * 4
    assert off <= c.E0 + E_SZ, off

    c.PS = [nc.alloc_psum_tensor(f"ps{i}", [128, 512], F32) for i in range(7)]
    c.PSB = nc.alloc_psum_tensor("psb", [128, 1024], BF16)

    c.LQ = DmaQ(nc, "ld", nc.sync, 4)
    c.SQ = DmaQ(nc, "sp", nc.gpsimd, 4)
    c.sPE = Sem(nc, "sPE")
    c.sACT = Sem(nc, "sACT")
    c.sDVE = Sem(nc, "sDVE")
    c.sPOOL = Sem(nc, "sPOOL")

    emit_all(c)

    c.SQ.wait_all(nc.sync)
    c.LQ.wait_all(nc.sync)
    nc.all_engine_barrier()
    return nc, c


def bar(c):
    c.nc.all_engine_barrier()


def tap(c, name, src_ap, shape, dtype):
    if name not in c.debug:
        return
    t = c.nc.dram_tensor("tap_" + name, list(shape), dtype,
                         kind="ExternalOutput").ap()
    bar(c)
    c.SQ.begin()
    c.SQ.dma(t[:], src_ap)
    c.SQ.wait(c.nc.gpsimd)
    bar(c)




def emit_all(c):
    steps = [
        lambda: p0_load_consts(c),
        lambda: p_ln(c, 0),
        lambda: (tap(c, "xn0h", c.AH[:], [128, NB * T], BF16),
                 tap(c, "xn0l", c.AL[:], [128, NB * T], BF16)),
        lambda: p_vproj(c),
        lambda: p_ffn0(c),
        lambda: (p_rope(c), tap(c, "rxh", c.AH[:], [128, NB * T], BF16),
                 tap(c, "rxl", c.AL[:], [128, NB * T], BF16)),
        lambda: (p_qk(c), tap(c, "kh", c.KH[:], [128, NB * T], BF16)),
        lambda: p_attn(c),
        lambda: (p_oproj_combine(c),
                 tap(c, "x1", c.XF[:], [128, NB * T], F32)),
        lambda: p_ln(c, 1),
        lambda: p_l1_quant(c),
        lambda: (p_rvk_wkv(c), tap(c, "rwh", c.RWH[:], [128, NB * T], BF16)),
        lambda: p_outproj(c),
        lambda: p_ffn1_combine(c),
    ]
    for i, s in enumerate(steps):
        if i >= _PHASES:
            break
        s()


def p0_load_consts(c):
    nc = c.nc
    c.LQ.begin()
    for dst, src in ((c.k_masks, c.masks_h), (c.k_rotPT, c.rotPT_h),
                     (c.k_ident, c.ident_h), (c.k_ident_f32, c.identf_h),
                     (c.k_ones1, c.ones1_h),
                     (c.k_allones, c.allones_h), (c.k_onescol, c.onescol_h),
                     (c.k_bo, c.bo), (c.k_td, c.td), (c.k_tf, c.tf)):
        c.LQ.dma(dst[:], src[:])
    c.LQ.dma(c.XF[:].rearrange("p (b t) -> p b t", b=NB),
             c.xT[:].rearrange("(b p) t -> p b t", p=128))
    c.LQ.wait(nc.sync)
    bar(c)


def p_ln(c, layer):
    nc = c.nc
    c.SQ.begin()
    c.SQ.dma(c.sp_x[:].rearrange("(b p) t -> p b t", p=128),
             c.XF[:].rearrange("p (b t) -> p b t", b=NB))

    xh = [c.place(f"ln_xh{i}", [128, T], BF16, c.B0 + i * 4 * KB)
          for i in range(2)]
    xl = [c.place(f"ln_xl{i}", [128, T], BF16, c.B0 + 8 * KB + i * 4 * KB)
          for i in range(2)]
    qh = [c.place(f"ln_qh{i}", [128, T], BF16, c.B0 + 16 * KB + i * 4 * KB)
          for i in range(2)]
    ql = [c.place(f"ln_ql{i}", [128, T], BF16, c.B0 + 24 * KB + i * 4 * KB)
          for i in range(2)]
    sq = [c.place(f"ln_sq{i}", [128, T], F32, c.C0 + i * 8 * KB)
          for i in range(2)]
    xn = [c.place(f"ln_xn{i}", [128, 512], F32, c.C0 + 16 * KB + i * 2 * KB)
          for i in range(2)]
    r_mean = c.place("ln_rmean", [1, T], F32, c.C0 + 20 * KB)
    r_var = c.place("ln_rvar", [1, T], F32, c.C0 + 28 * KB)
    r_rstd = c.place("ln_rrstd", [1, T], F32, c.C0 + 36 * KB)
    r_tmp = c.place("ln_rtmp", [1, T], F32, c.C0)
    rb_mh = c.place("ln_rbmh", [1, T], BF16, c.C0 + 44 * KB)
    rb_ml = c.place("ln_rbml", [1, T], BF16, c.C0 + 48 * KB)
    rb_rh = c.place("ln_rbrh", [1, T], BF16, c.C0 + 52 * KB)
    rb_rl = c.place("ln_rbrl", [1, T], BF16, c.C0 + 56 * KB)

    PSmb, PSrb = c.PS[4], c.PS[5]

    pe_m = [0, 0]
    act_m = [0, 0]
    dve_m = [0, 0]
    pool_m = [0, 0]
    for rnd in range(2):
        tcs = (2 * rnd, 2 * rnd + 1)
        for cb in range(NB):
            j = cb % 2
            xb = c.XF[:, cb * T:(cb + 1) * T]
            nc.scalar.wait_ge(c.sPE.h, pe_m[j])
            nc.scalar.wait_ge(c.sDVE.h, dve_m[j])
            nc.scalar.wait_ge(c.sPOOL.h, pool_m[j])
            c.sACT.inc(nc.scalar.activation(xh[j][:], xb, AF.Copy, bias=0.0,
                                            scale=1.0))
            c.sACT.inc(nc.scalar.activation(sq[j][:], xb, AF.Square,
                                            bias=0.0, scale=1.0))
            act_m[j] = c.sACT.n
            nc.vector.wait_ge(c.sACT.h, act_m[j])
            nc.vector.wait_ge(c.sPE.h, pe_m[j])
            c.sDVE.inc(nc.vector.scalar_tensor_tensor(
                xl[j][:], xb, 1.0, xh[j][:], ALU.mult, ALU.subtract))
            dve_m[j] = c.sDVE.n
            nc.gpsimd.wait_ge(c.sACT.h, act_m[j])
            nc.gpsimd.wait_ge(c.sPE.h, pe_m[j])
            c.sPOOL.inc(nc.gpsimd.tensor_copy(qh[j][:], sq[j][:]))
            pool_m[j] = c.sPOOL.n
            nc.vector.wait_ge(c.sPOOL.h, pool_m[j])
            c.sDVE.inc(nc.vector.tensor_sub(ql[j][:], sq[j][:], qh[j][:]))
            dve_m[j] = c.sDVE.n
            nc.tensor.wait_ge(c.sDVE.h, dve_m[j])
            nc.tensor.wait_ge(c.sPOOL.h, pool_m[j])
            ins = None
            for tcj, tc in enumerate(tcs):
                sl = slice(tc * 512, tc * 512 + 512)
                st = (cb == 0)
                sp = (cb == NB - 1)
                nc.tensor.matmul(c.PS[tcj][0:1, :], c.k_onescol[:],
                                 xh[j][:, sl], start=st, stop=False)
                nc.tensor.matmul(c.PS[tcj][0:1, :], c.k_onescol[:],
                                 xl[j][:, sl], start=False, stop=sp)
                nc.tensor.matmul(c.PS[2 + tcj][0:1, :], c.k_onescol[:],
                                 qh[j][:, sl], start=st, stop=False)
                ins = nc.tensor.matmul(c.PS[2 + tcj][0:1, :], c.k_onescol[:],
                                       ql[j][:, sl], start=False, stop=sp)
            c.sPE.inc(ins)
            pe_m[j] = c.sPE.n
        nc.vector.wait_ge(c.sPE.h, c.sPE.n)
        for tcj, tc in enumerate(tcs):
            sl = slice(tc * 512, tc * 512 + 512)
            c.sDVE.inc(nc.vector.tensor_scalar(
                r_mean[:, sl], c.PS[tcj][0:1, :], 1.0 / D, None, ALU.mult))
            c.sDVE.inc(nc.vector.tensor_scalar(
                r_var[:, sl], c.PS[2 + tcj][0:1, :], 1.0 / D, None,
                ALU.mult))
        nc.tensor.wait_ge(c.sDVE.h, c.sDVE.n)

    nc.scalar.wait_ge(c.sDVE.h, c.sDVE.n)
    nc.scalar.wait_ge(c.sPOOL.h, c.sPOOL.n)
    c.sACT.inc(nc.scalar.activation(r_tmp[:], r_mean[:], AF.Square,
                                    bias=0.0, scale=1.0))
    nc.vector.wait_ge(c.sACT.h, c.sACT.n)
    c.sDVE.inc(nc.vector.tensor_sub(r_var[:], r_var[:], r_tmp[:]))
    nc.vector.wait_ge(c.sDVE.h, c.sDVE.n)
    c.sDVE.inc(nc.vector.tensor_scalar(r_var[:], r_var[:], EPS,
                                       None, ALU.add))
    nc.vector.wait_ge(c.sDVE.h, c.sDVE.n)
    c.sDVE.inc(nc.vector.reciprocal(r_tmp[:], r_var[:]))
    nc.scalar.wait_ge(c.sDVE.h, c.sDVE.n)
    c.sACT.inc(nc.scalar.activation(r_rstd[:], r_tmp[:], AF.Sqrt,
                                    bias=0.0, scale=1.0))
    c.sACT.inc(nc.scalar.activation(rb_mh[:], r_mean[:], AF.Copy,
                                    bias=0.0, scale=1.0))
    nc.scalar.wait_ge(c.sACT.h, c.sACT.n)
    c.sACT.inc(nc.scalar.activation(rb_rh[:], r_rstd[:], AF.Copy,
                                    bias=0.0, scale=1.0))
    nc.vector.wait_ge(c.sACT.h, c.sACT.n)
    c.sDVE.inc(nc.vector.scalar_tensor_tensor(
        rb_ml[:], r_mean[:], 1.0, rb_mh[:], ALU.mult, ALU.subtract))
    c.sDVE.inc(nc.vector.scalar_tensor_tensor(
        rb_rl[:], r_rstd[:], 1.0, rb_rh[:], ALU.mult, ALU.subtract))

    for tc in range(4):
        sl = slice(tc * 512, tc * 512 + 512)
        nc.tensor.wait_ge(c.sDVE.h, c.sDVE.n)
        nc.tensor.wait_ge(c.sPOOL.h, c.sPOOL.n)
        nc.tensor.matmul(PSmb[:], c.k_ones1[:], rb_mh[:, sl],
                         start=True, stop=False)
        c.sPE.inc(nc.tensor.matmul(PSmb[:], c.k_ones1[:], rb_ml[:, sl],
                                   start=False, stop=True))
        nc.tensor.matmul(PSrb[:], c.k_ones1[:], rb_rh[:, sl],
                         start=True, stop=False)
        c.sPE.inc(nc.tensor.matmul(PSrb[:], c.k_ones1[:], rb_rl[:, sl],
                                   start=False, stop=True))
        n_act = [0, 0]
        n_pool = [0, 0]
        pe_bc = c.sPE.n
        for cb in range(NB):
            j = cb % 2
            xb = c.XF[:, cb * T + tc * 512: cb * T + tc * 512 + 512]
            ab = slice(cb * T + tc * 512, cb * T + tc * 512 + 512)
            nc.vector.wait_ge(c.sPE.h, pe_bc)
            nc.vector.wait_ge(c.sACT.h, n_act[j])
            nc.vector.wait_ge(c.sPOOL.h, n_pool[j])
            c.sDVE.inc(nc.vector.tensor_sub(xn[j][:], xb, PSmb[:]))
            nc.vector.wait_ge(c.sDVE.h, c.sDVE.n)
            c.sDVE.inc(nc.vector.tensor_mul(xn[j][:], xn[j][:], PSrb[:]))
            xn_mark = c.sDVE.n
            nc.scalar.wait_ge(c.sDVE.h, xn_mark)
            c.sACT.inc(nc.scalar.activation(c.AH[:, ab], xn[j][:], AF.Copy,
                                            bias=0.0, scale=1.0))
            n_act[j] = c.sACT.n
            nc.gpsimd.wait_ge(c.sACT.h, n_act[j])
            c.sPOOL.inc(nc.gpsimd.tensor_sub(c.AL[:, ab], xn[j][:],
                                             c.AH[:, ab]))
            n_pool[j] = c.sPOOL.n
    bar(c)


def wcast0_gen(c):
    nc = c.nc
    ti = [c.place(f"wc_i{i}", [128, F0], I32, c.C0 + 36 * KB + i * 12 * KB)
          for i in range(1)]
    tb = [c.place(f"wc_b{i}", [128, F0], BF16, c.C0 + 48 * KB + i * 6 * KB)
          for i in range(2)]
    dve_mark = [0, 0]
    sp_toks = [None, None]
    for (srcp, dst, rows, cols) in ((c.wg0T, c.sp_wg0, NB, F0),
                                    (c.wu0T, c.sp_wu0, NB, F0),
                                    (c.wd0T, c.sp_wd0, NF0, D)):
        for r in range(rows):
            j = r % 2
            nc.sync.wait_ge(c.sDVE.h, dve_mark[0])
            c.LQ.begin()
            c.LQ.dma(ti[0][:, 0:cols], srcp[r * 128:(r + 1) * 128, :])
            tok = c.LQ.tok()
            c.LQ.wait(nc.vector, tok)
            if sp_toks[j] is not None:
                c.SQ.wait(nc.vector, sp_toks[j])
            c.sDVE.inc(nc.vector.tensor_copy(tb[j][:, 0:cols],
                                             ti[0][:, 0:cols]))
            dve_mark[0] = c.sDVE.n
            nc.gpsimd.wait_ge(c.sDVE.h, dve_mark[0])
            c.SQ.begin()
            c.SQ.dma(dst[r * 128:(r + 1) * 128, :], tb[j][:, 0:cols])
            sp_toks[j] = c.SQ.tok()
            yield
    while True:
        yield


def load_w_split(c, src_ap, wh, wl, nd):
    nc = c.nc
    W = wh.shape[1] // nd
    for d in range(nd):
        tmp = c.e_w32
        nc.sync.wait_ge(c.sACT.h, c.sACT.n)
        nc.sync.wait_ge(c.sDVE.h, c.sDVE.n)
        c.LQ.begin()
        c.LQ.dma(tmp[:, 0:W], src_ap[d * 128:(d + 1) * 128, :])
        c.LQ.wait(nc.scalar)
        c.sACT.inc(nc.scalar.activation(wh[:, d * W:(d + 1) * W], tmp[:, 0:W],
                                        AF.Copy, bias=0.0, scale=1.0))
        nc.vector.wait_ge(c.sACT.h, c.sACT.n)
        c.sDVE.inc(nc.vector.scalar_tensor_tensor(
            wl[:, d * W:(d + 1) * W], tmp[:, 0:W], 1.0,
            wh[:, d * W:(d + 1) * W], ALU.mult, ALU.subtract))
    nc.tensor.wait_ge(c.sDVE.h, c.sDVE.n)


def p_vproj(c):
    nc = c.nc
    wc = wcast0_gen(c)
    WVH = c.place("wvh", [128, NB * D], BF16, c.C0)
    WVL = c.place("wvl", [128, NB * D], BF16, c.C0 + 16 * KB)
    c.e_w32 = c.place("vw32", [128, D], F32, c.C0 + 32 * KB)
    load_w_split(c, c.wvT, WVH, WVL, NB)

    vhg = c.VH[:].rearrange("p (g w) -> p g w", w=65)
    vlg = c.VL[:].rearrange("p (g w) -> p g w", w=65)
    c.SQ.wait_all(nc.gpsimd)
    c.sPOOL.inc(nc.gpsimd.memset(vhg[:, :, 64:65], 1.0))
    c.sPOOL.inc(nc.gpsimd.memset(vlg[:, :, 64:65], 0.0))

    for tt in range(NT):
        for _ in range(3):
            next(wc)
        for cc in range(2):
            ps = c.PS[cc]
            nc.tensor.wait_ge(c.sACT.h, c.sACT.n)
            nc.tensor.wait_ge(c.sDVE.h, c.sDVE.n)
            for d in range(NB):
                ah = c.AH[:, d * T + tt * 128: d * T + tt * 128 + 128]
                al = c.AL[:, d * T + tt * 128: d * T + tt * 128 + 128]
                wh_s = WVH[:, d * D + cc * 512: d * D + cc * 512 + 512]
                wl_s = WVL[:, d * D + cc * 512: d * D + cc * 512 + 512]
                st = (d == 0)
                nc.tensor.matmul(ps[:], ah, wh_s, start=st, stop=False)
                nc.tensor.matmul(ps[:], ah, wl_s, start=False, stop=False)
                ins = nc.tensor.matmul(ps[:], al, wh_s, start=False,
                                       stop=(d == NB - 1))
            c.sPE.inc(ins)
        for cc in range(2):
            ps = c.PS[cc]
            dst_h = c.VH[:].rearrange("p (g w) -> p g w", w=65)[
                :, tt * 16 + cc * 8: tt * 16 + cc * 8 + 8, 0:64]
            dst_l = c.VL[:].rearrange("p (g w) -> p g w", w=65)[
                :, tt * 16 + cc * 8: tt * 16 + cc * 8 + 8, 0:64]
            src = ps[:].rearrange("p (g w) -> p g w", w=64)
            nc.scalar.wait_ge(c.sPE.h, c.sPE.n)
            c.sACT.inc(nc.scalar.activation(dst_h, src, AF.Copy, bias=0.0,
                                            scale=1.0))
            nc.vector.wait_ge(c.sACT.h, c.sACT.n)
            nc.vector.wait_ge(c.sPE.h, c.sPE.n)
            c.sDVE.inc(nc.vector.scalar_tensor_tensor(
                dst_l, src, 1.0, dst_h, ALU.mult, ALU.subtract))
    for _ in range(40):
        next(wc)
    nc.gpsimd.wait_ge(c.sDVE.h, c.sDVE.n)
    nc.gpsimd.wait_ge(c.sACT.h, c.sACT.n)
    nc.gpsimd.wait_ge(c.sPOOL.h, c.sPOOL.n)
    c.SQ.begin()
    c.SQ.dma(c.sp_vh[:], c.VH[:])
    c.SQ.dma(c.sp_vl[:], c.VL[:])
    c.SQ.wait(nc.sync)
    bar(c)


def p_ffn0(c):
    nc = c.nc
    wgt = [c.place(f"f0_wg{i}", [128, NB * 128], BF16, c.C0 + i * 2 * KB)
           for i in range(2)]
    wut = [c.place(f"f0_wu{i}", [128, NB * 128], BF16,
                   c.C0 + 4 * KB + i * 2 * KB) for i in range(2)]
    sg = c.place("f0_sg", [128, T], F32, c.C0 + 8 * KB)
    hh = [c.place(f"f0_hh{i}", [128, T], BF16, c.C0 + 16 * KB + i * 4 * KB)
          for i in range(2)]
    hl = [c.place(f"f0_hl{i}", [128, T], BF16, c.C0 + 24 * KB + i * 4 * KB)
          for i in range(2)]
    hf = [c.place(f"f0_hf{i}", [128, 512], F32, c.C0 + 32 * KB + i * 2 * KB)
          for i in range(2)]
    w_toks = [None, None]
    buf_pe = [0, 0]
    act_g = [0] * 4
    dve_u = [0] * 4
    act_hf = [0, 0]
    pool_hf = [0, 0]
    sp_toks = [None, None]
    hl_pool = [0, 0]

    def load_w(cb):
        j = cb % 2
        nc.sync.wait_ge(c.sPE.h, buf_pe[j])
        c.LQ.begin()
        c.LQ.dma(wgt[j][:].rearrange("p (d w) -> p d w", w=128),
                 c.sp_wg0[:, cb * 128:(cb + 1) * 128].rearrange(
                     "(d p) w -> p d w", p=128))
        c.LQ.dma(wut[j][:].rearrange("p (d w) -> p d w", w=128),
                 c.sp_wu0[:, cb * 128:(cb + 1) * 128].rearrange(
                     "(d p) w -> p d w", p=128))
        w_toks[j] = c.LQ.tok()

    wq = wq1_gen(c)
    load_w(0)
    for cb in range(NF0):
        for _ in range(5):
            next(wq)
        j = cb % 2
        if cb + 1 < NF0:
            load_w(cb + 1)
        c.LQ.wait(nc.tensor, w_toks[j])
        for tc in range(4):
            nc.tensor.wait_ge(c.sACT.h, act_g[tc])
            if tc == 0:
                nc.tensor.wait_ge(c.sDVE.h, dve_u[3])
            ins = None
            for d in range(NB):
                lh = wgt[j][:, d * 128:(d + 1) * 128]
                ah = c.AH[:, d * T + tc * 512: d * T + tc * 512 + 512]
                al = c.AL[:, d * T + tc * 512: d * T + tc * 512 + 512]
                nc.tensor.matmul(c.PS[tc][:], lh, ah, start=(d == 0),
                                 stop=False)
                ins = nc.tensor.matmul(c.PS[tc][:], lh, al, start=False,
                                       stop=(d == NB - 1))
            c.sPE.inc(ins)
            pe_mark = c.sPE.n
            nc.scalar.wait_ge(c.sPE.h, pe_mark)
            nc.scalar.wait_ge(c.sDVE.h, dve_u[tc])
            c.sACT.inc(nc.scalar.activation(
                sg[:, tc * 512:(tc + 1) * 512], c.PS[tc][:], AF.Silu,
                bias=0.0, scale=1.0 / 127.0))
            act_g[tc] = c.sACT.n
        for tc in range(4):
            bank = 4 + tc if tc < 3 else 0
            nc.tensor.wait_ge(c.sDVE.h, dve_u[tc])
            if bank == 6:
                nc.tensor.wait_ge(c.sDVE.h, getattr(c, "ps6_user", 0))
            if tc == 3:
                nc.tensor.wait_ge(c.sACT.h, act_g[0])
            ins = None
            for d in range(NB):
                lh = wut[j][:, d * 128:(d + 1) * 128]
                ah = c.AH[:, d * T + tc * 512: d * T + tc * 512 + 512]
                al = c.AL[:, d * T + tc * 512: d * T + tc * 512 + 512]
                nc.tensor.matmul(c.PS[bank][:], lh, ah, start=(d == 0),
                                 stop=False)
                ins = nc.tensor.matmul(c.PS[bank][:], lh, al, start=False,
                                       stop=(d == NB - 1))
            c.sPE.inc(ins)
            pe_mark = c.sPE.n
            k = tc % 2
            nc.vector.wait_ge(c.sPE.h, pe_mark)
            nc.vector.wait_ge(c.sACT.h, max(act_g[tc], act_hf[k]))
            nc.vector.wait_ge(c.sPOOL.h, pool_hf[k])
            c.sDVE.inc(nc.vector.scalar_tensor_tensor(
                hf[k][:], c.PS[bank][:], 1.0 / 127.0,
                sg[:, tc * 512:(tc + 1) * 512], ALU.mult, ALU.mult))
            dve_u[tc] = c.sDVE.n
            nc.scalar.wait_ge(c.sDVE.h, dve_u[tc])
            if sp_toks[j] is not None:
                c.SQ.wait(nc.scalar, sp_toks[j])
            c.sACT.inc(nc.scalar.activation(
                hh[j][:, tc * 512:(tc + 1) * 512], hf[k][:], AF.Copy,
                bias=0.0, scale=1.0))
            act_hf[k] = c.sACT.n
            nc.gpsimd.wait_ge(c.sACT.h, act_hf[k])
            if sp_toks[j] is not None:
                c.SQ.wait(nc.gpsimd, sp_toks[j])
            c.sPOOL.inc(nc.gpsimd.tensor_sub(
                hl[j][:, tc * 512:(tc + 1) * 512], hf[k][:],
                hh[j][:, tc * 512:(tc + 1) * 512]))
            pool_hf[k] = c.sPOOL.n
            if bank == 6:
                c.ps6_guard = dve_u[tc]
        hl_pool[j] = c.sPOOL.n
        buf_pe[j] = c.sPE.n
        nc.gpsimd.wait_ge(c.sPOOL.h, hl_pool[j])
        nc.gpsimd.wait_ge(c.sACT.h, c.sACT.n)
        c.SQ.begin()
        c.SQ.dma(c.sp_hh[cb * 128:(cb + 1) * 128, :], hh[j][:])
        c.SQ.dma(c.sp_hl[cb * 128:(cb + 1) * 128, :], hl[j][:])
        sp_toks[j] = c.SQ.tok()
    c.SQ.wait_all(nc.sync)
    bar(c)

    hhc = c.place("f0_hhc", [128, NF0 * 512], BF16, c.C0)
    hlc = c.place("f0_hlc", [128, NF0 * 512], BF16, c.C0 + 24 * KB)
    wdc = [c.place(f"f0_wdc{i}", [128, NF0 * 128], BF16,
                   c.C0 + 48 * KB + i * 6 * KB) for i in range(2)]
    c.e_g512 = c.place("f0_g512", [128, 512], F32, c.C0 + 60 * KB)
    f0_bank = [0, 0]
    for tc in range(4):
        nc.sync.wait_ge(c.sPE.h, c.sPE.n)
        c.LQ.begin()
        c.LQ.dma(hhc[:].rearrange("p (h w) -> p h w", w=512),
                 c.sp_hh[:, tc * 512:(tc + 1) * 512].rearrange(
                     "(h p) w -> p h w", p=128))
        c.LQ.dma(hlc[:].rearrange("p (h w) -> p h w", w=512),
                 c.sp_hl[:, tc * 512:(tc + 1) * 512].rearrange(
                     "(h p) w -> p h w", p=128))
        h_tok = c.LQ.tok()
        for db in range(NB):
            buf = wdc[db % 2]
            nc.sync.wait_ge(c.sPE.h, max(0, c.sPE.n - 1))
            c.LQ.begin()
            c.LQ.dma(buf[:].rearrange("p (h w) -> p h w", w=128),
                     c.sp_wd0[:, db * 128:(db + 1) * 128].rearrange(
                         "(h p) w -> p h w", p=128))
            c.LQ.wait(nc.tensor)
            c.LQ.wait(nc.tensor, h_tok)
            nc.tensor.wait_ge(c.sACT.h, f0_bank[db % 2])
            ps = c.PS[db % 2]
            for hb in range(NF0):
                lh = buf[:, hb * 128:(hb + 1) * 128]
                nc.tensor.matmul(ps[:], lh,
                                 hhc[:, hb * 512:(hb + 1) * 512],
                                 start=(hb == 0), stop=False)
                ins = nc.tensor.matmul(ps[:], lh,
                                       hlc[:, hb * 512:(hb + 1) * 512],
                                       start=False, stop=(hb == NF0 - 1))
            c.sPE.inc(ins)
            nc.scalar.wait_ge(c.sPE.h, c.sPE.n)
            c.SQ.wait_all(nc.scalar)
            c.sACT.inc(nc.scalar.activation(
                c.e_g512[:], ps[:], AF.Silu, bias=0.0, scale=1.0 / 127.0))
            f0_bank[db % 2] = c.sACT.n
            nc.gpsimd.wait_ge(c.sACT.h, c.sACT.n)
            c.SQ.begin()
            c.SQ.dma(c.sp_g[db * 128:(db + 1) * 128, tc * 512:(tc + 1) * 512],
                     c.e_g512[:])
            for _ in range(2):
                next(wq)
    for _ in range(180):
        next(wq)
    c.SQ.wait_all(nc.sync)
    bar(c)


def p_rope(c):
    nc = c.nc
    sct = c.place("rp_sc", [128, 4 * T], F32, c.C0)
    xnr = c.place("rp_xnr", [128, T], F32, c.C0 + 32 * KB)
    ra = c.place("rp_a", [128, 512], F32, c.C0 + 40 * KB)
    rb = c.place("rp_b", [128, 512], F32, c.C0 + 42 * KB)
    rx = c.place("rp_rx", [128, 512], F32, c.C0 + 44 * KB)
    rc = c.place("rp_c", [128, 512], F32, c.C0 + 46 * KB)
    for cb in range(NB):
        nc.sync.wait_ge(c.sDVE.h, c.sDVE.n)
        nc.sync.wait_ge(c.sPOOL.h, c.sPOOL.n)
        c.LQ.begin()
        c.LQ.dma(sct[:, 0:T], c.sin[cb * 128:(cb + 1) * 128, :])
        c.LQ.dma(sct[:, T:2 * T], c.cos[cb * 128:(cb + 1) * 128, :])
        c.LQ.dma(sct[:, 2 * T:3 * T], c.sinl[cb * 128:(cb + 1) * 128, :])
        c.LQ.dma(sct[:, 3 * T:4 * T], c.cosl[cb * 128:(cb + 1) * 128, :])
        ah_b = c.AH[:, cb * T:(cb + 1) * T]
        al_b = c.AL[:, cb * T:(cb + 1) * T]
        nc.vector.wait_ge(c.sACT.h, c.sACT.n)
        c.sDVE.inc(nc.vector.tensor_add(xnr[:], ah_b, al_b))
        for tc in range(4):
            sl = slice(tc * 512, (tc + 1) * 512)
            nc.tensor.wait_ge(c.sPOOL.h, c.sPOOL.n)
            nc.tensor.wait_ge(c.sDVE.h, c.sDVE.n)
            nc.tensor.matmul(c.PS[0][:], c.k_rotPT[:], ah_b[:, sl],
                             start=True, stop=False)
            c.sPE.inc(nc.tensor.matmul(c.PS[0][:], c.k_rotPT[:], al_b[:, sl],
                                       start=False, stop=True))
            c.LQ.wait(nc.gpsimd)
            nc.gpsimd.wait_ge(c.sDVE.h, c.sDVE.n)
            c.sPOOL.inc(nc.gpsimd.tensor_mul(
                ra[:], xnr[:, sl], sct[:, T + tc * 512: T + tc * 512 + 512]))
            nc.gpsimd.wait_ge(c.sPOOL.h, c.sPOOL.n)
            c.sPOOL.inc(nc.gpsimd.tensor_mul(
                rc[:], xnr[:, sl],
                sct[:, 3 * T + tc * 512: 3 * T + tc * 512 + 512]))
            nc.gpsimd.wait_ge(c.sPOOL.h, c.sPOOL.n)
            c.sPOOL.inc(nc.gpsimd.tensor_add(ra[:], ra[:], rc[:]))
            nc.vector.wait_ge(c.sPE.h, c.sPE.n)
            c.LQ.wait(nc.vector)
            nc.vector.wait_ge(c.sDVE.h, c.sDVE.n)
            c.sDVE.inc(nc.vector.tensor_mul(rb[:], c.PS[0][:],
                                            sct[:, tc * 512: tc * 512 + 512]))
            nc.vector.wait_ge(c.sDVE.h, c.sDVE.n)
            c.sDVE.inc(nc.vector.tensor_mul(
                rx[:], c.PS[0][:],
                sct[:, 2 * T + tc * 512: 2 * T + tc * 512 + 512]))
            nc.vector.wait_ge(c.sDVE.h, c.sDVE.n)
            c.sDVE.inc(nc.vector.tensor_add(rb[:], rb[:], rx[:]))
            nc.vector.wait_ge(c.sPOOL.h, c.sPOOL.n)
            nc.vector.wait_ge(c.sDVE.h, c.sDVE.n)
            c.sDVE.inc(nc.vector.tensor_add(rx[:], ra[:], rb[:]))
            nc.scalar.wait_ge(c.sDVE.h, c.sDVE.n)
            c.sACT.inc(nc.scalar.activation(ah_b[:, sl], rx[:], AF.Copy,
                                            bias=0.0, scale=1.0))
            nc.vector.wait_ge(c.sACT.h, c.sACT.n)
            c.sDVE.inc(nc.vector.scalar_tensor_tensor(
                al_b[:, sl], rx[:], 1.0, ah_b[:, sl], ALU.mult, ALU.subtract))
    bar(c)


def p_qk(c):
    nc = c.nc
    c.e_w32 = c.place("qk_w32", [128, D], F32, c.A0)
    qh = c.place("qk_qh", [128, T], BF16, c.A0 + 4 * KB)
    ql = c.place("qk_ql", [128, T], BF16, c.A0 + 8 * KB)
    wbh = c.place("qk_wbh", [128, NB * D], BF16, c.A0 + 12 * KB)
    wbl = c.place("qk_wbl", [128, NB * D], BF16, c.A0 + 28 * KB)
    for which, wsrc in (("k", c.wkT), ("q", c.wqT)):
        load_w_split(c, wsrc, wbh, wbl, NB)
        qk_act = [c.sACT.n] * 4
        qk_dve = [c.sDVE.n] * 4
        for cb in range(NB):
            pe0 = c.sPE.n
            for tc in range(4):
                ps = c.PS[tc]
                nc.tensor.wait_ge(c.sACT.h, qk_act[tc])
                nc.tensor.wait_ge(c.sDVE.h, qk_dve[tc])
                for d in range(NB):
                    lh_h = wbh[:, d * D + cb * 128: d * D + cb * 128 + 128]
                    lh_l = wbl[:, d * D + cb * 128: d * D + cb * 128 + 128]
                    bh = c.AH[:, d * T + tc * 512: d * T + tc * 512 + 512]
                    bl = c.AL[:, d * T + tc * 512: d * T + tc * 512 + 512]
                    nc.tensor.matmul(ps[:], lh_h, bh, start=(d == 0),
                                     stop=False)
                    nc.tensor.matmul(ps[:], lh_h, bl, start=False, stop=False)
                    ins = nc.tensor.matmul(ps[:], lh_l, bh, start=False,
                                           stop=(d == NB - 1))
                c.sPE.inc(ins)
            nc.scalar.wait_ge(c.sPE.h, c.sPE.n)
            for tc in range(4):
                sl = slice(tc * 512, (tc + 1) * 512)
                if which == "k":
                    dsth = c.KH[:, cb * T + tc * 512: cb * T + tc * 512 + 512]
                    dstl = c.KL[:, cb * T + tc * 512: cb * T + tc * 512 + 512]
                else:
                    dsth, dstl = qh[:, sl], ql[:, sl]
                c.sACT.inc(nc.scalar.activation(dsth, c.PS[tc][:], AF.Copy,
                                                bias=0.0, scale=1.0))
                qk_act[tc] = c.sACT.n
                nc.vector.wait_ge(c.sACT.h, c.sACT.n)
                c.sDVE.inc(nc.vector.scalar_tensor_tensor(
                    dstl, c.PS[tc][:], 1.0, dsth, ALU.mult, ALU.subtract))
                qk_dve[tc] = c.sDVE.n
            if which == "q":
                nc.gpsimd.wait_ge(c.sDVE.h, c.sDVE.n)
                nc.gpsimd.wait_ge(c.sACT.h, c.sACT.n)
                c.SQ.begin()
                c.SQ.dma(c.sp_qh[cb * 128:(cb + 1) * 128, :], qh[:])
                c.SQ.dma(c.sp_ql[cb * 128:(cb + 1) * 128, :], ql[:])
                c.SQ.wait(nc.scalar)
                c.SQ.wait(nc.vector)
        bar(c)
    bar(c)


def p_attn(c):
    nc = c.nc
    c.LQ.begin()
    c.LQ.dma(c.VH[:], c.sp_vh[:])
    c.LQ.dma(c.VL[:], c.sp_vl[:])
    v_tok = c.LQ.tok()
    B0 = c.B0
    qch = c.place("at_qch", [128, NB * 512], BF16, B0)
    qcl = c.place("at_qcl", [128, NB * 512], BF16, B0 + 8 * KB)
    eth = c.place("at_eth", [128, 8 * 512], BF16, B0 + 16 * KB)
    etl = c.place("at_etl", [128, 8 * 512], BF16, B0 + 24 * KB)
    e32 = c.place("at_e32", [128, 8 * 512], F32, B0 + 32 * KB)
    zbs = c.place("at_zbs", [64, 512], F32, B0 + 48 * KB)
    t32 = c.place("at_t32", [64, 2 * 512], F32, B0 + 50 * KB)
    ohst = c.place("at_oh", [64, 2 * 512], BF16, B0 + 54 * KB)
    olst = c.place("at_ol", [64, 2 * 512], BF16, B0 + 56 * KB)
    zrow = c.place("at_zr", [1, 512], F32, B0 + 58 * KB)
    zinv = c.place("at_zi", [1, 512], F32, B0 + 60 * KB)
    zih = c.place("at_zih", [1, 512], BF16, B0 + 62 * KB)
    zil = c.place("at_zil", [1, 512], BF16, B0 + 63 * KB)

    act_bank = [0] * 4
    pe_bank = [0] * 4
    dve_slot = [0] * 8
    pool_slot = [0] * 8
    pe_slot = [0] * 8
    dve_mask = [0] * 4
    pv_pe = [0, 0]
    evac_done = [0, 0]
    sp_tok = [None, None]

    for qc in range(NQC):
        nc.sync.wait_ge(c.sPE.h, c.sPE.n)
        c.LQ.begin()
        c.LQ.dma(qch[:].rearrange("p (b w) -> p b w", w=512),
                 c.sp_qh[:, qc * 512:(qc + 1) * 512].rearrange(
                     "(b p) w -> p b w", p=128))
        c.LQ.dma(qcl[:].rearrange("p (b w) -> p b w", w=512),
                 c.sp_ql[:, qc * 512:(qc + 1) * 512].rearrange(
                     "(b p) w -> p b w", p=128))
        c.LQ.wait(nc.tensor)
        c.LQ.wait(nc.tensor, v_tok)
        nkc = 4 * qc + 4
        for hp in range(NH // 2):
            co = hp

            def s_group(kc):
                p = kc % 2
                for hi_half in range(2):
                    bank = 2 * hi_half + p
                    po = 64 * hi_half
                    ps = c.PS[bank]
                    lh_h = c.KH[po:po + 64,
                                co * T + kc * 128: co * T + kc * 128 + 128]
                    lh_l = c.KL[po:po + 64,
                                co * T + kc * 128: co * T + kc * 128 + 128]
                    r_h = qch[po:po + 64, co * 512:(co + 1) * 512]
                    r_l = qcl[po:po + 64, co * 512:(co + 1) * 512]
                    nc.tensor.wait_ge(c.sACT.h, act_bank[bank])
                    nc.tensor.wait_ge(c.sDVE.h, dve_mask[bank])
                    nc.tensor.matmul(ps[:], lh_h, r_h, start=True, stop=False)
                    nc.tensor.matmul(ps[:], lh_h, r_l, start=False,
                                     stop=False)
                    c.sPE.inc(nc.tensor.matmul(ps[:], lh_l, r_h, start=False,
                                               stop=True))
                    pe_bank[bank] = c.sPE.n

            def epv(kc):
                p = kc % 2
                delta = kc * 128 - qc * 512
                for hi_half in range(2):
                    bank = 2 * hi_half + p
                    slot = (kc % 4) * 2 + hi_half
                    ps = c.PS[bank]
                    sl = slice(slot * 512, slot * 512 + 512)
                    if delta >= 0:
                        j = delta // 128
                        nc.vector.wait_ge(c.sPE.h, pe_bank[bank])
                        c.sDVE.inc(nc.vector.tensor_add(
                            ps[:], ps[:],
                            c.k_masks[:, j * 512:(j + 1) * 512]))
                        dve_mask[bank] = c.sDVE.n
                        nc.scalar.wait_ge(c.sDVE.h, dve_mask[bank])
                    else:
                        nc.scalar.wait_ge(c.sPE.h, pe_bank[bank])
                    nc.scalar.wait_ge(c.sPOOL.h, pool_slot[slot])
                    nc.scalar.wait_ge(c.sPE.h, pe_slot[slot])
                    c.sACT.inc(nc.scalar.activation(
                        eth[:, sl], ps[:], AF.Exp, bias=0.0, scale=0.125))
                    hi_mark = c.sACT.n
                    c.sACT.inc(nc.scalar.activation(
                        e32[:, sl], ps[:], AF.Exp, bias=0.0, scale=0.125))
                    act_bank[bank] = c.sACT.n
                    dve_slot[slot] = hi_mark
                    nc.gpsimd.wait_ge(c.sACT.h, act_bank[bank])
                    nc.gpsimd.wait_ge(c.sPE.h, pe_slot[slot])
                    c.sPOOL.inc(nc.gpsimd.tensor_sub(etl[:, sl], e32[:, sl],
                                                     eth[:, sl]))
                    pool_slot[slot] = c.sPOOL.n
                    h = 2 * hp + hi_half
                    pvps = c.PS[4 + hi_half]
                    lh_vh = c.VH[:, (kc * 16 + h) * 65:
                                  (kc * 16 + h) * 65 + 65]
                    lh_vl = c.VL[:, (kc * 16 + h) * 65:
                                  (kc * 16 + h) * 65 + 65]
                    nc.tensor.wait_ge(c.sACT.h, hi_mark)
                    if kc == 0:
                        nc.tensor.wait_ge(c.sDVE.h, evac_done[hi_half])
                    st = (kc == 0)
                    sp = (kc == nkc - 1)
                    nc.tensor.matmul(pvps[0:65, :], lh_vh, eth[:, sl],
                                     start=st, stop=False)
                    nc.tensor.matmul(pvps[0:65, :], lh_vl, eth[:, sl],
                                     start=False, stop=False)
                    nc.tensor.wait_ge(c.sPOOL.h, pool_slot[slot])
                    c.sPE.inc(nc.tensor.matmul(pvps[0:65, :], lh_vh,
                                               etl[:, sl], start=False,
                                               stop=sp))
                    pe_slot[slot] = c.sPE.n
                    pv_pe[hi_half] = c.sPE.n

            s_group(0)
            for kc in range(nkc):
                if kc + 1 < nkc:
                    s_group(kc + 1)
                epv(kc)
            for hi_half in range(2):
                h = 2 * hp + hi_half
                pvps = c.PS[4 + hi_half]
                hsl = slice(hi_half * 512, hi_half * 512 + 512)
                nc.vector.wait_ge(c.sPE.h, pv_pe[hi_half])
                if sp_tok[hi_half] is not None:
                    c.SQ.wait(nc.vector, sp_tok[hi_half])
                c.sDVE.inc(nc.vector.tensor_copy(zrow[:],
                                                 pvps[64:65, :]))
                nc.vector.wait_ge(c.sDVE.h, c.sDVE.n)
                c.sDVE.inc(nc.vector.reciprocal(zinv[:], zrow[:]))
                nc.scalar.wait_ge(c.sDVE.h, c.sDVE.n)
                c.sACT.inc(nc.scalar.activation(zih[:], zinv[:],
                                                AF.Copy, bias=0.0, scale=1.0))
                nc.vector.wait_ge(c.sACT.h, c.sACT.n)
                c.sDVE.inc(nc.vector.scalar_tensor_tensor(
                    zil[:], zinv[:], 1.0, zih[:], ALU.mult,
                    ALU.subtract))
                nc.tensor.wait_ge(c.sDVE.h, c.sDVE.n)
                nc.tensor.wait_ge(c.sDVE.h, evac_done[hi_half])
                nc.tensor.matmul(c.PS[6][0:64, :], c.k_ones1[:, 0:64],
                                 zih[:], start=True, stop=False)
                c.sPE.inc(nc.tensor.matmul(c.PS[6][0:64, :],
                                           c.k_ones1[:, 0:64], zil[:],
                                           start=False, stop=True))
                nc.vector.wait_ge(c.sPE.h, c.sPE.n)
                c.sDVE.inc(nc.vector.tensor_copy(zbs[:], c.PS[6][0:64, :]))
                nc.vector.wait_ge(c.sDVE.h, c.sDVE.n)
                c.sDVE.inc(nc.vector.tensor_mul(t32[:, hsl], pvps[0:64, :],
                                                zbs[:]))
                evac_done[hi_half] = c.sDVE.n
                nc.scalar.wait_ge(c.sDVE.h, evac_done[hi_half])
                if sp_tok[hi_half] is not None:
                    c.SQ.wait(nc.scalar, sp_tok[hi_half])
                    c.SQ.wait(nc.gpsimd, sp_tok[hi_half])
                c.sACT.inc(nc.scalar.activation(ohst[:, hsl], t32[:, hsl],
                                                AF.Copy, bias=0.0, scale=1.0))
                nc.gpsimd.wait_ge(c.sACT.h, c.sACT.n)
                nc.gpsimd.wait_ge(c.sDVE.h, evac_done[hi_half])
                c.sPOOL.inc(nc.gpsimd.tensor_sub(olst[:, hsl], t32[:, hsl],
                                                 ohst[:, hsl]))
                nc.gpsimd.wait_ge(c.sPOOL.h, c.sPOOL.n)
                c.SQ.begin()
                c.SQ.dma(c.sp_oh[h * 64:(h + 1) * 64,
                                 qc * 512:(qc + 1) * 512], ohst[:, hsl])
                c.SQ.dma(c.sp_ol[h * 64:(h + 1) * 64,
                                 qc * 512:(qc + 1) * 512], olst[:, hsl])
                sp_tok[hi_half] = c.SQ.tok()
    bar(c)


def p_oproj_combine(c):
    nc = c.nc
    c.e_w32 = c.place("op_w32", [128, D], F32, c.A0)
    woh = c.place("op_woh", [128, NB * D], BF16, c.A0 + 4 * KB)
    wol = c.place("op_wol", [128, NB * D], BF16, c.A0 + 20 * KB)
    ohc = c.place("op_ohc", [128, NB * 512], BF16, c.A0 + 36 * KB)
    olc = c.place("op_olc", [128, NB * 512], BF16, c.A0 + 44 * KB)
    load_w_split(c, c.woT, woh, wol, NB)
    ob_bank = [c.sACT.n] * 2
    for tc in range(4):
        nc.sync.wait_ge(c.sPE.h, c.sPE.n)
        c.SQ.wait_all(nc.sync)
        c.LQ.begin()
        c.LQ.dma(ohc[:].rearrange("p (b w) -> p b w", w=512),
                 c.sp_oh[:, tc * 512:(tc + 1) * 512].rearrange(
                     "(b p) w -> p b w", p=128))
        c.LQ.dma(olc[:].rearrange("p (b w) -> p b w", w=512),
                 c.sp_ol[:, tc * 512:(tc + 1) * 512].rearrange(
                     "(b p) w -> p b w", p=128))
        c.LQ.wait(nc.tensor)
        for cb in range(NB):
            ps = c.PS[cb % 2]
            nc.tensor.wait_ge(c.sACT.h, ob_bank[cb % 2])
            for d in range(NB):
                lh_h = woh[:, d * D + cb * 128: d * D + cb * 128 + 128]
                lh_l = wol[:, d * D + cb * 128: d * D + cb * 128 + 128]
                oh_s = ohc[:, d * 512:(d + 1) * 512]
                ol_s = olc[:, d * 512:(d + 1) * 512]
                nc.tensor.matmul(ps[:], lh_h, oh_s, start=(d == 0),
                                 stop=False)
                nc.tensor.matmul(ps[:], lh_h, ol_s, start=False, stop=False)
                ins = nc.tensor.matmul(ps[:], lh_l, oh_s, start=False,
                                       stop=(d == NB - 1))
            c.sPE.inc(ins)
            nc.scalar.wait_ge(c.sPE.h, c.sPE.n)
            c.sACT.inc(nc.scalar.activation(
                c.SF[:, cb * T + tc * 512: cb * T + tc * 512 + 512], ps[:],
                AF.Identity, bias=c.k_bo[:, cb:cb + 1], scale=1.0))
            ob_bank[cb % 2] = c.sACT.n
    bar(c)
    xb = c.place("cmb_x", [128, 512], F32, c.C0)
    gb = c.place("cmb_g", [128, 512], F32, c.C0 + 2 * KB)
    tb = c.place("cmb_t", [128, 512], F32, c.C0 + 4 * KB)
    for cb in range(NB):
        for tc in range(4):
            sl = slice(cb * T + tc * 512, cb * T + tc * 512 + 512)
            nc.sync.wait_ge(c.sDVE.h, c.sDVE.n)
            nc.sync.wait_ge(c.sPOOL.h, c.sPOOL.n)
            c.LQ.begin()
            c.LQ.dma(xb[:], c.sp_x[cb * 128:(cb + 1) * 128,
                                   tc * 512:(tc + 1) * 512])
            c.LQ.dma(gb[:], c.sp_g[cb * 128:(cb + 1) * 128,
                                   tc * 512:(tc + 1) * 512])
            c.LQ.wait(nc.gpsimd)
            nc.gpsimd.wait_ge(c.sDVE.h, c.sDVE.n)
            c.sPOOL.inc(nc.gpsimd.tensor_mul(tb[:], c.SF[:, sl], gb[:]))
            nc.vector.wait_ge(c.sPOOL.h, c.sPOOL.n)
            c.sDVE.inc(nc.vector.tensor_add(c.XF[:, sl], xb[:], tb[:]))
    bar(c)


def ladder_floor(c, t, u, cl, width):
    nc = c.nc
    nc.vector.wait_ge(c.sPOOL.h, c.sPOOL.n)
    nc.vector.wait_ge(c.sDVE.h, c.sDVE.n)
    c.sDVE.inc(nc.vector.tensor_copy(u[:, 0:width], t[:, 0:width]))
    for i in range(7, -1, -1):
        v = float(2 ** i)
        nc.vector.wait_ge(c.sDVE.h, c.sDVE.n)
        c.sDVE.inc(nc.vector.tensor_scalar(
            cl[:, 0:width], u[:, 0:width], v, -v, ALU.is_ge, ALU.mult))
        nc.vector.wait_ge(c.sDVE.h, c.sDVE.n)
        c.sDVE.inc(nc.vector.tensor_add(u[:, 0:width], u[:, 0:width],
                                        cl[:, 0:width]))


def p_l1_quant(c):
    nc = c.nc
    c.SQ.wait_all(nc.scalar)
    xt = [c.place(f"q1_xt{i}", [128, D], F32, c.A0 + i * 4 * KB)
          for i in range(2)]
    tq = c.place("q1_tq", [128, D], F32, c.A0 + 8 * KB)
    uq = c.place("q1_uq", [128, D], F32, c.A0 + 12 * KB)
    cq = c.place("q1_cq", [128, D], F32, c.A0 + 16 * KB)
    ntm = [c.place(f"q1_ntm{i}", [128, D], F32, c.A0 + 20 * KB + i * 4 * KB)
           for i in range(2)]
    nfs = [c.place(f"q1_nfs{i}", [128, D], BF16, c.C0 + 16 * KB + i * 2 * KB)
           for i in range(2)]
    colm = [c.place(f"q1_colm{i}", [128, 4], F32, c.A0 + 28 * KB + i * 64)
            for i in range(2)]
    xt_free = [0, 0]
    ntm_free = [0, 0]
    ps4_act = [0]
    psb_state = [0]
    nfs_tok = [None, None]
    amax_mark = [0, 0]

    def stage_A(tt):
        par = tt % 2
        psb_mark = psb_state[0]
        for cb in range(NB):
            nc.tensor.wait_ge(c.sDVE.h, psb_mark)
            if cb == 0:
                nc.tensor.wait_ge(c.sDVE.h, xt_free[par])
            nc.tensor.transpose(
                c.PSB[:, 0:128],
                c.AH[:, cb * T + tt * 128: cb * T + tt * 128 + 128],
                c.k_ident[:])
            c.sPE.inc(nc.tensor.transpose(
                c.PSB[:, 128:256],
                c.AL[:, cb * T + tt * 128: cb * T + tt * 128 + 128],
                c.k_ident[:]))
            nc.scalar.wait_ge(c.sPE.h, c.sPE.n)
            c.sACT.inc(nc.scalar.activation(
                xt[par][:, cb * 128:(cb + 1) * 128], c.PSB[:, 0:128],
                AF.Copy, bias=0.0, scale=1.0))
            nc.vector.wait_ge(c.sACT.h, c.sACT.n)
            c.sDVE.inc(nc.vector.tensor_add(
                xt[par][:, cb * 128:(cb + 1) * 128],
                xt[par][:, cb * 128:(cb + 1) * 128], c.PSB[:, 128:256]))
            psb_mark = c.sDVE.n
        psb_state[0] = c.sDVE.n
        cm = colm[par]
        nc.vector.wait_ge(c.sDVE.h, c.sDVE.n)
        c.sDVE.inc(nc.vector.tensor_reduce(
            cm[:, 0:1], xt[par][:], AX.X, ALU.max,
            apply_absolute_value=True))
        nc.vector.wait_ge(c.sDVE.h, c.sDVE.n)
        c.sDVE.inc(nc.vector.tensor_scalar(cm[:, 1:2], cm[:, 0:1], 1e-5,
                                           None, ALU.max))
        nc.vector.wait_ge(c.sDVE.h, c.sDVE.n)
        c.sDVE.inc(nc.vector.reciprocal(cm[:, 2:3], cm[:, 1:2]))
        nc.vector.wait_ge(c.sDVE.h, c.sDVE.n)
        c.sDVE.inc(nc.vector.tensor_scalar(cm[:, 3:4], cm[:, 2:3], 127.0,
                                           None, ALU.mult))
        c.sDVE.inc(nc.vector.tensor_scalar(c.sinv[:, tt:tt + 1],
                                           cm[:, 1:2], 1.0 / 127.0,
                                           None, ALU.mult))
        amax_mark[par] = c.sDVE.n

    def stage_C(tt):
        par = tt % 2
        nc.vector.wait_ge(c.sDVE.h, amax_mark[par])
        c.sDVE.inc(nc.vector.tensor_scalar(
            tq[:], xt[par][:], colm[par][:, 3:4], 128.5, ALU.mult, ALU.add))
        xt_free[par] = c.sDVE.n
        ladder_floor(c, tq, uq, cq, D)
        nc.vector.wait_ge(c.sDVE.h, c.sDVE.n)
        nc.vector.wait_ge(c.sPOOL.h, c.sPOOL.n)
        nc.vector.wait_ge(c.sPE.h, ntm_free[par])
        c.sDVE.inc(nc.vector.scalar_tensor_tensor(
            ntm[par][:], tq[:], -128.0, uq[:], ALU.add, ALU.subtract))

    def stage_D(tt):
        par = tt % 2
        stt_mark = c.sDVE.n
        ev_mark = ps4_act[0]
        for cb4 in range(0, NB, 4):
            nc.tensor.wait_ge(c.sDVE.h, stt_mark)
            nc.tensor.wait_ge(c.sACT.h, ev_mark)
            for j in range(4):
                cb = cb4 + j
                ins = nc.tensor.transpose(
                    c.PS[4][:, j * 128:(j + 1) * 128],
                    ntm[par][:, cb * 128:(cb + 1) * 128], c.k_ident_f32[:])
            c.sPE.inc(ins)
            nc.scalar.wait_ge(c.sPE.h, c.sPE.n)
            if nfs_tok[par] is not None and cb4 == 0:
                c.SQ.wait(nc.scalar, nfs_tok[par])
            for j in range(4):
                cb = cb4 + j
                ins = nc.scalar.activation(
                    nfs[par][:, cb * 128:(cb + 1) * 128],
                    c.PS[4][:, j * 128:(j + 1) * 128], AF.Copy,
                    bias=0.0, scale=1.0)
            c.sACT.inc(ins)
            ev_mark = c.sACT.n
        ps4_act[0] = c.sACT.n
        ntm_free[par] = c.sPE.n
        nc.gpsimd.wait_ge(c.sACT.h, c.sACT.n)
        c.SQ.begin()
        c.SQ.dma(c.sp_n[:, tt * 128:(tt + 1) * 128].rearrange(
            "(cb p) w -> p cb w", p=128),
            nfs[par][:].rearrange("p (cb w) -> p cb w", w=128))
        nfs_tok[par] = c.SQ.tok()

    stage_A(0)
    for tt in range(NT):
        stage_C(tt)
        if tt + 1 < NT:
            stage_A(tt + 1)
        stage_D(tt)
    c.SQ.wait_all(nc.sync)
    bar(c)


def wq1_gen(c):
    nc = c.nc
    CW = 2048
    wbuf = [c.place(f"wq_b{i}", [128, CW], F32, c.A0 + i * 8 * KB)
            for i in range(2)]
    tb = [c.place(f"wq_t{i}", [128, CW], F32, c.A0 + 16 * KB + i * 8 * KB)
          for i in range(2)]
    cb1 = [c.place(f"wq_c{i}", [128, CW], F32, c.A0 + 32 * KB + i * 8 * KB)
           for i in range(2)]
    qb = [c.place(f"wq_q{i}", [128, CW], BF16, c.A0 + 48 * KB + i * 4 * KB)
          for i in range(2)]
    acc = c.place("wq_acc", [128, 8], F32, c.A0 + 56 * KB)
    acch = c.place("wq_acch", [128, 1], BF16, c.A0 + 57 * KB)
    accl = c.place("wq_accl", [128, 1], BF16, c.A0 + 57 * KB + 512)
    toks = [None, None]
    sp_toks = [None, None]
    dve_mark = [0, 0]
    pool_mark = [0, 0]
    for i, (wsrc, wdst, R, W) in enumerate(
            ((c.wg1T, c.sp_wg1, NB, F1), (c.wu1T, c.sp_wu1, NB, F1),
             (c.wd1T, c.sp_wd1, NF1, D))):
        NCH = W // CW if W >= CW else 1
        CWi = min(W, CW)
        nc.vector.wait_ge(c.sPOOL.h, c.sPOOL.n)
        c.sDVE.inc(nc.vector.memset(acc[:, 0:1], 0.0))
        for r in range(R):
            for ch in range(NCH):
                j = (r * NCH + ch) % 2
                nc.sync.wait_ge(c.sDVE.h, dve_mark[j])
                nc.sync.wait_ge(c.sPOOL.h, pool_mark[j])
                c.LQ.begin()
                c.LQ.dma(wbuf[j][:, 0:CWi],
                         wsrc[r * 128:(r + 1) * 128,
                              ch * CWi:(ch + 1) * CWi])
                toks[j] = c.LQ.tok()
                c.LQ.wait(nc.vector, toks[j])
                nc.vector.wait_ge(c.sPOOL.h, c.sPOOL.n)
                c.sDVE.inc(nc.vector.tensor_reduce(
                    acc[:, 1 + j:2 + j], wbuf[j][:, 0:CWi], AX.X, ALU.add,
                    apply_absolute_value=True))
                dve_mark[j] = c.sDVE.n
                pool_mark[j] = 0
                nc.gpsimd.wait_ge(c.sDVE.h, c.sDVE.n)
                c.sPOOL.inc(nc.gpsimd.tensor_add(acc[:, 0:1], acc[:, 0:1],
                                                 acc[:, 1 + j:2 + j]))
                yield
        nc.scalar.wait_ge(c.sPOOL.h, c.sPOOL.n)
        c.sACT.inc(nc.scalar.activation(acch[:], acc[:, 0:1], AF.Copy,
                                        bias=0.0, scale=1.0))
        nc.vector.wait_ge(c.sACT.h, c.sACT.n)
        c.sDVE.inc(nc.vector.scalar_tensor_tensor(
            accl[:], acc[:, 0:1], 1.0, acch[:], ALU.mult, ALU.subtract))
        nc.tensor.wait_ge(c.sDVE.h, c.sDVE.n)
        nc.tensor.wait_ge(c.sPOOL.h, c.sPOOL.n)
        nc.tensor.wait_ge(c.sDVE.h, getattr(c, "ps6_guard", 0))
        nc.tensor.matmul(c.PS[6][:, 0:1], c.k_allones[:], acch[:],
                         start=True, stop=False)
        c.sPE.inc(nc.tensor.matmul(c.PS[6][:, 0:1], c.k_allones[:], accl[:],
                                   start=False, stop=True))
        nc.vector.wait_ge(c.sPE.h, c.sPE.n)
        c.sDVE.inc(nc.vector.tensor_scalar(
            acc[:, 4:5], c.PS[6][:, 0:1], 1.0 / float(R * 128 * W), None,
            ALU.mult))
        c.ps6_user = c.sDVE.n
        nc.vector.wait_ge(c.sDVE.h, c.sDVE.n)
        c.sDVE.inc(nc.vector.tensor_scalar(
            c.cols[:, 2 * i + 1: 2 * i + 2], acc[:, 4:5], 1e-5, None,
            ALU.max))
        nc.vector.wait_ge(c.sDVE.h, c.sDVE.n)
        c.sDVE.inc(nc.vector.reciprocal(c.cols[:, 2 * i: 2 * i + 1],
                                        c.cols[:, 2 * i + 1: 2 * i + 2]))
        scale_mark = c.sDVE.n
        yield
        for r in range(R):
            for ch in range(NCH):
                j = (r * NCH + ch) % 2
                nc.sync.wait_ge(c.sPOOL.h, pool_mark[j])
                nc.sync.wait_ge(c.sDVE.h, dve_mark[j])
                c.LQ.begin()
                c.LQ.dma(wbuf[j][:, 0:CWi],
                         wsrc[r * 128:(r + 1) * 128,
                              ch * CWi:(ch + 1) * CWi])
                toks[j] = c.LQ.tok()
                c.LQ.wait(nc.vector, toks[j])
                nc.vector.wait_ge(c.sDVE.h,
                                  scale_mark if r == 0 and ch == 0
                                  else dve_mark[j])
                nc.vector.wait_ge(c.sPOOL.h, pool_mark[j])
                c.sDVE.inc(nc.vector.tensor_scalar(
                    tb[j][:, 0:CWi], wbuf[j][:, 0:CWi],
                    c.cols[:, 2 * i: 2 * i + 1], None, ALU.mult))
                nc.vector.wait_ge(c.sDVE.h, c.sDVE.n)
                c.sDVE.inc(nc.vector.tensor_scalar(
                    cb1[j][:, 0:CWi], tb[j][:, 0:CWi], 0.5, None, ALU.is_ge))
                nc.vector.wait_ge(c.sDVE.h, c.sDVE.n)
                c.sDVE.inc(nc.vector.tensor_scalar(
                    wbuf[j][:, 0:CWi], tb[j][:, 0:CWi], -0.5, None,
                    ALU.is_le))
                dve_mark[j] = c.sDVE.n
                nc.gpsimd.wait_ge(c.sDVE.h, dve_mark[j])
                if sp_toks[j] is not None:
                    c.SQ.wait(nc.gpsimd, sp_toks[j])
                c.sPOOL.inc(nc.gpsimd.tensor_sub(qb[j][:, 0:CWi],
                                                 cb1[j][:, 0:CWi],
                                                 wbuf[j][:, 0:CWi]))
                pool_mark[j] = c.sPOOL.n
                nc.gpsimd.wait_ge(c.sPOOL.h, pool_mark[j])
                c.SQ.begin()
                c.SQ.dma(wdst[r * 128:(r + 1) * 128,
                              ch * CWi:(ch + 1) * CWi], qb[j][:, 0:CWi])
                sp_toks[j] = c.SQ.tok()
                yield
    while True:
        yield


def p_rvk_wkv(c):
    nc = c.nc
    ewc = c.place("wk_ew", [128, NB], F32, c.E0 + 704)
    nc.scalar.wait_ge(c.sDVE.h, c.sDVE.n)
    c.sACT.inc(nc.scalar.activation(ewc[:], c.k_td[:], AF.Exp, bias=0.0,
                                    scale=1.0))
    nc.scalar.wait_ge(c.sACT.h, c.sACT.n)
    c.sACT.inc(nc.scalar.activation(ewc[:], ewc[:], AF.Exp, bias=0.0,
                                    scale=-1.0))

    kf = c.place("wk_k", [128, T], F32, c.A0)
    vf = c.place("wk_v", [128, T], F32, c.A0 + 8 * KB)
    af = c.place("wk_a", [128, T], F32, c.A0 + 24 * KB)
    ewm = c.place("wk_ewm", [128, T], F32, c.A0 + 32 * KB)
    av = c.place("wk_av", [128, T], F32, c.A0 + 40 * KB)
    rf = c.place("wk_r", [128, T], F32, c.A0 + 40 * KB)
    sz1 = c.place("wk_s1", [128, T + 8], F32, c.A0 + 48 * KB)
    sz2 = c.place("wk_s2", [128, T + 8], F32, c.A0 + 48 * KB + 8224)
    w32 = c.place("wk_w32", [128, D], F32, c.A0 + 16 * KB)
    whb = c.place("wk_whb", [128, D], BF16, c.A0 + 20 * KB)
    wlb = c.place("wk_wlb", [128, D], BF16, c.A0 + 22 * KB)
    colw = c.place("wk_colw", [128, 4], F32, c.E0 + 768)

    evac_mark = [0, 0, 0, 0]

    def load_split(proj, cb):
        col0 = (proj * NB + cb) * 128
        nc.sync.wait_ge(c.sACT.h, c.sACT.n)
        nc.sync.wait_ge(c.sDVE.h, c.sDVE.n)
        c.LQ.begin()
        c.LQ.dma(w32[:].rearrange("p (d w) -> p d w", w=128),
                 c.wrvkT[:, col0:col0 + 128].rearrange(
                     "(d p) w -> p d w", p=128))
        c.LQ.wait(nc.scalar)
        nc.scalar.wait_ge(c.sPE.h, c.sPE.n)
        c.sACT.inc(nc.scalar.activation(whb[:], w32[:], AF.Copy,
                                        bias=0.0, scale=1.0))
        nc.vector.wait_ge(c.sACT.h, c.sACT.n)
        c.sDVE.inc(nc.vector.scalar_tensor_tensor(
            wlb[:], w32[:], 1.0, whb[:], ALU.mult, ALU.subtract))
        return c.sDVE.n

    def proj_mms(wsplit_mark):
        nc.tensor.wait_ge(c.sDVE.h, wsplit_mark)
        for tc in range(4):
            nc.tensor.wait_ge(c.sDVE.h, evac_mark[tc])
            ps = c.PS[tc]
            for d in range(NB):
                lh_h = whb[:, d * 128:(d + 1) * 128]
                lh_l = wlb[:, d * 128:(d + 1) * 128]
                ah = c.AH[:, d * T + tc * 512: d * T + tc * 512 + 512]
                al = c.AL[:, d * T + tc * 512: d * T + tc * 512 + 512]
                nc.tensor.matmul(ps[:], lh_h, ah, start=(d == 0),
                                 stop=False)
                nc.tensor.matmul(ps[:], lh_h, al, start=False, stop=False)
                ins = nc.tensor.matmul(ps[:], lh_l, ah, start=False,
                                       stop=(d == NB - 1))
            c.sPE.inc(ins)
        return c.sPE.n

    def proj_evac(dst):
        nc.vector.wait_ge(c.sPE.h, c.sPE.n)
        nc.vector.wait_ge(c.sPOOL.h, c.sPOOL.n)
        nc.vector.wait_ge(c.sACT.h, c.sACT.n)
        for tc in range(4):
            c.sDVE.inc(nc.vector.tensor_copy(
                dst[:, tc * 512:(tc + 1) * 512], c.PS[tc][:]))
            evac_mark[tc] = c.sDVE.n

    for cb in range(NB):
        for proj, dst in ((2, kf), (1, vf)):
            proj_mms(load_split(proj, cb))
            proj_evac(dst)
        r_mm = proj_mms(load_split(0, cb))
        ew_col = ewc[:, cb:cb + 1]
        nc.vector.wait_ge(c.sDVE.h, c.sDVE.n)
        c.sDVE.inc(nc.vector.tensor_reduce(colw[:, 0:1], kf[:], AX.X,
                                           ALU.max))
        nc.vector.wait_ge(c.sDVE.h, c.sDVE.n)
        c.sDVE.inc(nc.vector.tensor_scalar(colw[:, 1:2], colw[:, 0:1], -1.0,
                                           None, ALU.mult))
        nc.scalar.wait_ge(c.sDVE.h, c.sDVE.n)
        c.sACT.inc(nc.scalar.activation(af[:], kf[:], AF.Exp,
                                        bias=colw[:, 1:2], scale=1.0))
        c.sACT.inc(nc.scalar.activation(ewm[:], kf[:], AF.Identity,
                                        bias=ew_col, scale=0.0))
        nc.gpsimd.wait_ge(c.sACT.h, c.sACT.n)
        c.sPOOL.inc(nc.gpsimd.tensor_mul(av[:], af[:], vf[:]))
        nc.vector.wait_ge(c.sPOOL.h, c.sPOOL.n)
        c.sDVE.inc(nc.vector.memset(sz1[:, 0:1], 0.0))
        c.sDVE.inc(nc.vector.memset(sz2[:, 0:1], 0.0))
        nc.vector.wait_ge(c.sDVE.h, c.sDVE.n)
        c.sDVE.inc(nc.vector.tensor_tensor_scan(
            sz1[:, 1:T + 1], ewm[:], av[:], 0.0, ALU.mult, ALU.add))
        nc.vector.wait_ge(c.sDVE.h, c.sDVE.n)
        c.sDVE.inc(nc.vector.tensor_tensor_scan(
            sz2[:, 1:T + 1], ewm[:], af[:], 0.0, ALU.mult, ALU.add))
        nc.vector.wait_ge(c.sDVE.h, c.sDVE.n)
        c.sDVE.inc(nc.vector.tensor_add(colw[:, 2:3], colw[:, 1:2],
                                        c.k_tf[:, cb:cb + 1]))
        nc.scalar.wait_ge(c.sDVE.h, c.sDVE.n)
        c.sACT.inc(nc.scalar.activation(ewm[:], kf[:], AF.Exp,
                                        bias=colw[:, 2:3], scale=1.0))
        nc.gpsimd.wait_ge(c.sACT.h, c.sACT.n)
        nc.gpsimd.wait_ge(c.sDVE.h, c.sDVE.n)
        c.sPOOL.inc(nc.gpsimd.tensor_mul(av[:], ewm[:], vf[:]))
        nc.vector.wait_ge(c.sPOOL.h, c.sPOOL.n)
        c.sDVE.inc(nc.vector.tensor_add(vf[:], sz1[:, 0:T], av[:]))
        nc.vector.wait_ge(c.sPE.h, r_mm)
        nc.vector.wait_ge(c.sDVE.h, c.sDVE.n)
        for tc in range(4):
            c.sDVE.inc(nc.vector.tensor_copy(
                rf[:, tc * 512:(tc + 1) * 512], c.PS[tc][:]))
            evac_mark[tc] = c.sDVE.n
        nc.gpsimd.wait_ge(c.sDVE.h, c.sDVE.n)
        c.sPOOL.inc(nc.gpsimd.tensor_add(af[:], sz2[:, 0:T], ewm[:]))
        nc.vector.wait_ge(c.sPOOL.h, c.sPOOL.n)
        c.sDVE.inc(nc.vector.reciprocal(kf[:], af[:]))
        nc.gpsimd.wait_ge(c.sDVE.h, c.sDVE.n)
        c.sPOOL.inc(nc.gpsimd.tensor_mul(kf[:], kf[:], vf[:]))
        nc.scalar.wait_ge(c.sPOOL.h, c.sPOOL.n)
        nc.scalar.wait_ge(c.sDVE.h, c.sDVE.n)
        c.sACT.inc(nc.scalar.activation(af[:], rf[:], AF.Sigmoid, bias=0.0,
                                        scale=1.0))
        nc.vector.wait_ge(c.sACT.h, c.sACT.n)
        c.sDVE.inc(nc.vector.tensor_mul(kf[:], kf[:], af[:]))
        nc.scalar.wait_ge(c.sDVE.h, c.sDVE.n)
        c.sACT.inc(nc.scalar.activation(c.RWH[:, cb * T:(cb + 1) * T], kf[:],
                                        AF.Copy, bias=0.0, scale=1.0))
        nc.gpsimd.wait_ge(c.sACT.h, c.sACT.n)
        c.sPOOL.inc(nc.gpsimd.tensor_sub(
            c.RWL[:, cb * T:(cb + 1) * T], kf[:],
            c.RWH[:, cb * T:(cb + 1) * T]))
    bar(c)


def p_outproj(c):
    nc = c.nc
    c.e_w32 = c.place("o1_w32", [128, D], F32, c.A0)
    wh = c.place("o1_wh", [128, NB * D], BF16, c.A0 + 4 * KB)
    wl = c.place("o1_wl", [128, NB * D], BF16, c.A0 + 20 * KB)
    load_w_split(c, c.woutT, wh, wl, NB)
    op_bank = [0, 0]
    for cb in range(NB):
        for tc in range(4):
            ps = c.PS[tc % 2]
            nc.tensor.wait_ge(c.sACT.h, op_bank[tc % 2])
            for d in range(NB):
                lh_h = wh[:, d * D + cb * 128: d * D + cb * 128 + 128]
                lh_l = wl[:, d * D + cb * 128: d * D + cb * 128 + 128]
                rh = c.RWH[:, d * T + tc * 512: d * T + tc * 512 + 512]
                rl = c.RWL[:, d * T + tc * 512: d * T + tc * 512 + 512]
                nc.tensor.matmul(ps[:], lh_h, rh, start=(d == 0), stop=False)
                nc.tensor.matmul(ps[:], lh_h, rl, start=False, stop=False)
                ins = nc.tensor.matmul(ps[:], lh_l, rh, start=False,
                                       stop=(d == NB - 1))
            c.sPE.inc(ins)
            nc.scalar.wait_ge(c.sPE.h, c.sPE.n)
            c.sACT.inc(nc.scalar.activation(
                c.SF[:, cb * T + tc * 512: cb * T + tc * 512 + 512], ps[:],
                AF.Identity, bias=0.0, scale=1.0))
            op_bank[tc % 2] = c.sACT.n
    nc.gpsimd.wait_ge(c.sACT.h, c.sACT.n)
    c.SQ.begin()
    c.SQ.dma(c.sp_s[:].rearrange("(b p) t -> p b t", p=128),
             c.SF[:].rearrange("p (b t) -> p b t", b=NB))
    c.SQ.wait(nc.sync)
    bar(c)


def p_ffn1_combine(c):
    nc = c.nc
    G = [c.place(f"f1_g{i}", [128, F1], F32, c.A0 + i * 16 * KB)
         for i in range(2)]
    nfc = [c.place(f"f1_nfc{i}", [128, NB * 128], BF16,
                   c.A0 + 32 * KB + i * 2 * KB) for i in range(2)]
    gt = c.place("f1_gt", [128, D], F32, c.A0 + 36 * KB)
    ostg = c.place("f1_ostg", [128, NB * 128], F32, c.A0 + 36 * KB)
    nh = [c.place(f"f1_nh{i}", [128, F1], BF16, c.A0 + 40 * KB + i * 8 * KB)
          for i in range(2)]
    nhf = c.place("f1_nhf", [128, F1], BF16, c.A0 + 56 * KB)
    xsb = [c.place("f1_xs", [128, 256], F32, c.A0 + 64 * KB)] * 2
    wgb = [c.place(f"f1_wgb{i}", [128, NB * 512], BF16,
                   c.B0 + i * 8 * KB) for i in range(2)]
    wub = [c.place(f"f1_wub{i}", [128, NB * 512], BF16,
                   c.B0 + 16 * KB + i * 8 * KB) for i in range(2)]
    uq = c.place("f1_uq", [128, F1], F32, c.B0 + 32 * KB)
    cq = c.place("f1_cq", [128, F1], F32, c.B0 + 48 * KB)
    wdq = c.place("f1_wdq", [128, NF1 * D], BF16, c.C0)
    ppc = c.place("f1_ppc", [128, 16], F32, c.E0 + 832)

    c.LQ.begin()
    c.LQ.dma(wdq[:].rearrange("p (h w) -> p h w", w=D),
             c.sp_wd1[:].rearrange("(h p) w -> p h w", p=128))
    wdq_tok = c.LQ.tok()

    w_toks = [None, None]
    act_bank = [0] * 4
    dve_bank = [0] * 4
    g_free = [0, 0]
    nh_marks = [0, 0]
    nh_pe_free = [0, 0]
    out_tok = None
    state = {"gt_act": 0, "out_pool": 0}

    def load_w(cc):
        j = cc % 2
        nc.sync.wait_ge(c.sPE.h, c.sPE.n)
        c.LQ.begin()
        c.LQ.dma(wgb[j][:].rearrange("p (d w) -> p d w", w=512),
                 c.sp_wg1[:, cc * 512:(cc + 1) * 512].rearrange(
                     "(d p) w -> p d w", p=128))
        c.LQ.dma(wub[j][:].rearrange("p (d w) -> p d w", w=512),
                 c.sp_wu1[:, cc * 512:(cc + 1) * 512].rearrange(
                     "(d p) w -> p d w", p=128))
        w_toks[j] = c.LQ.tok()

    def gates_h(tcn):
        nc.vector.wait_ge(c.sACT.h, c.sACT.n)
        nc.vector.wait_ge(c.sDVE.h, c.sDVE.n)
        for ti in range(2):
            tt = 2 * tcn + ti
            c.sDVE.inc(nc.vector.tensor_mul(
                ppc[:, 2 * ti:2 * ti + 1], c.sinv[:, tt:tt + 1],
                c.cols[:, 1:2]))
            c.sDVE.inc(nc.vector.tensor_mul(
                ppc[:, 2 * ti + 1:2 * ti + 2], c.sinv[:, tt:tt + 1],
                c.cols[:, 3:4]))
        pp_mark = c.sDVE.n
        nfc_toks = [None, None]
        for ti in range(2):
            tt = 2 * tcn + ti
            nc.sync.wait_ge(c.sPE.h, c.sPE.n)
            c.LQ.begin()
            c.LQ.dma(nfc[ti][:].rearrange("p (d w) -> p d w", w=128),
                     c.sp_n[:, tt * 128:(tt + 1) * 128].rearrange(
                         "(d p) w -> p d w", p=128))
            nfc_toks[ti] = c.LQ.tok()
        load_w(0)
        for cc in range(NB):
            if cc + 1 < NB:
                load_w(cc + 1)
            j = cc % 2
            c.LQ.wait(nc.tensor, w_toks[j])
            for ti in range(2):
                c.LQ.wait(nc.tensor, nfc_toks[ti])
                psg, psu = c.PS[2 * ti], c.PS[2 * ti + 1]
                nc.tensor.wait_ge(c.sACT.h, act_bank[2 * ti])
                nc.tensor.wait_ge(c.sDVE.h, dve_bank[2 * ti + 1])
                ins = None
                for d in range(NB):
                    lh = nfc[ti][:, d * 128:(d + 1) * 128]
                    nc.tensor.matmul(psg[:], lh,
                                     wgb[j][:, d * 512:(d + 1) * 512],
                                     start=(d == 0), stop=(d == NB - 1))
                    ins = nc.tensor.matmul(psu[:], lh,
                                           wub[j][:, d * 512:(d + 1) * 512],
                                           start=(d == 0),
                                           stop=(d == NB - 1))
                c.sPE.inc(ins)
                mm_mark = c.sPE.n
                gsl = slice(cc * 512, (cc + 1) * 512)
                nc.scalar.wait_ge(c.sPE.h, mm_mark)
                nc.scalar.wait_ge(c.sDVE.h, max(pp_mark, g_free[ti]))
                c.sACT.inc(nc.scalar.activation(
                    G[ti][:, gsl], psg[:], AF.Silu, bias=0.0,
                    scale=ppc[:, 2 * ti:2 * ti + 1]))
                act_bank[2 * ti] = c.sACT.n
                nc.vector.wait_ge(c.sACT.h, act_bank[2 * ti])
                nc.vector.wait_ge(c.sDVE.h, g_free[ti])
                c.sDVE.inc(nc.vector.scalar_tensor_tensor(
                    G[ti][:, gsl], psu[:], ppc[:, 2 * ti + 1:2 * ti + 2],
                    G[ti][:, gsl], ALU.mult, ALU.mult))
                dve_bank[2 * ti + 1] = c.sDVE.n

    def quantA(tcn, ti):
        tt = 2 * tcn + ti
        hq = G[ti]
        q = 8 + 4 * ti
        nc.vector.wait_ge(c.sDVE.h, dve_bank[2 * ti + 1])
        nc.vector.wait_ge(c.sACT.h, act_bank[2 * ti])
        c.sDVE.inc(nc.vector.tensor_reduce(
            ppc[:, q:q + 1], hq[:], AX.X, ALU.max, apply_absolute_value=True))
        nc.vector.wait_ge(c.sDVE.h, c.sDVE.n)
        c.sDVE.inc(nc.vector.tensor_scalar(ppc[:, q:q + 1], ppc[:, q:q + 1],
                                           1e-5, None, ALU.max))
        nc.vector.wait_ge(c.sDVE.h, c.sDVE.n)
        c.sDVE.inc(nc.vector.reciprocal(ppc[:, q + 1:q + 2], ppc[:, q:q + 1]))
        nc.vector.wait_ge(c.sDVE.h, c.sDVE.n)
        c.sDVE.inc(nc.vector.tensor_scalar(
            c.shinv[:, tt:tt + 1], ppc[:, q:q + 1], 1.0 / 127.0, None,
            ALU.mult))
        c.sDVE.inc(nc.vector.tensor_scalar(ppc[:, q + 1:q + 2],
                                           ppc[:, q + 1:q + 2], 127.0, None,
                                           ALU.mult))
        nc.vector.wait_ge(c.sDVE.h, c.sDVE.n)
        c.sDVE.inc(nc.vector.tensor_scalar(hq[:], hq[:], ppc[:, q + 1:q + 2],
                                           128.5, ALU.mult, ALU.add))
        ladder_floor(c, hq, uq, cq, F1)
        nc.vector.wait_ge(c.sDVE.h, c.sDVE.n)
        nc.vector.wait_ge(c.sPOOL.h, c.sPOOL.n)
        nc.vector.wait_ge(c.sPE.h, nh_pe_free[ti])
        c.sDVE.inc(nc.vector.scalar_tensor_tensor(
            nh[ti][:], hq[:], -128.0, uq[:], ALU.add, ALU.subtract))
        nh_marks[ti] = c.sDVE.n
        g_free[ti] = c.sDVE.n

    def peB(tcn, ti):
        nonlocal out_tok
        tt = 2 * tcn + ti
        ev_mark = 0
        nc.tensor.wait_ge(c.sDVE.h, nh_marks[ti])
        for hb4 in range(0, NF1, 4):
            nc.tensor.wait_ge(c.sACT.h, ev_mark)
            for j in range(4):
                hb = hb4 + j
                ins = nc.tensor.transpose(
                    c.PSB[:, 512 + j * 128:512 + (j + 1) * 128],
                    nh[ti][:, hb * 128:(hb + 1) * 128], c.k_ident[:])
            c.sPE.inc(ins)
            nc.scalar.wait_ge(c.sPE.h, c.sPE.n)
            for j in range(4):
                hb = hb4 + j
                ins = nc.scalar.activation(
                    nhf[:, hb * 128:(hb + 1) * 128],
                    c.PSB[:, 512 + j * 128:512 + (j + 1) * 128],
                    AF.Copy, bias=0.0, scale=1.0)
            c.sACT.inc(ins)
            ev_mark = c.sACT.n
        tr_mark = c.sACT.n
        nh_pe_free[ti] = c.sPE.n
        c.LQ.wait(nc.tensor, wdq_tok)
        nc.tensor.wait_ge(c.sACT.h, tr_mark)
        for dc in range(2):
            ps = c.PS[4 + dc]
            nc.tensor.wait_ge(c.sACT.h, state["gt_act"])
            ins = None
            for hb in range(NF1):
                ins = nc.tensor.matmul(
                    ps[:], nhf[:, hb * 128:(hb + 1) * 128],
                    wdq[:, hb * D + dc * 512: hb * D + dc * 512 + 512],
                    start=(hb == 0), stop=(hb == NF1 - 1))
            c.sPE.inc(ins)
        down_mark = c.sPE.n
        nc.vector.wait_ge(c.sDVE.h, c.sDVE.n)
        c.sDVE.inc(nc.vector.tensor_mul(
            ppc[:, 6:7], c.shinv[:, tt:tt + 1], c.cols[:, 5:6]))
        ppd_mark = c.sDVE.n
        nc.scalar.wait_ge(c.sPE.h, down_mark)
        nc.scalar.wait_ge(c.sDVE.h, ppd_mark)
        if out_tok is not None:
            c.SQ.wait(nc.scalar, out_tok)
        nc.scalar.wait_ge(c.sPOOL.h, state["out_pool"])
        for dc in range(2):
            c.sACT.inc(nc.scalar.activation(
                gt[:, dc * 512:(dc + 1) * 512], c.PS[4 + dc][:], AF.Silu,
                bias=0.0, scale=ppc[:, 6:7]))
        g_mark = c.sACT.n
        state["gt_act"] = g_mark
        for grp in range(2):
            nc.tensor.wait_ge(c.sACT.h, g_mark)
            nc.tensor.wait_ge(c.sDVE.h, c.sDVE.n)
            for q4 in range(4):
                cb = grp * 4 + q4
                ins = nc.tensor.transpose(
                    c.PS[6][:, q4 * 128:(q4 + 1) * 128],
                    gt[:, cb * 128:(cb + 1) * 128], c.k_ident_f32[:])
            c.sPE.inc(ins)
            tr4 = c.sPE.n
            for q4 in range(4):
                cb = grp * 4 + q4
                par = cb % 2
                nc.sync.wait_ge(c.sDVE.h, c.sDVE.n)
                nc.sync.wait_ge(c.sPOOL.h, c.sPOOL.n)
                c.LQ.begin()
                c.LQ.dma(xsb[par][:, 0:128],
                         c.sp_x[cb * 128:(cb + 1) * 128,
                                tt * 128:(tt + 1) * 128])
                c.LQ.dma(xsb[par][:, 128:256],
                         c.sp_s[cb * 128:(cb + 1) * 128,
                                tt * 128:(tt + 1) * 128])
                xs_tok = c.LQ.tok()
                c.LQ.wait(nc.vector, xs_tok)
                nc.vector.wait_ge(c.sPE.h, tr4)
                nc.vector.wait_ge(c.sPOOL.h, c.sPOOL.n)
                if out_tok is not None:
                    c.SQ.wait(nc.vector, out_tok)
                c.sDVE.inc(nc.vector.tensor_mul(
                    ostg[:, cb * 128:(cb + 1) * 128],
                    xsb[par][:, 128:256], c.PS[6][:, q4 * 128:(q4 + 1) * 128]))
                nc.gpsimd.wait_ge(c.sDVE.h, c.sDVE.n)
                if out_tok is not None:
                    c.SQ.wait(nc.gpsimd, out_tok)
                c.sPOOL.inc(nc.gpsimd.tensor_add(
                    ostg[:, cb * 128:(cb + 1) * 128],
                    ostg[:, cb * 128:(cb + 1) * 128],
                    xsb[par][:, 0:128]))
        state["out_pool"] = c.sPOOL.n
        nc.gpsimd.wait_ge(c.sPOOL.h, c.sPOOL.n)
        c.SQ.begin()
        c.SQ.dma(c.out[:, tt * 128:(tt + 1) * 128].rearrange(
            "(b p) w -> p b w", p=128),
            ostg[:].rearrange("p (b w) -> p b w", w=128))
        out_tok = c.SQ.tok()

    pending = None
    for tcn in range(NT // 2):
        gates_h(tcn)
        quantA(tcn, 0)
        if pending is not None:
            peB(*pending)
        quantA(tcn, 1)
        peB(tcn, 0)
        pending = (tcn, 1)
    peB(*pending)
    c.SQ.wait_all(nc.sync)
    bar(c)




def kernel(L0_attn_input, L0_attn_Wq, L0_attn_Wk, L0_attn_Wv, L0_attn_Wo,
           L0_attn_bo, L0_ffn_w_gate, L0_ffn_w_up, L0_ffn_w_down,
           L1_attn_W_rvk, L1_attn_W_out, L1_attn_time_decay,
           L1_attn_time_first, L1_ffn_w_gate, L1_ffn_w_up, L1_ffn_w_down):
    import ml_dtypes

    def bf(x):
        return np.ascontiguousarray(x).astype(ml_dtypes.bfloat16)

    consts = _host_consts()
    common = dict(
        wqT=np.ascontiguousarray(np.asarray(L0_attn_Wq).T),
        wkT=np.ascontiguousarray(np.asarray(L0_attn_Wk).T),
        wvT=np.ascontiguousarray(np.asarray(L0_attn_Wv).T),
        woT=np.ascontiguousarray(np.asarray(L0_attn_Wo).T),
        bo=np.ascontiguousarray(np.asarray(L0_attn_bo).reshape(NB, 128).T),
        wg0T=np.ascontiguousarray(np.asarray(L0_ffn_w_gate).T),
        wu0T=np.ascontiguousarray(np.asarray(L0_ffn_w_up).T),
        wd0T=np.ascontiguousarray(np.asarray(L0_ffn_w_down).T),
        wrvkT=np.ascontiguousarray(np.asarray(L1_attn_W_rvk).T),
        woutT=np.ascontiguousarray(np.asarray(L1_attn_W_out).T),
        td=np.ascontiguousarray(np.asarray(L1_attn_time_decay).reshape(NB, 128).T),
        tf=np.ascontiguousarray(np.asarray(L1_attn_time_first).reshape(NB, 128).T),
        wg1T=np.ascontiguousarray(np.asarray(L1_ffn_w_gate).T),
        wu1T=np.ascontiguousarray(np.asarray(L1_ffn_w_up).T),
        wd1T=np.ascontiguousarray(np.asarray(L1_ffn_w_down).T),
        sin_fm=consts["sin_fm"], cos_fm=consts["cos_fm"],
        sin_lo=consts["sin_lo"], cos_lo=consts["cos_lo"],
        rotPT=bf(consts["rotPT"]), ident=bf(consts["ident"]),
        identf=consts["ident"],
        ones1=bf(consts["ones1"]), allones=bf(consts["allones"]),
        onescol=bf(consts["onescol"]), masks=bf(consts["masks"]),
    )
    x = np.asarray(L0_attn_input)
    nc, _ = build_nc(tuple(_DEBUG))
    in_maps = []
    for b in range(B):
        m = dict(common)
        m["xT"] = np.ascontiguousarray(x[b].T)
        in_maps.append(m)
    res = run_bass_kernel_spmd(nc, in_maps, core_ids=list(range(B)))
    out = np.stack([np.asarray(res.results[b]["out"]).T for b in range(B)], 0)
    return out
